# version 19
# baseline (speedup 1.0000x reference)
import numpy as np
import ml_dtypes
import concourse.bass as bass
import concourse.mybir as mybir
from concourse.bass_utils import run_bass_kernel_spmd

F32 = mybir.dt.float32
BF16 = mybir.dt.bfloat16
AF = mybir.ActivationFunctionType
ALU = mybir.AluOpType
AX = mybir.AxisListType

D = 1024
L = 4096
H = 4
DKV = 256
T = 512
NT = L // T
NCK = 8
CC = 128
NCH = T // CC
KS, KM, KL = 3, 15, 63
DH = 4096

DBG = False
P_NT = NT
P_ATTN = True
P_DELTA = True
P_FIR = True
P_MIX = True
P_PROJ = True
P_GATES = True
P_F1 = True
P_F2 = True
P_F3 = True
P_F4 = True

ENG2BLK = {"pe": "tensor", "act": "scalar", "dve": "vector", "pool": "gpsimd", "sp": "sync"}
DMA_ENG = {"sp", "pool"}


class _Res:
    __slots__ = ("w", "rs")

    def __init__(self):
        self.w = None
        self.rs = []


DMA_NSEM = 12


class Prog:
    def __init__(self):
        self.items = {e: [] for e in ENG2BLK}
        self.lab = []
        self.cnt = {e: 0 for e in ENG2BLK}
        for e in DMA_ENG:
            self.cnt[e + "_c"] = 0
            for i in range(DMA_NSEM):
                self.cnt[f"{e}{i}"] = 0
        self.dmasem_i = {e: 0 for e in DMA_ENG}
        self.sem_last = {}
        self.res = {}
        self.pending_join = None
        self.join_left = set()
        self.last_lab = {}

    def R(self, key):
        r = self.res.get(key)
        if r is None:
            r = self.res[key] = _Res()
        return r

    def op(self, eng, fn, r=(), w=(), xw=(), small=False, dma=None):
        is_dma = (eng in DMA_ENG) if dma is None else dma
        waits = []
        for k in r:
            rr = self.R(k)
            if rr.w is not None:
                waits.append(rr.w)
        for k in w:
            rr = self.R(k)
            waits.extend(rr.rs)
            if rr.w is not None:
                waits.append(rr.w)
        waits.extend(xw)
        if self.pending_join is not None and eng in self.join_left:
            waits.extend(self.pending_join)
            self.join_left.discard(eng)
            if not self.join_left:
                self.pending_join = None
        if is_dma:
            si = self.dmasem_i[eng]
            self.dmasem_i[eng] = (si + 1) % DMA_NSEM
            semkey = f"{eng}{si}"
            if semkey in self.sem_last:
                waits.append(self.sem_last[semkey])
            self.cnt[semkey] += 16
        else:
            semkey = eng if eng not in DMA_ENG else eng + "_c"
            self.cnt[semkey] += 1
        lid = len(self.lab)
        self.lab.append((semkey, self.cnt[semkey], is_dma or small or eng in DMA_ENG,
                         16 if is_dma else 1))
        self.sem_last[semkey] = lid
        self.items[eng].append((fn, tuple(sorted(set(waits))), lid))
        self.last_lab[eng] = lid
        for k in r:
            self.R(k).rs.append(lid)
        for k in w:
            rr = self.R(k)
            rr.w = lid
            rr.rs = []
        return lid

    def waitop(self, eng, r=(), xw=()):
        waits = list(xw)
        for k in r:
            rr = self.R(k)
            if rr.w is not None:
                waits.append(rr.w)
        self.items[eng].append((None, tuple(sorted(set(waits))), None))

    def join(self):
        labs = [self.last_lab[e] for e in ENG2BLK if e in self.last_lab]
        self.pending_join = labs
        self.join_left = set(ENG2BLK)

    def emit(self, nc):
        semkeys = list(self.cnt.keys())
        sems = {k: nc.semaphore("sem_" + k).__enter__() for k in semkeys}
        lab = self.lab
        for k in semkeys:
            nc.sync.sem_clear(sems[k])
        nc.all_engine_barrier()
        with nc.Block() as blk:
            for eng, blkname in ENG2BLK.items():
                items = self.items[eng]

                def body(h, eng=eng, items=items):
                    seen = {k: 0 for k in semkeys}
                    for fn, waits, lid in items:
                        tgt = {}
                        for wl in waits:
                            we, wc, nofilt, _inc = lab[wl]
                            if we == eng and not nofilt:
                                continue
                            if wc > tgt.get(we, 0):
                                tgt[we] = wc
                        for we in sorted(tgt):
                            wc = tgt[we]
                            if wc > seen[we]:
                                h.wait_ge(sems[we], wc)
                                seen[we] = wc
                        if fn is None:
                            continue
                        inst = fn(h)
                        inst.then_inc(sems[lab[lid][0]], lab[lid][3])

                getattr(blk, blkname)(body)


def _bf(x):
    return np.ascontiguousarray(x).astype(ml_dtypes.bfloat16)


def build_program(debug=False):
    nc = bass.Bass("TRN2", target_bir_lowering=False)
    p = Prog()

    def din(name, shape, dt=F32):
        return nc.dram_tensor(name, list(shape), dt, kind="ExternalInput").ap()

    x_in = din("x", [D, L])
    wq_d = din("wq", [128, 8, 8, 128], BF16)
    wk_d = din("wk", [128, 8, 8, 128], BF16)
    wv_d = din("wv", [128, 8, 8, 128], BF16)
    wo_d = din("wo", [128, 8, 8, 128])
    wb_d = din("wb", [128, 8, 4])
    gw_d = din("gw", [128, 8, 20])
    gb_d = din("gb", [1, 20], BF16)
    ltmp_d = din("ltmp", [1, 20])
    eps_d = din("eps", [1, 20])
    nw_d = din("nw", [128, 8])
    cq_d = din("cq", [128, 8, 4])
    ck_d = din("ck", [128, 8, 4])
    cv_d = din("cv", [128, 8, 4])
    fcf_d = din("fcf", [32, 128, 21 * 32])
    rsel_d = din("rsel", [128, 4 * 32], BF16)
    w1_d = din("w1", [32, 128, 8, 128], BF16)
    w2_d = din("w2", [32, 128, 8, 128], BF16)
    h1_d = din("h1", [32, 128, 32, 128], BF16)
    h2_d = din("h2", [32, 128, 32, 128], BF16)
    w3_d = din("w3", [128, 8, 32, 128], BF16)
    s1_d = din("s1", [128, 32])
    s2_d = din("s2", [128, 32])
    sh1_d = din("sh1", [128, 32])
    sh2_d = din("sh2", [128, 32])
    s3_d = din("s3", [128, 8])
    ident_d = din("ident", [128, 128], BF16)
    onesc_d = din("onesc", [128, 1], BF16)
    onesr_d = din("onesr", [1, 128], BF16)
    ones33_d = din("ones33", [33, 128], BF16)
    e4_d = din("e4", [4, 4, 128], BF16)
    e20_d = din("e20", [20, 20, 128], BF16)
    mstril_d = din("mstril", [128, 128], BF16)
    mstriu_d = din("mstriu", [128, 128], BF16)
    mtriu_d = din("mtriu", [128, 128], BF16)

    out_d = nc.dram_tensor("out", [128, 8, L], F32, kind="ExternalOutput").ap()

    on_h = nc.dram_tensor("on_h", [128, 8, L], BF16, kind="Internal").ap()
    h0_h = nc.dram_tensor("h0_h", [128, 32, L], BF16, kind="Internal").ap()
    h1o_h = nc.dram_tensor("h1o_h", [128, 32, L], BF16, kind="Internal").ap()
    h2o_h = nc.dram_tensor("h2o_h", [128, 32, L], BF16, kind="Internal").ap()

    dbg_outs = {}

    def dump(name, src_fn, shape, dt, rkeys):
        if not debug:
            return
        ap = nc.dram_tensor("dbg_" + name, list(shape), dt, kind="ExternalOutput").ap()
        dbg_outs[name] = ap
        p.op("sp", (lambda h, ap=ap, src_fn=src_fn: h.dma_start(out=ap, in_=src_fn())), r=rkeys)

    ps = [nc.alloc_psum_tensor(f"ps{i}", [128, 512], F32) for i in range(6)]
    ps_tr = nc.alloc_psum_tensor("ps_tr", [128, 2, 1024], BF16)
    PSB = [f"ps{i}" for i in range(6)]
    PTR = ["ptr0", "ptr1"]

    sb = nc.alloc_sbuf_tensor
    ident_s = sb("ident_s", [128, 128], BF16)
    onesc_s = sb("onesc_s", [128, 1], BF16)
    onesr_s = sb("onesr_s", [1, 128], BF16)
    ones33_s = sb("ones33_s", [33, 128], BF16)
    e4_s = sb("e4_s", [4, 4 * 128], BF16)
    e20_s = sb("e20_s", [20, 20 * 128], BF16)
    mstril_s = sb("mstril_s", [128, 128], BF16)
    mstriu_s = sb("mstriu_s", [128, 128], BF16)
    mtriu_s = sb("mtriu_s", [128, 128], BF16)
    gb_s = sb("gb_s", [1, 20], BF16)
    eps_s = sb("eps_s", [1, 20], F32)
    itbc_s = sb("itbc_s", [128, 80], BF16)
    eabc_s = sb("eabc_s", [128, 80], BF16)
    ebbc_s = sb("ebbc_s", [128, 80], BF16)
    nw_s = sb("nw_s", [128, 8], F32)

    warm_s = sb("warm_s", [1, 4], F32)
    def warmup(h):
        h.memzero(warm_s[:])
        h.activation(warm_s[:], warm_s[:], AF.Exp)
        h.activation(warm_s[:], warm_s[:], AF.Ln)
        h.activation(warm_s[:], warm_s[:], AF.Silu)
        h.activation(warm_s[:], warm_s[:], AF.Gelu)
        h.activation(warm_s[:], warm_s[:], AF.Sigmoid)
        h.activation(warm_s[:], warm_s[:], AF.Exp)
        return h.activation(warm_s[:], warm_s[:], AF.Copy)
    p.op("act", warmup, w=["warm"])

    for dst, src in [(ident_s, ident_d), (onesc_s, onesc_d), (onesr_s, onesr_d),
                     (mstril_s, mstril_d), (mstriu_s, mstriu_d), (mtriu_s, mtriu_d),
                     (gb_s, gb_d)]:
        p.op("sp", (lambda h, dst=dst, src=src: h.dma_start(out=dst[:], in_=src)),
             w=[dst.name])
    p.op("sp", (lambda h: h.dma_start(out=e4_s[:], in_=e4_d.rearrange("a b c -> a (b c)"))), w=["e4_s"])
    p.op("sp", (lambda h: h.dma_start(out=ones33_s[:], in_=ones33_d)), w=["ones33_s"])
    p.op("sp", (lambda h: h.dma_start(out=e20_s[:], in_=e20_d.rearrange("a b c -> a (b c)"))), w=["e20_s"])
    p.op("sp", (lambda h: h.dma_start(out=nw_s[:], in_=nw_d)), w=["nw_s"])

    ltmp_s = sb("ltmp_s", [1, 20], F32)
    p.op("sp", (lambda h: h.dma_start(out=ltmp_s[:], in_=ltmp_d)), w=["ltmp_s"])
    p.op("sp", (lambda h: h.dma_start(out=eps_s[:], in_=eps_d)), w=["eps_s"])

    r1_s = sb("r1_s", [1, 20], F32)
    r2_s = sb("r2_s", [1, 20], F32)
    r1b_s = sb("r1b_s", [1, 20], BF16)

    p.op("act", (lambda h: h.activation(r1_s[:], ltmp_s[:], AF.Exp, scale=-1.0)),
         r=["ltmp_s"], w=["r1_s"])
    p.op("dve", (lambda h: h.tensor_copy(r1b_s[:], r1_s[:])), r=["r1_s"], w=["r1b_s"], small=True)

    def bc80(h, rowap):
        for tt in range(4):
            i = h.matmul(ps[2][0:128, tt * 20:tt * 20 + 20], onesr_s[:], rowap,
                         start=True, stop=True)
        return i

    p.op("pe", (lambda h: bc80(h, r1b_s[:])), r=["r1b_s", "onesr_s"], w=["ps2"])
    p.op("act", (lambda h: h.activation(itbc_s[:], ps[2][0:128, 0:80], AF.Copy)),
         r=["ps2"], w=["itbc_s"])
    p.op("dve", (lambda h: h.tensor_scalar(r2_s[:], eps_s[:], 0.0, 0.2, ALU.max, ALU.min)),
         r=["eps_s"], w=["r2_s"], small=True)
    r2b_s = sb("r2b_s", [1, 20], BF16)
    p.op("dve", (lambda h: h.tensor_copy(r2b_s[:], r2_s[:])), r=["r2_s"], w=["r2b_s"], small=True)
    p.op("pe", (lambda h: bc80(h, r2b_s[:])), r=["r2b_s", "onesr_s"], w=["ps2"])
    p.op("act", (lambda h: h.activation(ebbc_s[:], ps[2][0:128, 0:80], AF.Copy)),
         r=["ps2"], w=["ebbc_s"])
    if debug:
        dump("r2", (lambda: r2_s[:]), [1, 20], F32, ["r2_s"])
        dump("r2b", (lambda: r2b_s[:]), [1, 20], BF16, ["r2b_s"])
        dump("eps_in", (lambda: eps_s[:]), [1, 20], F32, ["eps_s"])
    p.op("dve", (lambda h: h.tensor_scalar(eabc_s[:], ebbc_s[:], -5.0, 1.0, ALU.mult, ALU.add)),
         r=["ebbc_s"], w=["eabc_s"], small=True)

    ag = []

    def asb(name, shape, dt):
        g = nc.sbuf_tensor(name, list(shape), dt)
        t = g.__enter__()
        ag.append(g)
        return t

    wqr = asb("wqr", [128, 4, 8, 128], BF16)
    wkr = asb("wkr", [128, 4, 8, 128], BF16)
    wvr = asb("wvr", [128, 4, 8, 128], BF16)
    wb_s = asb("wb_s", [128, 8 * 4], BF16)
    gw_s = asb("gw_s", [128, 8 * 20], BF16)
    cq_s = asb("cq_s", [128, 8 * 4], F32)
    ck_s = asb("ck_s", [128, 8 * 4], F32)
    cv_s = asb("cv_s", [128, 8 * 4], F32)
    fcf_s = asb("fcf_s", [128, 4, 21 * 32], BF16)
    vrep_s = asb("vrep_s", [128, 2, 573], BF16)
    rsel_s = asb("rsel_s", [128, 4 * 32], BF16)
    w1r = asb("w1r", [128, 3, 8, 128], BF16)
    w2r = asb("w2r", [128, 3, 8, 128], BF16)
    t1r = asb("t1r", [128, 2, T], BF16)
    h0st = asb("h0st", [128, 2, T], BF16)
    s1_s = asb("s1_s", [128, 32], F32)
    s2_s = asb("s2_s", [128, 32], F32)

    for dst, src in [(wb_s, wb_d), (gw_s, gw_d)]:
        p.op("pool", (lambda h, dst=dst, src=src: h.dma_start(
            out=dst[:], in_=src.rearrange("p a b -> p (a b)") if src.ndim == 3
            else src.rearrange("p a b c -> p (a b c)"))), w=[dst.name])
    for dst, src in [(cq_s, cq_d), (ck_s, ck_d), (cv_s, cv_d)]:
        p.op("sp", (lambda h, dst=dst, src=src: h.dma_start(
            out=dst[:], in_=src.rearrange("p a b -> p (a b)"))), w=[dst.name])
    p.op("sp", (lambda h: h.dma_start(out=rsel_s[:], in_=rsel_d)), w=["rsel_s"])
    p.op("sp", (lambda h: h.dma_start(out=s1_s[:], in_=s1_d)), w=["s1_s"])
    p.op("sp", (lambda h: h.dma_start(out=s2_s[:], in_=s2_d)), w=["s2_s"])

    def w_at(ws, ki, co):
        return ws[:, (ki * 8 + co) * 128:(ki * 8 + co) * 128 + 128]

    x_bf = asb("x_bf", [128, 8, T], BF16)
    xn_s = asb("xn_s", [128, 2, 8, T], BF16)
    q_s = asb("q_s", [128, 8, T], BF16)
    k_s = asb("k_s", [128, 8, T], BF16)
    v_s = asb("v_s", [128, 8, 64 + T], BF16)
    pre_s = asb("pre_s", [128, 2, 3 + T], BF16)
    tails = asb("tails", [128, 3, 8, 3], BF16)
    cacc_s = asb("cacc_s", [128, 2, T], BF16)
    beta_s = asb("beta_s", [4, T], F32)
    beta_b = asb("beta_b", [4, T], BF16)
    m_row = asb("m_row", [1, T], F32)
    v_row = asb("v_row", [1, T], F32)
    mb_row = asb("mb_row", [1, T], BF16)
    rb_row = asb("rb_row", [1, T], BF16)
    sq_s = asb("sq_s", [128, 16, 128], BF16)
    lr_row = asb("lr_row", [1, T], F32)
    lr64_s = asb("lr64_s", [64, T], F32)
    lrb64_s = asb("lrb64_s", [64, T], BF16)
    rqbc_s = asb("rqbc_s", [128, T], BF16)
    rkbc_s = asb("rkbc_s", [128, T], BF16)
    bbc_s = asb("bbc_s", [128, T], BF16)
    ql_s = asb("ql_s", [128, 2, 8, 128], BF16)
    kl_s = asb("kl_s", [128, 8, 128], BF16)
    kb_s = asb("kb_s", [128, 8, 128], BF16)
    vb_s = asb("vb_s", [128, 8, 128], BF16)
    ktok_s = asb("ktok_s", [128, 2, 4, 768], BF16)
    Bp_s = asb("Bp_s", [128, 4, 128], BF16)
    Bpt_s = asb("Bpt_s", [128, 4, 128], BF16)
    Bp2_s = asb("Bp2_s", [128, 4, 128], BF16)
    Bpt2_s = asb("Bpt2_s", [128, 4, 128], BF16)
    Pt_s = asb("Pt_s", [128, 2, 4, 128], BF16)
    at_s = asb("at_s", [128, 2, 4, 128], BF16)
    npt_s = asb("npt_s", [128, 4, 128], BF16)
    nwT_s = asb("nwT_s", [128, 2, 512], BF16)
    uadj_s = asb("uadj_s", [128, 2, 512], BF16)
    otok_s = asb("otok_s", [128, 2, 512], BF16)
    S_s = asb("S_s", [128, 4, 512], F32)
    Sb_s = asb("Sb_s", [128, 4, 512], BF16)
    delta_s = asb("delta_s", [128, 8, T], BF16)
    omix_s = asb("omix_s", [128, 8, T], BF16)
    mixt_s = asb("mixt_s", [128, T], BF16)
    mixtp_s = asb("mixtp_s", [128, T], BF16)
    mixq_s = asb("mixq_s", [128, T], BF16)
    pbc5_s = asb("pbc5_s", [128, 5, T], BF16)
    probs_s = asb("probs_s", [128, 80], F32)
    probsb_s = asb("probsb_s", [128, 80], BF16)
    probsf_s = asb("probsf_s", [20, T], BF16)
    gsc_s = asb("gsc_s", [128, 80], F32)
    gmx_s = asb("gmx_s", [128, 16], F32)
    gsum_s = asb("gsum_s", [128, 16], F32)
    gexp_s = asb("gexp_s", [128, 80], BF16)

    p.op("dve", (lambda h: h.memset(v_s[:, :, 0:64], 0.0)),
         w=[f"v_s{c}" for c in range(NCK)])
    p.op("dve", (lambda h: h.memset(tails[:], 0.0)), w=["tails_all"])
    p.op("dve", (lambda h: h.memset(S_s[:], 0.0)),
         w=[f"S{hh}" for hh in range(H)])

    CQS = {0: cq_s, 1: ck_s, 2: cv_s}
    WS = {0: wqr, 1: wkr, 2: wvr}
    WD = {0: wq_d, 1: wk_d, 2: wv_d}
    OUTS = {0: q_s, 1: k_s, 2: v_s}

    BGQ = []
    bgs = {"i": 0, "banks": (5,), "j": 0}

    def bgbank():
        b = bgs["banks"][bgs["j"] % len(bgs["banks"])]
        bgs["j"] += 1
        return b

    def pump(reserve=0, n=1, warm=0):
        for _ in range(n):
            if len(BGQ) - bgs["i"] > reserve:
                BGQ[bgs["i"]]()
                bgs["i"] += 1
            else:
                break
        if False and warm and len(BGQ) - bgs["i"] <= 0:
            p.op("pe", (lambda h: h.matmul(ps[5][:, 0:128], ident_s[:], ident_s[:],
                                           start=True, stop=True)),
                 r=["ident_s"], w=["ps5"])

    def bg_drain(upto):
        while bgs["i"] < min(upto, len(BGQ)):
            BGQ[bgs["i"]]()
            bgs["i"] += 1

    def mk_c0(co):
        ws = co % 3
        def c():
            p.op("sp", (lambda h: h.dma_start(out=w1r[:, ws, :, :], in_=w1_d[co])),
                 w=[f"w1r{ws}"])
            p.op("pool", (lambda h: h.dma_start(out=w2r[:, ws, :, :], in_=w2_d[co])),
                 w=[f"w2r{ws}"])
        return c

    def mk_u1(t, co):
        ws, tb, pr = co % 3, co % 2, t % 2
        def c():
            b = bgbank()
            def mm(h):
                for ki in range(8):
                    i = h.matmul(ps[b][:], w1r[:, ws, ki, :], xn_s[:, pr, ki, :],
                                 start=(ki == 0), stop=(ki == 7))
                return i
            p.op("pe", mm, r=[f"w1r{ws}"] + [f"xn{pr}{ki}" for ki in range(8)], w=[PSB[b]])
            p.op("act", (lambda h: h.activation(t1r[:, tb, :], ps[b][:], AF.Silu,
                                                scale=s1_s[:, co:co + 1])),
                 r=[PSB[b], "s1_s"], w=[f"t1r{tb}"])
        return c

    def mk_u2(t, co):
        ws, tb, hs, pr = co % 3, co % 2, co % 2, t % 2
        def c():
            b = bgbank()
            def mm(h):
                for ki in range(8):
                    i = h.matmul(ps[b][:], w2r[:, ws, ki, :], xn_s[:, pr, ki, :],
                                 start=(ki == 0), stop=(ki == 7))
                return i
            p.op("pe", mm, r=[f"w2r{ws}"] + [f"xn{pr}{ki}" for ki in range(8)], w=[PSB[b]])
            p.op("dve", (lambda h: h.scalar_tensor_tensor(
                h0st[:, hs, :], ps[b][:], s2_s[:, co:co + 1], t1r[:, tb, :],
                ALU.mult, ALU.mult)), r=[PSB[b], "s2_s", f"t1r{tb}"], w=[f"h0st{hs}"])
            p.op("sp", (lambda h: h.dma_start(out=h0_h[:, co, t * T:(t + 1) * T],
                                              in_=h0st[:, hs, :])),
                 r=[f"h0st{hs}"], w=["h0_h", f"h0_h{t}"])
        return c

    def append_f1(t):
        if not P_F1:
            return
        BGQ.append(mk_c0(0))
        BGQ.append(mk_c0(1))
        for co in range(32):
            BGQ.append(mk_u1(t, co))
            BGQ.append(mk_u2(t, co))
            if co + 2 < 32:
                BGQ.append(mk_c0(co + 2))

    drain_target = 0
    for t in range(P_NT if P_ATTN else 0):
        t0 = t * T
        pr = t % 2
        xk = "x0"
        def xload(tt_):
            p.op("pool", (lambda h, tt_=tt_: h.dma_start(
                out=x_bf[:],
                in_=x_in.rearrange("(ck p) t -> p ck t", p=128)[:, :, tt_ * T:tt_ * T + T])),
                w=["x0"])
        if t == 0:
            xload(0)

        def ln_sum(h):
            for ck in range(NCK):
                i = h.matmul(ps[4][0:1, :], onesc_s[:], x_bf[:, ck, :],
                             start=(ck == 0), stop=(ck == NCK - 1))
            return i
        p.op("pe", ln_sum, r=[xk, "onesc_s"], w=["ps4"])
        for ck in range(NCK):
            sp_ = ck % 2
            p.op("dve", (lambda h, ck=ck, sp_=sp_: h.tensor_tensor(
                sq_s[:, 4 * sp_:4 * sp_ + 4, :].rearrange("p a b -> p (a b)"),
                x_bf[:, ck, :], x_bf[:, ck, :], ALU.mult)),
                r=[xk], w=[f"xsq{sp_}"])
            p.op("pe", (lambda h, ck=ck, sp_=sp_: h.matmul(
                ps[4][32:33, :], onesc_s[:],
                sq_s[:, 4 * sp_:4 * sp_ + 4, :].rearrange("p a b -> p (a b)"),
                start=(ck == 0), stop=(ck == NCK - 1), tile_position=(0, 32))),
                r=[f"xsq{sp_}", "onesc_s"], w=["ps4"])

        p.op("dve", (lambda h: h.tensor_scalar_mul(m_row[:], ps[4][0:1, :], 1.0 / D)),
             r=["ps4"], w=["m_row"], small=True)
        p.op("dve", (lambda h: h.tensor_scalar_mul(v_row[:], ps[4][32:33, :], 1.0 / D)),
             r=["ps4"], w=["v_row"], small=True)
        p.op("dve", (lambda h: h.tensor_tensor(lr_row[:], m_row[:], m_row[:], ALU.mult)),
             r=["m_row"], w=["lr_row"], small=True)
        p.op("dve", (lambda h: h.scalar_tensor_tensor(
            v_row[:], v_row[:], 1e-5, lr_row[:], ALU.add, ALU.subtract)),
            r=["lr_row"], w=["v_row"], small=True)
        p.op("act", (lambda h: h.activation(v_row[:], v_row[:], AF.Ln)), w=["v_row"])
        p.op("act", (lambda h: h.activation(v_row[:], v_row[:], AF.Exp, scale=-0.5)), w=["v_row"])
        p.op("dve", (lambda h: h.tensor_copy(mb_row[:], m_row[:])), r=["m_row"], w=["mb_row"], small=True)
        p.op("dve", (lambda h: h.tensor_copy(rb_row[:], v_row[:])), r=["v_row"], w=["rb_row"], small=True)
        p.op("pe", (lambda h: h.matmul(ps[2][:], onesr_s[:], mb_row[:], start=True, stop=True)),
             r=["mb_row", "onesr_s"], w=["ps2"])
        p.op("pe", (lambda h: h.matmul(ps[3][:], onesr_s[:], rb_row[:], start=True, stop=True)),
             r=["rb_row", "onesr_s"], w=["ps3"])
        for ck in range(NCK):
            p.op("dve", (lambda h, pr=pr, ck=ck: h.tensor_sub(
                xn_s[:, pr, ck, :], x_bf[:, ck, :], ps[2][:])), r=[xk, "ps2"], w=[f"xn{pr}{ck}"])
            p.op("dve", (lambda h, pr=pr, ck=ck: h.tensor_mul(
                xn_s[:, pr, ck, :], xn_s[:, pr, ck, :], ps[3][:])), r=["ps3"], w=[f"xn{pr}{ck}"])
        if t + 1 < (P_NT if P_ATTN else 0):
            xload(t + 1)
        drain_target = len(BGQ)
        append_f1(t)

        for tn in range(3 if P_PROJ else 0):
            ws = WS[tn]
            outb = OUTS[tn]
            cqs = CQS[tn]
            ooff = 64 if tn == 2 else 0

            def wload(tn_, co_):
                sl = co_ % 4
                q_ = "sp" if tn_ != 1 else "pool"
                p.op(q_, (lambda h, tn_=tn_, co_=co_, sl=sl: h.dma_start(
                    out=WS[tn_][:, sl, :, :], in_=WD[tn_][:, co_, :, :])),
                    w=[f"{WS[tn_].name}{sl}"])
            for co_ in range(4):
                wload(tn, co_)
            for co in range(NCK):
                bank = co % 4
                pk = PSB[bank]
                pp = (tn * NCK + co) % 2
                sl = co % 4

                def proj(h, ws=ws, sl=sl, bank=bank, pr=pr):
                    for ki in range(NCK):
                        i = h.matmul(ps[bank][:], ws[:, sl, ki, :], xn_s[:, pr, ki, :],
                                     start=(ki == 0), stop=(ki == NCK - 1))
                    return i
                p.op("pe", proj, r=[f"{ws.name}{sl}"] + [f"xn{pr}{ki}" for ki in range(NCK)],
                     w=[pk])
                if co + 4 < NCK:
                    wload(tn, co + 4)
                p.op("act", (lambda h, bank=bank, pp=pp: h.activation(
                    pre_s[:, pp, 3:3 + T], ps[bank][:], AF.Copy)), r=[pk], w=[f"pre{pp}"])
                p.op("dve", (lambda h, tn=tn, co=co, pp=pp: h.tensor_copy(
                    pre_s[:, pp, 0:3], tails[:, tn, co, :])), r=["tails_all"],
                    w=[f"pre{pp}"], small=True)
                p.op("dve", (lambda h, tn=tn, co=co, pp=pp: h.tensor_copy(
                    tails[:, tn, co, :], pre_s[:, pp, T:T + 3])), r=[f"pre{pp}"],
                    w=["tails_all"], small=True)
                p.op("dve", (lambda h, cqs=cqs, co=co, pp=pp: h.tensor_scalar_mul(
                    cacc_s[:, pp, :], pre_s[:, pp, 0:T], cqs[:, co * 4:co * 4 + 1])),
                    r=[f"pre{pp}", cqs.name], w=[f"cacc{pp}"])
                for j in range(1, 4):
                    p.op("dve", (lambda h, cqs=cqs, co=co, j=j, pp=pp: h.scalar_tensor_tensor(
                        cacc_s[:, pp, :], pre_s[:, pp, j:j + T], cqs[:, co * 4 + j:co * 4 + j + 1],
                        cacc_s[:, pp, :], ALU.mult, ALU.add)), r=[f"pre{pp}", cqs.name],
                        w=[f"cacc{pp}"])
                p.op("act", (lambda h, outb=outb, co=co, ooff=ooff, pp=pp: h.activation(
                    outb[:, co, ooff:ooff + T], cacc_s[:, pp, :], AF.Silu)),
                    r=[f"cacc{pp}"], w=[f"{outb.name}{co}"])
                pump(reserve=96)

        if P_PROJ:
            def beta_mm(h, pr=pr):
                for ki in range(NCK):
                    i = h.matmul(ps[4][0:4, :], wb_s[:, ki * 4:ki * 4 + 4], xn_s[:, pr, ki, :],
                                 start=(ki == 0), stop=(ki == NCK - 1))
                return i
            p.op("pe", beta_mm, r=["wb_s"] + [f"xn{pr}{ki}" for ki in range(NCK)], w=["ps4"])
            p.op("act", (lambda h: h.activation(beta_s[:], ps[4][0:4, :], AF.Sigmoid)),
                 r=["ps4"], w=["beta_s"])
            p.op("dve", (lambda h: h.tensor_copy(beta_b[:], beta_s[:])), r=["beta_s"], w=["beta_b"])
            pump(n=6, warm=1)

        def d1_chunk(c):
            cs0 = c * CC
            cp = c % 2
            for hh in range(H):
                for ci, ck in enumerate((2 * hh, 2 * hh + 1)):
                    for qi, src in enumerate((q_s, k_s)):
                        idx = hh * 4 + ci * 2 + qi
                        p.op("dve", (lambda h, src=src, ck=ck, cs0=cs0, idx=idx:
                                     h.tensor_tensor(
                            sq_s[:, idx, :], src[:, ck, cs0:cs0 + CC],
                            src[:, ck, cs0:cs0 + CC],
                            ALU.mult)), r=[f"{src.name}{ck}"], w=["xsq0"], small=True)

            def l2mm(h):
                for hh in range(H):
                    for qi in range(2):
                        for ci in range(2):
                            idx = hh * 4 + ci * 2 + qi
                            row = 0 if qi == 0 else 32
                            i = h.matmul(ps[4][row:row + 1, hh * 128:(hh + 1) * 128],
                                         onesc_s[:], sq_s[:, idx, :],
                                         start=(ci == 0), stop=(ci == 1),
                                         tile_position=(0, row))
                return i
            p.op("pe", l2mm, r=["xsq0", "onesc_s"], w=["ps4"])
            p.op("dve", (lambda h: h.tensor_scalar_add(lr64_s[:], ps[4][0:64, :], 1e-6)),
                 r=["ps4"], w=["lr64"], small=True)
            p.op("act", (lambda h: h.activation(lr64_s[:], lr64_s[:], AF.Ln)), w=["lr64"])
            p.op("act", (lambda h: h.activation(lr64_s[:], lr64_s[:], AF.Exp, scale=-0.5)),
                 w=["lr64"])
            p.op("dve", (lambda h: h.tensor_copy(lrb64_s[:], lr64_s[:])), r=["lr64"],
                 w=["lrb64"], small=True)

            def bc_q(h):
                for hh in range(H):
                    i = h.matmul(ps[2][:, hh * 128:(hh + 1) * 128], onesr_s[:],
                                 lrb64_s[0:1, hh * 128:(hh + 1) * 128], start=True, stop=True)
                return i
            p.op("pe", bc_q, r=["lrb64", "onesr_s"], w=["ps2"])
            p.op("act", (lambda h: h.activation(rqbc_s[:], ps[2][:], AF.Copy)), r=["ps2"], w=["rqbc"])
            pump(warm=1)

            def bc_b(h, cs0=cs0):
                for hh in range(H):
                    i = h.matmul(ps[2][:, hh * 128:(hh + 1) * 128],
                                 e4_s[:, hh * 128:(hh + 1) * 128],
                                 beta_b[:, cs0:cs0 + CC], start=True, stop=True)
                return i
            p.op("pe", bc_b, r=["beta_b", "e4_s"], w=["ps2"])
            p.op("act", (lambda h: h.activation(bbc_s[:], ps[2][:], AF.Copy)), r=["ps2"], w=["bbc"])
            pump(warm=1)

            def bc_k(h):
                for hh in range(H):
                    i = h.matmul(ps[3][:, hh * 128:(hh + 1) * 128], ones33_s[32:33, :],
                                 lrb64_s[32:33, hh * 128:(hh + 1) * 128], start=True, stop=True)
                return i
            p.op("pe", bc_k, r=["lrb64", "ones33_s"], w=["ps3"])
            p.op("act", (lambda h: h.activation(rkbc_s[:], ps[3][:], AF.Copy)), r=["ps3"], w=["rkbc"])
            pump(warm=1)

            for hh in range(H):
                hc = slice(hh * 128, (hh + 1) * 128)
                for ck in (2 * hh, 2 * hh + 1):
                    p.op("dve", (lambda h, ck=ck, hc=hc, cs0=cs0, cp=cp: h.tensor_tensor(
                        ql_s[:, cp, ck, :], q_s[:, ck, cs0:cs0 + CC], rqbc_s[:, hc], ALU.mult)),
                        r=[f"q_s{ck}", "rqbc"], w=[f"ql{cp}{ck}"], small=True)
                    p.op("dve", (lambda h, ck=ck, hc=hc, cs0=cs0: h.tensor_tensor(
                        kl_s[:, ck, :], k_s[:, ck, cs0:cs0 + CC], rkbc_s[:, hc], ALU.mult)),
                        r=[f"k_s{ck}", "rkbc"], w=[f"kl{ck}"], small=True)
                    p.op("dve", (lambda h, ck=ck, hc=hc: h.tensor_tensor(
                        kb_s[:, ck, :], kl_s[:, ck, :], bbc_s[:, hc], ALU.mult)),
                        r=[f"kl{ck}", "bbc"], w=[f"kb{ck}"], small=True)
                    p.op("dve", (lambda h, ck=ck, hc=hc, cs0=cs0: h.tensor_tensor(
                        vb_s[:, ck, :], v_s[:, ck, 64 + cs0:64 + cs0 + CC], bbc_s[:, hc], ALU.mult)),
                        r=[f"v_s{ck}", "bbc"], w=[f"vb{ck}"], small=True)

            for hh in range(H):
                tp = hh % 2

                def trs(h, hh=hh, tp=tp):
                    for si, src in enumerate((kl_s, kb_s, vb_s)):
                        for ci, ck in enumerate((2 * hh, 2 * hh + 1)):
                            i = h.transpose(ps_tr[:, tp, si * 256 + ci * 128: si * 256 + ci * 128 + 128],
                                            src[:, ck, :], ident_s[:])
                    return i
                p.op("pe", trs, r=[f"kl{2*hh}", f"kl{2*hh+1}", f"kb{2*hh}", f"kb{2*hh+1}",
                                   f"vb{2*hh}", f"vb{2*hh+1}", "ident_s"], w=[PTR[tp]])
                p.op("dve", (lambda h, hh=hh, tp=tp, cp=cp: h.tensor_copy(
                    ktok_s[:, cp, hh, :], ps_tr[:, tp, 0:768])), r=[PTR[tp]], w=[f"ktok{cp}{hh}"])
                pump(warm=1)

            Bx_s, Bxt_s = (Bp_s, Bpt_s) if c % 2 == 0 else (Bp2_s, Bpt2_s)
            BK, BKT = ("Bp", "Bpt") if c % 2 == 0 else ("Bq", "Bqt")
            for hh in range(H):
                def bmm(h, hh=hh, cp=cp):
                    i = h.matmul(ps[0][:, 0:128], kb_s[:, 2 * hh, :], kl_s[:, 2 * hh, :],
                                 start=True, stop=False)
                    i = h.matmul(ps[0][:, 0:128], kb_s[:, 2 * hh + 1, :], kl_s[:, 2 * hh + 1, :],
                                 start=False, stop=True)
                    i = h.matmul(ps[0][:, 128:256], kl_s[:, 2 * hh, :], kb_s[:, 2 * hh, :],
                                 start=True, stop=False)
                    i = h.matmul(ps[0][:, 128:256], kl_s[:, 2 * hh + 1, :], kb_s[:, 2 * hh + 1, :],
                                 start=False, stop=True)
                    i = h.matmul(ps[0][:, 256:384], kl_s[:, 2 * hh, :], ql_s[:, cp, 2 * hh, :],
                                 start=True, stop=False)
                    i = h.matmul(ps[0][:, 256:384], kl_s[:, 2 * hh + 1, :], ql_s[:, cp, 2 * hh + 1, :],
                                 start=False, stop=True)
                    return i
                p.op("pe", bmm, r=[f"kb{2*hh}", f"kb{2*hh+1}", f"kl{2*hh}", f"kl{2*hh+1}",
                                   f"ql{cp}{2*hh}", f"ql{cp}{2*hh+1}"], w=["ps0"])
                p.op("dve", (lambda h, hh=hh, Bx_s=Bx_s: h.tensor_tensor(
                    Bx_s[:, hh, :], ps[0][:, 0:128], mstril_s[:], ALU.mult)),
                    r=["ps0", "mstril_s"], w=[f"{BK}{hh}"], small=True)
                p.op("dve", (lambda h, hh=hh, Bxt_s=Bxt_s: h.tensor_tensor(
                    Bxt_s[:, hh, :], ps[0][:, 128:256], mstriu_s[:], ALU.mult)),
                    r=["ps0", "mstriu_s"], w=[f"{BKT}{hh}"], small=True)
                p.op("dve", (lambda h, hh=hh, cp=cp: h.tensor_tensor(
                    at_s[:, cp, hh, :], ps[0][:, 256:384], mtriu_s[:], ALU.mult)),
                    r=["ps0", "mtriu_s"], w=[f"at{cp}{hh}"], small=True)
                p.op("dve", (lambda h, hh=hh, cp=cp, Bxt_s=Bxt_s: h.tensor_sub(
                    Pt_s[:, cp, hh, :], ident_s[:], Bxt_s[:, hh, :])),
                    r=[f"{BKT}{hh}", "ident_s"], w=[f"Pt{cp}{hh}"], small=True)
                pump(warm=1)

        def pairwaves(a, b):
            CFG = ((a, 4, 0, Bp_s, Bpt_s, "Bp", "Bpt"),
                   (b, 5, 1, Bp2_s, Bpt2_s, "Bq", "Bqt"))
            for it in range(1, 7):
                last = (it == 6)
                for c, bank, pslot, Bx, Bxt, BK, BKT in CFG:
                    BALL = [f"{BK}{hh}" for hh in range(H)]
                    BTALL = [f"{BKT}{hh}" for hh in range(H)]

                    def tsq(h, bank=bank, Bx=Bx, Bxt=Bxt):
                        for hh in range(H):
                            i = h.matmul(ps[bank][:, hh * 128:(hh + 1) * 128],
                                         Bxt[:, hh, :], Bx[:, hh, :], start=True, stop=True)
                        return i
                    p.op("pe", tsq, r=BALL + BTALL, w=[PSB[bank]])
                    p.op("act", (lambda h, bank=bank, Bx=Bx: h.activation(
                        Bx[:].rearrange("p a b -> p (a b)"), ps[bank][:], AF.Copy)),
                        r=[PSB[bank]], w=BALL)
                if not last:
                    for c, bank, pslot, Bx, Bxt, BK, BKT in CFG:
                        BALL = [f"{BK}{hh}" for hh in range(H)]
                        BTALL = [f"{BKT}{hh}" for hh in range(H)]

                        def ttr(h, pslot=pslot, Bx=Bx):
                            for hh in range(H):
                                i = h.transpose(ps_tr[:, pslot, hh * 128:(hh + 1) * 128],
                                                Bx[:, hh, :], ident_s[:])
                            return i
                        p.op("pe", ttr, r=BALL + ["ident_s"], w=[PTR[pslot]])
                        p.op("dve", (lambda h, pslot=pslot, Bxt=Bxt: h.tensor_copy(
                            Bxt[:].rearrange("p a b -> p (a b)"), ps_tr[:, pslot, 0:512])),
                            r=[PTR[pslot]], w=BTALL, small=True)
                for c, bank, pslot, Bx, Bxt, BK, BKT in CFG:
                    BALL = [f"{BK}{hh}" for hh in range(H)]
                    cp = c % 2
                    PTALL = [f"Pt{cp}{hh}" for hh in range(H)]

                    def tch(h, bank=bank, Bx=Bx, cp=cp):
                        for hh in range(H):
                            h.matmul(ps[bank][:, hh * 128:(hh + 1) * 128], Bx[:, hh, :],
                                     Pt_s[:, cp, hh, :], start=True, stop=False)
                            i = h.matmul(ps[bank][:, hh * 128:(hh + 1) * 128], ident_s[:],
                                         Pt_s[:, cp, hh, :], start=False, stop=True)
                        return i
                    p.op("pe", tch, r=BALL + PTALL + ["ident_s"], w=[PSB[bank]])
                    p.op("act", (lambda h, bank=bank, cp=cp: h.activation(
                        Pt_s[:, cp, :, :].rearrange("p a b -> p (a b)"), ps[bank][:], AF.Copy)),
                        r=[PSB[bank]], w=PTALL)

        def nwsl(hh, lo):
            return nwT_s[:, hh % 2, (hh // 2) * 256 + lo:(hh // 2) * 256 + lo + 128]

        def uasl(hh):
            return uadj_s[:, hh % 2, (hh // 2) * 256:(hh // 2) * 256 + 256]

        def post_waves(c):
            cs0 = c * CC
            cp = c % 2
            PTALL = [f"Pt{cp}{hh}" for hh in range(H)]
            waves = []

            def pw0():
                for hh in range(H):
                    p.op("dve", (lambda h, hh=hh, cp=cp: h.tensor_scalar_mul(
                        npt_s[:, hh, :], Pt_s[:, cp, hh, :], -1.0)), r=[f"Pt{cp}{hh}"],
                        w=[f"npt{hh}"], small=True)
                    p.op("dve", (lambda h, hh=hh: h.tensor_copy(Sb_s[:, hh, :], S_s[:, hh, :])),
                         r=[f"S{hh}"], w=[f"Sb{hh}"])
            waves.append(pw0)

            def pw1():
                def nwmm(h, cp=cp):
                    for hh in range(H):
                        nb = hh % 2
                        cb = (hh // 2) * 256
                        h.matmul(ps[nb][:, cb:cb + 128], ktok_s[:, cp, hh, 256:384],
                                 npt_s[:, hh, :], start=True, stop=True)
                        i = h.matmul(ps[nb][:, cb + 128:cb + 256], ktok_s[:, cp, hh, 384:512],
                                     npt_s[:, hh, :], start=True, stop=True)
                    return i
                p.op("pe", nwmm, r=[f"ktok{cp}{hh}" for hh in range(H)] +
                     [f"npt{hh}" for hh in range(H)], w=["ps0", "ps1"])
                p.op("act", (lambda h: h.activation(nwT_s[:, 0, :], ps[0][:], AF.Copy)),
                     r=["ps0"], w=["nwT0"])
                p.op("act", (lambda h: h.activation(nwT_s[:, 1, :], ps[1][:], AF.Copy)),
                     r=["ps1"], w=["nwT1"])
                pump(warm=1)
            waves.append(pw1)

            def pw2():
                def uadj(h, cp=cp):
                    for hh in range(H):
                        ub = 2 + hh % 2
                        cb = (hh // 2) * 256
                        h.matmul(ps[ub][:, cb:cb + 256], Pt_s[:, cp, hh, :],
                                 ktok_s[:, cp, hh, 512:768], start=True, stop=False)
                        h.matmul(ps[ub][:, cb:cb + 256], nwsl(hh, 0), Sb_s[:, hh, 0:256],
                                 start=False, stop=False)
                        i = h.matmul(ps[ub][:, cb:cb + 256], nwsl(hh, 128), Sb_s[:, hh, 256:512],
                                     start=False, stop=True)
                    return i
                p.op("pe", uadj, r=PTALL + [f"ktok{cp}{hh}" for hh in range(H)] + ["nwT0", "nwT1"] +
                     [f"Sb{hh}" for hh in range(H)], w=["ps2", "ps3"])
                p.op("act", (lambda h: h.activation(uadj_s[:, 0, :], ps[2][:], AF.Copy)),
                     r=["ps2"], w=["uadj0"])
                p.op("act", (lambda h: h.activation(uadj_s[:, 1, :], ps[3][:], AF.Copy)),
                     r=["ps3"], w=["uadj1"])
                pump(warm=1)
            waves.append(pw2)

            def pw3():
                def omm(h, cp=cp):
                    for hh in range(H):
                        ob = hh % 2
                        cb = (hh // 2) * 256
                        h.matmul(ps[ob][:, cb:cb + 256], ql_s[:, cp, 2 * hh, :], Sb_s[:, hh, 0:256],
                                 start=True, stop=False)
                        h.matmul(ps[ob][:, cb:cb + 256], ql_s[:, cp, 2 * hh + 1, :],
                                 Sb_s[:, hh, 256:512], start=False, stop=False)
                        i = h.matmul(ps[ob][:, cb:cb + 256], at_s[:, cp, hh, :], uasl(hh),
                                     start=False, stop=True)
                    return i
                p.op("pe", omm, r=[f"ql{cp}{ck}" for ck in range(NCK)] + ["uadj0", "uadj1"] +
                     [f"Sb{hh}" for hh in range(H)] + [f"at{cp}{hh}" for hh in range(H)],
                     w=["ps0", "ps1"])
                p.op("act", (lambda h: h.activation(otok_s[:, 0, :], ps[0][:], AF.Copy)),
                     r=["ps0"], w=["otok0"])
                p.op("act", (lambda h: h.activation(otok_s[:, 1, :], ps[1][:], AF.Copy)),
                     r=["ps1"], w=["otok1"])
                pump(warm=1)
            waves.append(pw3)

            def pw4():
                for wv in range(2):
                    def smm(h, wv=wv, cp=cp):
                        for hh in (2 * wv, 2 * wv + 1):
                            sb_ = 2 + hh % 2
                            h.matmul(ps[sb_][:, 0:256], ktok_s[:, cp, hh, 0:128], uasl(hh),
                                     start=True, stop=True)
                            i = h.matmul(ps[sb_][:, 256:512], ktok_s[:, cp, hh, 128:256], uasl(hh),
                                         start=True, stop=True)
                        return i
                    p.op("pe", smm, r=[f"ktok{cp}{2*wv}", f"ktok{cp}{2*wv+1}", "uadj0", "uadj1"],
                         w=["ps2", "ps3"])
                    for hh in (2 * wv, 2 * wv + 1):
                        p.op("dve", (lambda h, hh=hh: h.tensor_add(
                            S_s[:, hh, :], S_s[:, hh, :], ps[2 + hh % 2][:])),
                            r=[PSB[2 + hh % 2]], w=[f"S{hh}"])
                    pump(warm=1)
            waves.append(pw4)

            def pw5():
                for wv in range(2):
                    def otrw(h, wv=wv):
                        for hh in (2 * wv, 2 * wv + 1):
                            tp = hh % 2
                            cb = (hh // 2) * 256
                            h.transpose(ps_tr[:, tp, 768:896], otok_s[:, tp, cb:cb + 128],
                                        ident_s[:])
                            i = h.transpose(ps_tr[:, tp, 896:1024],
                                            otok_s[:, tp, cb + 128:cb + 256], ident_s[:])
                        return i
                    p.op("pe", otrw, r=["otok0", "otok1", "ident_s"], w=[PTR[0], PTR[1]])
                    for hh in (2 * wv, 2 * wv + 1):
                        tp = hh % 2
                        p.op("act", (lambda h, hh=hh, tp=tp, cs0=cs0: h.activation(
                            delta_s[:, 2 * hh, cs0:cs0 + CC], ps_tr[:, tp, 768:896], AF.Copy)),
                            r=[PTR[tp]], w=[f"delta{2*hh}"])
                        p.op("act", (lambda h, hh=hh, tp=tp, cs0=cs0: h.activation(
                            delta_s[:, 2 * hh + 1, cs0:cs0 + CC], ps_tr[:, tp, 896:1024], AF.Copy)),
                            r=[PTR[tp]], w=[f"delta{2*hh+1}"])
                    pump(warm=1)
            waves.append(pw5)
            return waves

        NCH_ = NCH if P_DELTA else 0
        if NCH_:
            for pa in range(0, NCH_, 2):
                if pa >= 2:
                    for wfn in post_waves(pa - 2):
                        wfn()
                    for wfn in post_waves(pa - 1):
                        wfn()
                d1_chunk(pa)
                d1_chunk(pa + 1)
                pairwaves(pa, pa + 1)
            for wfn in post_waves(NCH_ - 2):
                wfn()
            for wfn in post_waves(NCH_ - 1):
                wfn()

        if debug and t == 0:
            dump("q0", (lambda: q_s[:]), [128, 8, T], BF16, [f"q_s{c}" for c in range(8)])
            dump("k0", (lambda: k_s[:]), [128, 8, T], BF16, [f"k_s{c}" for c in range(8)])
            dump("v0", (lambda: v_s[:]), [128, 8, 64 + T], BF16, [f"v_s{c}" for c in range(8)])
            dump("beta0", (lambda: beta_s[:]), [4, T], F32, ["beta_s"])
            dump("delta0", (lambda: delta_s[:]), [128, 8, T], BF16,
                 [f"delta{c}" for c in range(8)])
            dump("xn0", (lambda: xn_s[:]), [128, 8, T], BF16, [f"xn{c}" for c in range(8)])

        if P_GATES:
            def gmm(h, pr=pr):
                for tt in range(4):
                    for ki in range(NCK):
                        i = h.matmul(ps[4][:, tt * 20:tt * 20 + 20],
                                     xn_s[:, pr, ki, tt * 128:(tt + 1) * 128],
                                     gw_s[:, ki * 20:(ki + 1) * 20],
                                     start=(ki == 0), stop=False)
                    i = h.matmul(ps[4][:, tt * 20:tt * 20 + 20],
                                 onesr_s[:], gb_s[:], start=False, stop=True)
                return i
            p.op("pe", gmm, r=[f"xn{pr}{ki}" for ki in range(NCK)] + ["gw_s", "gb_s", "onesr_s"], w=["ps4"])
            p.op("dve", (lambda h: h.tensor_tensor(
                gsc_s[:], ps[4][:, 0:80], itbc_s[:], ALU.mult)),
                r=["ps4", "itbc_s"], w=["gsc"], small=True)
            if debug and t == 0:
                dump("glog0", (lambda: gsc_s[:]), [128, 80], F32, ["gsc"])
                dump("itbc", (lambda: itbc_s[:]), [128, 80], BF16, ["itbc_s"])
                dump("eabc", (lambda: eabc_s[:]), [128, 80], BF16, ["eabc_s"])
                dump("ebbc", (lambda: ebbc_s[:]), [128, 80], BF16, ["ebbc_s"])
            p.op("dve", (lambda h: h.tensor_reduce(
                gmx_s[:], gsc_s[:].rearrange("p (a b) -> p a b", b=5), AX.X, ALU.max)),
                r=["gsc"], w=["gmx"], small=True)
            if debug and t == 0:
                dump("gmx0", (lambda: gmx_s[:]), [128, 16], F32, ["gmx"])
            for j in range(5):
                p.op("dve", (lambda h, j=j: h.tensor_copy(
                    gexp_s[:].rearrange("p (a b) -> p a b", b=5)[:, :, j], gmx_s[:])),
                    r=["gmx"], w=["gexp"], small=True)
            p.op("dve", (lambda h: h.tensor_sub(gsc_s[:], gsc_s[:], gexp_s[:])),
                 r=["gexp"], w=["gsc"], small=True)
            if debug and t == 0:
                dump("gsub0", (lambda: gsc_s[:]), [128, 80], F32, ["gsc"])
            pump()
            p.op("act", (lambda h: h.activation(gsc_s[:], gsc_s[:], AF.Exp)), w=["gsc"])
            if debug and t == 0:
                dump("gexpo0", (lambda: gsc_s[:]), [128, 80], F32, ["gsc"])
            p.op("dve", (lambda h: h.tensor_reduce(
                gsum_s[:], gsc_s[:].rearrange("p (a b) -> p a b", b=5), AX.X, ALU.add)),
                r=["gsc"], w=["gsum"], small=True)
            p.op("dve", (lambda h: h.reciprocal(gsum_s[:], gsum_s[:])), w=["gsum"], small=True)
            if debug and t == 0:
                dump("gsum0", (lambda: gsum_s[:]), [128, 16], F32, ["gsum"])
            for j in range(5):
                p.op("dve", (lambda h, j=j: h.tensor_copy(
                    gexp_s[:].rearrange("p (a b) -> p a b", b=5)[:, :, j], gsum_s[:])),
                    r=["gsum"], w=["gexp"], small=True)
            p.op("dve", (lambda h: h.tensor_mul(probs_s[:], gsc_s[:], gexp_s[:])),
                 r=["gsc", "gexp"], w=["probs"], small=True)
            if debug and t == 0:
                dump("gexp0", (lambda: gexp_s[:]), [128, 80], BF16, ["gexp"])
            p.op("dve", (lambda h: h.tensor_mul(probs_s[:], probs_s[:], eabc_s[:])),
                 r=["eabc_s"], w=["probs"], small=True)
            p.op("dve", (lambda h: h.tensor_add(probs_s[:], probs_s[:], ebbc_s[:])),
                 r=["ebbc_s"], w=["probs"], small=True)
            p.op("dve", (lambda h: h.tensor_copy(probsb_s[:], probs_s[:])), r=["probs"], w=["probsb"], small=True)

            def ptr_(h):
                for tt in range(4):
                    i = h.transpose(ps_tr[0:20, 0, tt * 128:tt * 128 + 128],
                                    probsb_s[:, tt * 20:tt * 20 + 20], ident_s[:])
                return i
            p.op("pe", ptr_, r=["probsb", "ident_s"], w=[PTR[0]])
            p.op("act", (lambda h: h.activation(probsf_s[:], ps_tr[0:20, 0, 0:512], AF.Copy)),
                 r=[PTR[0]], w=["probsf"])
            pump()

        FIRDEF = ((2, 0, 16), (3, 16, 4), (4, 20, 1))
        for ck in range(NCK if P_MIX else 0):
            hh = ck // 2
            if ck % 2 == 0:
                for j in range(5):
                    r_ = hh * 5 + j
                    p.op("pe", (lambda h, r_=r_: h.matmul(
                        ps[4][:], e20_s[:, r_ * 128:(r_ + 1) * 128], probsf_s[:],
                        start=True, stop=True)), r=["probsf", "e20_s"], w=["ps4"])
                    p.op("act", (lambda h, j=j: h.activation(
                        pbc5_s[:, j, :], ps[4][:], AF.Copy)), r=["ps4"], w=[f"pbc5_{j}"])
                    pump(warm=1)
            if P_FIR:
                def fload(rc_):
                    p.op("pool", (lambda h, rc_=rc_: h.dma_start(
                        out=fcf_s[:, rc_ % 4, :], in_=fcf_d[rc_])), w=[f"fcf{rc_ % 4}"])
                if ck == 0:
                    for rc_ in range(3):
                        fload(rc_)
                for sub in range(4):
                    rc = ck * 4 + sub
                    par = rc % 4
                    if rc + 3 < 32:
                        fload(rc + 3)

                    vpar = rc % 2
                    def repl(h, ck=ck, sub=sub):
                        for r in range(4):
                            h.matmul(ps[0][32 * r:32 * r + 32, :],
                                     rsel_s[:, sub * 32:sub * 32 + 32],
                                     v_s[:, ck, r:r + 512], start=True, stop=True,
                                     tile_position=(0, 32 * r))
                        for r in range(4):
                            i = h.matmul(ps[1][32 * r:32 * r + 32, 0:61],
                                         rsel_s[:, sub * 32:sub * 32 + 32],
                                         v_s[:, ck, r + 512:r + 573], start=True, stop=True,
                                         tile_position=(0, 32 * r))
                        return i
                    p.op("pe", repl, r=[f"v_s{ck}", "rsel_s"], w=["ps0", "ps1"])
                    p.op("act", (lambda h, vpar=vpar: h.activation(
                        vrep_s[:, vpar, 0:512], ps[0][:], AF.Copy)), r=["ps0"], w=[f"vrep{vpar}"])
                    p.op("act", (lambda h, vpar=vpar: h.activation(
                        vrep_s[:, vpar, 512:573], ps[1][:, 0:61], AF.Copy)), r=["ps1"], w=[f"vrep{vpar}"])
                    pump(warm=1)

                    def firmm(h, sub=sub, par=par, vpar=vpar):
                        for bank, goff, G in FIRDEF:
                            offbase = 64 - (4 * G - 1)
                            for g in range(G):
                                off = offbase + 4 * g
                                i = h.matmul(ps[bank][32 * sub:32 * sub + 32, :],
                                             fcf_s[:, par, (goff + g) * 32:(goff + g + 1) * 32],
                                             vrep_s[:, vpar, off:off + 512],
                                             start=(g == 0), stop=(g == G - 1),
                                             tile_position=(0, 32 * sub))
                        return i
                    p.op("pe", firmm, r=[f"fcf{par}", f"vrep{vpar}"],
                         w=[PSB[2], PSB[3], PSB[4]])
                    pump(warm=1)
            mdefs = [
                ("L", (lambda: ps[2][:]), PSB[2], 2),
                ("M", (lambda: ps[3][:]), PSB[3], 1),
                ("S", (lambda: ps[4][:]), PSB[4], 0),
                ("d", (lambda ck=ck: delta_s[:, ck, :]), f"delta{ck}", 3),
                ("v", (lambda ck=ck: v_s[:, ck, 64:64 + T]), f"v_s{ck}", 4),
            ]
            if P_FIR:
                p.op("pool", (lambda h, ck=ck: h.tensor_tensor(
                    mixtp_s[:], delta_s[:, ck, :], pbc5_s[:, 3, :], ALU.mult)),
                    r=[f"delta{ck}", "pbc5_3"], w=["mixtp"], dma=False)
                p.op("pool", (lambda h, ck=ck: h.tensor_tensor(
                    mixq_s[:], v_s[:, ck, 64:64 + T], pbc5_s[:, 4, :], ALU.mult)),
                    r=[f"v_s{ck}", "pbc5_4"], w=["mixq"], dma=False)
                p.op("pool", (lambda h: h.tensor_add(
                    mixtp_s[:], mixtp_s[:], mixq_s[:])), r=["mixq"], w=["mixtp"], dma=False)
                for bi, (nm, srcap, rk, j) in enumerate(mdefs[:3]):
                    if bi == 0:
                        p.op("dve", (lambda h, srcap=srcap, j=j, ck=ck: h.tensor_tensor(
                            omix_s[:, ck, :], srcap(), pbc5_s[:, j, :], ALU.mult)),
                            r=[rk, f"pbc5_{j}"], w=[f"omix{ck}"])
                    else:
                        p.op("dve", (lambda h, srcap=srcap, j=j: h.tensor_tensor(
                            mixt_s[:], srcap(), pbc5_s[:, j, :], ALU.mult)),
                            r=[rk, f"pbc5_{j}"], w=["mixt"])
                        p.op("dve", (lambda h, ck=ck: h.tensor_add(
                            omix_s[:, ck, :], omix_s[:, ck, :], mixt_s[:])),
                            r=["mixt"], w=[f"omix{ck}"])
            else:
                p.op("dve", (lambda h, ck=ck: h.tensor_tensor(
                    omix_s[:, ck, :], delta_s[:, ck, :], pbc5_s[:, 3, :], ALU.mult)),
                    r=[f"delta{ck}", "pbc5_3"], w=[f"omix{ck}"])
                p.op("pool", (lambda h, ck=ck: h.tensor_tensor(
                    mixtp_s[:], v_s[:, ck, 64:64 + T], pbc5_s[:, 4, :], ALU.mult)),
                    r=[f"v_s{ck}", "pbc5_4"], w=["mixtp"], dma=False)
            p.op("dve", (lambda h, ck=ck: h.tensor_add(
                omix_s[:, ck, :], omix_s[:, ck, :], mixtp_s[:])),
                r=["mixtp"], w=[f"omix{ck}"])
            pump(warm=1)

        for hh in range(H if P_MIX else 0):
            row = 32 * hh
            for ci, ck2 in enumerate((2 * hh, 2 * hh + 1)):
                p.op("dve", (lambda h, ck2=ck2, ci=ci: h.tensor_tensor(
                    sq_s[:, 4 * ci:4 * ci + 4, :].rearrange("p a b -> p (a b)"),
                    omix_s[:, ck2, :], omix_s[:, ck2, :], ALU.mult)),
                    r=[f"omix{ck2}"], w=[f"xsq{ci}"])
            def rmm(h, row=row):
                i = h.matmul(ps[4][row:row + 1, :], onesc_s[:],
                             sq_s[:, 0:4, :].rearrange("p a b -> p (a b)"),
                             start=True, stop=False, tile_position=(0, row))
                i = h.matmul(ps[4][row:row + 1, :], onesc_s[:],
                             sq_s[:, 4:8, :].rearrange("p a b -> p (a b)"),
                             start=False, stop=True, tile_position=(0, row))
                return i
            p.op("pe", rmm, r=["xsq0", "xsq1", "onesc_s"], w=["ps4"])
            p.op("dve", (lambda h, row=row: h.tensor_scalar(
                lr_row[:], ps[4][row:row + 1, :], 1.0 / DKV, 1e-5, ALU.mult, ALU.add)),
                r=["ps4"], w=["lr_row"], small=True)
            p.op("act", (lambda h: h.activation(lr_row[:], lr_row[:], AF.Ln)), w=["lr_row"])
            p.op("act", (lambda h: h.activation(
                lr_row[:], lr_row[:], AF.Exp, scale=-0.5)), w=["lr_row"])
            pump()
            p.op("dve", (lambda h: h.tensor_copy(
                lrb64_s[0:1, :], lr_row[:])), r=["lr_row"], w=["lrb64"], small=True)
            tp = hh % 2
            p.op("pe", (lambda h, tp=tp: h.matmul(
                ps[2 + tp][:], onesr_s[:], lrb64_s[0:1, :], start=True, stop=True)),
                r=["lrb64", "onesr_s"], w=[PSB[2 + tp]])
            for ck2 in (2 * hh, 2 * hh + 1):
                p.op("dve", (lambda h, ck2=ck2, tp=tp: h.tensor_mul(
                    omix_s[:, ck2, :], omix_s[:, ck2, :], ps[2 + tp][:])),
                    r=[PSB[2 + tp]], w=[f"omix{ck2}"])
                p.op("dve", (lambda h, ck2=ck2: h.tensor_scalar_mul(
                    delta_s[:, ck2, :], omix_s[:, ck2, :], nw_s[:, ck2:ck2 + 1])),
                    r=[f"omix{ck2}", "nw_s"], w=[f"delta{ck2}"])
                pump()
        if P_MIX:
            p.op("sp", (lambda h, t0=t0: h.dma_start(out=on_h[:, :, t0:t0 + T], in_=delta_s[:])),
                 r=[f"delta{ck2}" for ck2 in range(NCK)], w=["on_h"])

        if debug and t == 0:
            dump("probs0", (lambda: probs_s[:]), [128, 80], F32, ["probs"])
            dump("onorm0", (lambda: delta_s[:]), [128, 8, T], BF16,
                 [f"delta{c}" for c in range(8)])

        for ckk in range(NCK):
            p.op("dve", (lambda h, ckk=ckk: h.tensor_copy(
                v_s[:, ckk, 0:64], v_s[:, ckk, T:T + 64])), w=[f"v_s{ckk}"], small=True)
        bg_drain(drain_target)

    bgs["banks"] = (0, 1, 2, 5)
    bg_drain(len(BGQ))

    p.join()
    for g in reversed(ag):
        g.__exit__(None, None, None)

    fg = []

    def fsb(name, shape, dt):
        g = nc.sbuf_tensor(name, list(shape), dt)
        tt_ = g.__enter__()
        fg.append(g)
        return tt_

    def hidden_layer(src_h, dst_h, w_d_, s_d_, tag, qkeys):
        TQ = 1024
        hin = fsb(f"hin{tag}", [128, 2, 32, TQ], BF16)
        wb_ = fsb(f"whb{tag}", [128, 2, 32, 128], BF16)
        ss_ = fsb(f"ss{tag}", [128, 32], F32)
        ost = fsb(f"ost{tag}", [128, 2, TQ], BF16)
        p.op("sp", (lambda h: h.dma_start(out=ss_[:], in_=s_d_)), w=[f"ss{tag}"])

        def hload(t4):
            hp = t4 % 2
            for kq in range(4):
                p.op("sp", (lambda h, t4=t4, hp=hp, kq=kq: h.dma_start(
                    out=hin[:, hp, kq * 8:(kq + 1) * 8, :],
                    in_=src_h[:, kq * 8:(kq + 1) * 8, t4 * TQ:(t4 + 1) * TQ])),
                    r=qkeys(t4), w=[f"hin{tag}{hp}"])

        hload(0)
        for t4 in range(4):
            hp = t4 % 2
            if t4 + 1 < 4:
                hload(t4 + 1)
            for co in range(32):
                wp = co % 2
                p.op("pool", (lambda h, co=co, wp=wp: h.dma_start(
                    out=wb_[:, wp, :, :], in_=w_d_[co])), w=[f"whb{tag}{wp}"])
                for sub in range(2):
                    bb = 2 * (co % 2) + (sub % 2)
                    def f2mm(h, wp=wp, sub=sub, bb=bb, hp=hp):
                        for ki in range(32):
                            i = h.matmul(ps[bb][:], wb_[:, wp, ki, :],
                                         hin[:, hp, ki, sub * T:(sub + 1) * T],
                                         start=(ki == 0), stop=(ki == 31))
                        return i
                    p.op("pe", f2mm, r=[f"whb{tag}{wp}", f"hin{tag}{hp}"], w=[PSB[bb]])
                    p.op("act", (lambda h, co=co, wp=wp, sub=sub, bb=bb: h.activation(
                        ost[:, wp, sub * T:(sub + 1) * T], ps[bb][:], AF.Gelu,
                        scale=ss_[:, co:co + 1])), r=[PSB[bb], f"ss{tag}"], w=[f"ost{tag}{wp}"])
                p.op("sp", (lambda h, co=co, wp=wp, t4=t4: h.dma_start(
                    out=dst_h[:, co, t4 * TQ:(t4 + 1) * TQ], in_=ost[:, wp, :])),
                    r=[f"ost{tag}{wp}"], w=[f"{tag}dst", f"{tag}dst{t4}"])

    if P_F2:
        hidden_layer(h0_h, h1o_h, h1_d, sh1_d, "h1",
                     lambda t4: [f"h0_h{2 * t4}", f"h0_h{2 * t4 + 1}"])
    p.join()
    for g in reversed(fg):
        g.__exit__(None, None, None)
    fg = []
    if P_F3:
        hidden_layer(h1o_h, h2o_h, h2_d, sh2_d, "h2", lambda t4: [f"h1dst{t4}"])
    p.join()
    for g in reversed(fg):
        g.__exit__(None, None, None)
    fg = []

    h2in = fsb("h2in", [128, 2, 32, T], BF16)
    onin = fsb("onin", [128, 2, 8, T], BF16)
    xf32 = fsb("xf32", [128, 8, T], F32)
    w3s_ = fsb("w3s_", [128, 8, 32, 128], BF16)
    wos_ = fsb("wos_", [128, 8 * 8 * 128], BF16)
    s3_s = fsb("s3_s", [128, 8], F32)
    outst = fsb("outst", [128, 2, T], F32)

    p.op("sp", (lambda h: h.dma_start(out=s3_s[:], in_=s3_d)), w=["s3_s"])
    p.op("pool", (lambda h: h.dma_start(out=w3s_[:, :, :, :], in_=w3_d)), w=["w3s"])
    p.op("pool", (lambda h: h.dma_start(
        out=wos_[:], in_=wo_d.rearrange("p a b c -> p (a b c)"))), w=["wos"])

    def f4load(tt):
        fp = tt % 2
        p.op("sp", (lambda h, tt=tt, fp=fp: h.dma_start(
            out=h2in[:, fp, :, :], in_=h2o_h[:, :, tt * T:(tt + 1) * T])),
            r=[f"h2dst{tt // 2}"], w=[f"h2in{fp}"])
        p.op("pool", (lambda h, tt=tt, fp=fp: h.dma_start(
            out=onin[:, fp, :, :], in_=on_h[:, :, tt * T:(tt + 1) * T])),
            r=["on_h"], w=[f"onin{fp}"])

    def x4load(tt):
        p.op("pool", (lambda h, tt=tt: h.dma_start(
            out=xf32[:],
            in_=x_in.rearrange("(ck p) t -> p ck t", p=128)[:, :, tt * T:(tt + 1) * T])),
            w=["xf32"])

    if P_F4:
        f4load(0)
        x4load(0)
    for tt in range(NT if P_F4 else 0):
        fp = tt % 2
        if tt + 1 < NT:
            f4load(tt + 1)
        for co in range(8):
            bf_ = 2 * (co % 2)
            ba_ = bf_ + 1
            def f4mm(h, fp=fp, co=co, bf_=bf_, ba_=ba_):
                for ki in range(32):
                    i = h.matmul(ps[bf_][:], w3s_[:, co, ki, :], h2in[:, fp, ki, :],
                                 start=(ki == 0), stop=(ki == 31))
                for ki in range(8):
                    i = h.matmul(ps[ba_][:], w_at(wos_, ki, co), onin[:, fp, ki, :],
                                 start=(ki == 0), stop=(ki == 7))
                return i
            p.op("pe", f4mm, r=[f"h2in{fp}", f"onin{fp}", "w3s", "wos"], w=[PSB[bf_], PSB[ba_]])
            osl = co % 2
            p.op("dve", (lambda h, co=co, bf_=bf_, osl=osl: h.scalar_tensor_tensor(
                outst[:, osl, :], ps[bf_][:], s3_s[:, co:co + 1], xf32[:, co, :],
                ALU.mult, ALU.add)), r=[PSB[bf_], "s3_s", "xf32"], w=[f"outst{osl}"])
            p.op("dve", (lambda h, ba_=ba_, osl=osl: h.tensor_add(
                outst[:, osl, :], outst[:, osl, :], ps[ba_][:])),
                r=[PSB[ba_]], w=[f"outst{osl}"])
            p.op("sp", (lambda h, tt=tt, co=co, osl=osl: h.dma_start(
                out=out_d[:, co, tt * T:(tt + 1) * T], in_=outst[:, osl, :])),
                r=[f"outst{osl}"], w=["out_done"])
        if tt + 1 < NT:
            x4load(tt + 1)

    if debug:
        dump("on_h", (lambda: on_h), [128, 8, L], BF16, ["on_h"])
        dump("h0_h", (lambda: h0_h), [128, 32, L], BF16, ["h0_h"])
        dump("h1_h", (lambda: h1o_h), [128, 32, L], BF16, ["h1dst"])
        dump("h2_h", (lambda: h2o_h), [128, 32, L], BF16, ["h2dst"])
    p.waitop("sp", r=["out_done"], xw=([p.last_lab["sp"]] if debug else []))

    p.emit(nc)
    return nc, dbg_outs


def host_inputs(inputs):
    g = {}

    def blk8(w):
        wt = np.ascontiguousarray(w.T)
        return np.ascontiguousarray(wt.reshape(8, 128, 8, 128).transpose(1, 0, 2, 3))

    def blk8co(w):
        wt = np.ascontiguousarray(w.T)
        return np.ascontiguousarray(wt.reshape(8, 128, 8, 128).transpose(1, 2, 0, 3))

    g["wq"] = _bf(blk8co(inputs["L0_attn_Wq"]))
    g["wk"] = _bf(blk8co(inputs["L0_attn_Wk"]))
    g["wv"] = _bf(blk8co(inputs["L0_attn_Wv"]))
    g["wo"] = blk8(inputs["L0_attn_Wo"])
    g["wb"] = np.ascontiguousarray(
        inputs["L0_attn_Wb"].T.reshape(8, 128, 4).transpose(1, 0, 2))
    g["gw"] = np.ascontiguousarray(
        inputs["L0_attn_gate_w"].reshape(20, 1024).T.reshape(8, 128, 20).transpose(1, 0, 2))
    g["gb"] = _bf(inputs["L0_attn_gate_b"].reshape(1, 20))
    g["ltmp"] = np.ascontiguousarray(
        np.repeat(inputs["L0_attn_log_temp"].reshape(4), 5).reshape(1, 20))
    g["eps"] = np.ascontiguousarray(
        np.repeat(inputs["L0_attn_eps_param"].reshape(4), 5).reshape(1, 20))
    nw = inputs["L0_attn_norm_w"]
    g["nw"] = np.ascontiguousarray(
        np.stack([nw[(ck % 2) * 128:(ck % 2) * 128 + 128] for ck in range(8)], axis=1))
    for nm, key in (("cq", "L0_attn_conv_q"), ("ck", "L0_attn_conv_k"), ("cv", "L0_attn_conv_v")):
        g[nm] = np.ascontiguousarray(inputs[key].reshape(8, 128, 4).transpose(1, 0, 2))
    fcf = np.zeros((32, 128, 21, 32), np.float32)
    firs = [(0, inputs["L0_attn_fir_l"].reshape(1024, KL), KL, 16),
            (16, inputs["L0_attn_fir_m"].reshape(1024, KM), KM, 4),
            (20, inputs["L0_attn_fir_s"].reshape(1024, KS), KS, 1)]
    cidx = np.arange(32)
    for rc in range(32):
        ch0 = (rc // 4) * 128 + (rc % 4) * 32
        for goff, fl_, K, G in firs:
            pad = 4 * G - K
            for gi in range(G):
                for r in range(4):
                    jp = 4 * gi + r - pad
                    if jp >= 0:
                        fcf[rc, 32 * r + cidx, goff + gi, cidx] = fl_[ch0:ch0 + 32, jp]
    g["fcf"] = np.ascontiguousarray(fcf.reshape(32, 128, 21 * 32))
    rsel = np.zeros((128, 4, 32), np.float32)
    for b in range(4):
        rsel[32 * b + cidx, b, cidx] = 1.0
    g["rsel"] = _bf(rsel.reshape(128, 128))

    def blkffn(w, nki, nco):
        wt = np.ascontiguousarray(w.T)
        return np.ascontiguousarray(
            wt.reshape(nki, 128, nco, 128).transpose(2, 1, 0, 3))

    g["w1"] = _bf(blkffn(inputs["L0_ffn_w1_w"], 8, 32))
    g["w2"] = _bf(blkffn(inputs["L0_ffn_w2_w"], 8, 32))
    g["h1"] = _bf(blkffn(inputs["L0_ffn_h1_w"], 32, 32))
    g["h2"] = _bf(blkffn(inputs["L0_ffn_h2_w"], 32, 32))
    g["w3"] = _bf(np.ascontiguousarray(
        np.ascontiguousarray(inputs["L0_ffn_w3_w"].T).reshape(32, 128, 8, 128)
        .transpose(1, 2, 0, 3)))
    g["s1"] = np.ascontiguousarray(inputs["L0_ffn_w1_s"].reshape(32, 128).T)
    g["s2"] = np.ascontiguousarray(inputs["L0_ffn_w2_s"].reshape(32, 128).T)
    g["sh1"] = np.ascontiguousarray(inputs["L0_ffn_h1_s"].reshape(32, 128).T)
    g["sh2"] = np.ascontiguousarray(inputs["L0_ffn_h2_s"].reshape(32, 128).T)
    g["s3"] = np.ascontiguousarray(inputs["L0_ffn_w3_s"].reshape(8, 128).T)

    g["ident"] = _bf(np.eye(128, dtype=np.float32))
    g["onesc"] = _bf(np.ones((128, 1), np.float32))
    g["onesr"] = _bf(np.ones((1, 128), np.float32))
    g["ones33"] = _bf(np.ones((33, 128), np.float32))
    e4 = np.zeros((4, 4, 128), np.float32)
    for hh in range(4):
        e4[hh, hh, :] = 1.0
    g["e4"] = _bf(e4)
    e20 = np.zeros((20, 20, 128), np.float32)
    for r in range(20):
        e20[r, r, :] = 1.0
    g["e20"] = _bf(e20)
    one = np.ones((128, 128), np.float32)
    g["mstril"] = _bf(np.tril(one, -1))
    g["mstriu"] = _bf(np.triu(one, 1))
    g["mtriu"] = _bf(np.triu(one, 0))
    for k in g:
        if g[k].dtype == np.float64:
            g[k] = g[k].astype(np.float32)
    return g


_CACHE = {}


def kernel(**inputs):
    x = np.asarray(inputs["L0_attn_hidden_states"], np.float32)
    B = x.shape[0]
    shared = host_inputs(inputs)
    if "nc" not in _CACHE:
        _CACHE["nc"], _CACHE["dbg"] = build_program(debug=DBG)
    nc = _CACHE["nc"]
    in_maps = []
    for b in range(B):
        m = dict(shared)
        m["x"] = np.ascontiguousarray(x[b].T)
        in_maps.append(m)
    res = run_bass_kernel_spmd(nc, in_maps, core_ids=list(range(B)))
    outs = []
    for b in range(B):
        o = np.asarray(res.results[b]["out"])
        outs.append(o.transpose(2, 1, 0).reshape(L, D))
    if DBG:
        kernel.last_results = res
    return np.stack(outs, 0).astype(np.float32)


# version 20
# speedup vs baseline: 1.0269x; 1.0269x over previous
import numpy as np
import ml_dtypes
import concourse.bass as bass
import concourse.mybir as mybir
from concourse.bass_utils import run_bass_kernel_spmd

F32 = mybir.dt.float32
BF16 = mybir.dt.bfloat16
AF = mybir.ActivationFunctionType
ALU = mybir.AluOpType
AX = mybir.AxisListType

D = 1024
L = 4096
H = 4
DKV = 256
T = 512
NT = L // T
NCK = 8
CC = 128
NCH = T // CC
KS, KM, KL = 3, 15, 63
DH = 4096

DBG = False
P_NT = NT
P_ATTN = True
P_DELTA = True
P_FIR = True
P_MIX = True
P_PROJ = True
P_GATES = True
P_F1 = True
P_F2 = True
P_F3 = True
P_F4 = True

ENG2BLK = {"pe": "tensor", "act": "scalar", "dve": "vector", "pool": "gpsimd", "sp": "sync"}
DMA_ENG = {"sp", "pool"}


class _Res:
    __slots__ = ("w", "rs")

    def __init__(self):
        self.w = None
        self.rs = []


DMA_NSEM = 12


class Prog:
    def __init__(self):
        self.items = {e: [] for e in ENG2BLK}
        self.lab = []
        self.cnt = {e: 0 for e in ENG2BLK}
        for e in DMA_ENG:
            self.cnt[e + "_c"] = 0
            for i in range(DMA_NSEM):
                self.cnt[f"{e}{i}"] = 0
        self.dmasem_i = {e: 0 for e in DMA_ENG}
        self.sem_last = {}
        self.res = {}
        self.pending_join = None
        self.join_left = set()
        self.last_lab = {}

    def R(self, key):
        r = self.res.get(key)
        if r is None:
            r = self.res[key] = _Res()
        return r

    def op(self, eng, fn, r=(), w=(), xw=(), small=False, dma=None):
        is_dma = (eng in DMA_ENG) if dma is None else dma
        waits = []
        for k in r:
            rr = self.R(k)
            if rr.w is not None:
                waits.append(rr.w)
        for k in w:
            rr = self.R(k)
            waits.extend(rr.rs)
            if rr.w is not None:
                waits.append(rr.w)
        waits.extend(xw)
        if self.pending_join is not None and eng in self.join_left:
            waits.extend(self.pending_join)
            self.join_left.discard(eng)
            if not self.join_left:
                self.pending_join = None
        if is_dma:
            si = self.dmasem_i[eng]
            self.dmasem_i[eng] = (si + 1) % DMA_NSEM
            semkey = f"{eng}{si}"
            if semkey in self.sem_last:
                waits.append(self.sem_last[semkey])
            self.cnt[semkey] += 16
        else:
            semkey = eng if eng not in DMA_ENG else eng + "_c"
            self.cnt[semkey] += 1
        lid = len(self.lab)
        self.lab.append((semkey, self.cnt[semkey], is_dma or small or eng in DMA_ENG,
                         16 if is_dma else 1))
        self.sem_last[semkey] = lid
        self.items[eng].append((fn, tuple(sorted(set(waits))), lid))
        self.last_lab[eng] = lid
        for k in r:
            self.R(k).rs.append(lid)
        for k in w:
            rr = self.R(k)
            rr.w = lid
            rr.rs = []
        return lid

    def waitop(self, eng, r=(), xw=()):
        waits = list(xw)
        for k in r:
            rr = self.R(k)
            if rr.w is not None:
                waits.append(rr.w)
        self.items[eng].append((None, tuple(sorted(set(waits))), None))

    def join(self):
        labs = [self.last_lab[e] for e in ENG2BLK if e in self.last_lab]
        self.pending_join = labs
        self.join_left = set(ENG2BLK)

    def emit(self, nc):
        semkeys = list(self.cnt.keys())
        sems = {k: nc.semaphore("sem_" + k).__enter__() for k in semkeys}
        lab = self.lab
        for k in semkeys:
            nc.sync.sem_clear(sems[k])
        nc.all_engine_barrier()
        with nc.Block() as blk:
            for eng, blkname in ENG2BLK.items():
                items = self.items[eng]

                def body(h, eng=eng, items=items):
                    seen = {k: 0 for k in semkeys}
                    for fn, waits, lid in items:
                        tgt = {}
                        for wl in waits:
                            we, wc, nofilt, _inc = lab[wl]
                            if we == eng and not nofilt:
                                continue
                            if wc > tgt.get(we, 0):
                                tgt[we] = wc
                        for we in sorted(tgt):
                            wc = tgt[we]
                            if wc > seen[we]:
                                h.wait_ge(sems[we], wc)
                                seen[we] = wc
                        if fn is None:
                            continue
                        inst = fn(h)
                        inst.then_inc(sems[lab[lid][0]], lab[lid][3])

                getattr(blk, blkname)(body)


def _bf(x):
    return np.ascontiguousarray(x).astype(ml_dtypes.bfloat16)


def build_program(debug=False):
    nc = bass.Bass("TRN2", target_bir_lowering=False)
    p = Prog()

    def din(name, shape, dt=F32):
        return nc.dram_tensor(name, list(shape), dt, kind="ExternalInput").ap()

    x_in = din("x", [D, L])
    wq_d = din("wq", [128, 8, 8, 128], BF16)
    wk_d = din("wk", [128, 8, 8, 128], BF16)
    wv_d = din("wv", [128, 8, 8, 128], BF16)
    wo_d = din("wo", [128, 8, 8, 128])
    wb_d = din("wb", [128, 8, 4])
    gw_d = din("gw", [128, 8, 20])
    gb_d = din("gb", [1, 20], BF16)
    ltmp_d = din("ltmp", [1, 20])
    eps_d = din("eps", [1, 20])
    nw_d = din("nw", [128, 8])
    cq_d = din("cq", [128, 8, 4])
    ck_d = din("ck", [128, 8, 4])
    cv_d = din("cv", [128, 8, 4])
    fcf_d = din("fcf", [32, 128, 21 * 32])
    rsel_d = din("rsel", [128, 4 * 32], BF16)
    w1_d = din("w1", [32, 128, 8, 128], BF16)
    w2_d = din("w2", [32, 128, 8, 128], BF16)
    h1_d = din("h1", [32, 128, 32, 128], BF16)
    h2_d = din("h2", [32, 128, 32, 128], BF16)
    w3_d = din("w3", [128, 8, 32, 128], BF16)
    s1_d = din("s1", [128, 32])
    s2_d = din("s2", [128, 32])
    sh1_d = din("sh1", [128, 32])
    sh2_d = din("sh2", [128, 32])
    s3_d = din("s3", [128, 8])
    ident_d = din("ident", [128, 128], BF16)
    onesc_d = din("onesc", [128, 1], BF16)
    onesr_d = din("onesr", [1, 128], BF16)
    ones33_d = din("ones33", [33, 128], BF16)
    e4_d = din("e4", [4, 4, 128], BF16)
    e20_d = din("e20", [20, 20, 128], BF16)
    mstril_d = din("mstril", [128, 128], BF16)
    mstriu_d = din("mstriu", [128, 128], BF16)
    mtriu_d = din("mtriu", [128, 128], BF16)

    out_d = nc.dram_tensor("out", [128, 8, L], F32, kind="ExternalOutput").ap()

    on_h = nc.dram_tensor("on_h", [128, 8, L], BF16, kind="Internal").ap()
    h0_h = nc.dram_tensor("h0_h", [128, 32, L], BF16, kind="Internal").ap()
    h1o_h = nc.dram_tensor("h1o_h", [128, 32, L], BF16, kind="Internal").ap()
    h2o_h = nc.dram_tensor("h2o_h", [128, 32, L], BF16, kind="Internal").ap()

    dbg_outs = {}

    def dump(name, src_fn, shape, dt, rkeys):
        if not debug:
            return
        ap = nc.dram_tensor("dbg_" + name, list(shape), dt, kind="ExternalOutput").ap()
        dbg_outs[name] = ap
        p.op("sp", (lambda h, ap=ap, src_fn=src_fn: h.dma_start(out=ap, in_=src_fn())), r=rkeys)

    ps = [nc.alloc_psum_tensor(f"ps{i}", [128, 512], F32) for i in range(6)]
    ps_tr = nc.alloc_psum_tensor("ps_tr", [128, 2, 1024], BF16)
    PSB = [f"ps{i}" for i in range(6)]
    PTR = ["ptr0", "ptr1"]

    sb = nc.alloc_sbuf_tensor
    ident_s = sb("ident_s", [128, 128], BF16)
    onesc_s = sb("onesc_s", [128, 1], BF16)
    onesr_s = sb("onesr_s", [1, 128], BF16)
    ones33_s = sb("ones33_s", [33, 128], BF16)
    e4_s = sb("e4_s", [4, 4 * 128], BF16)
    e20_s = sb("e20_s", [20, 20 * 128], BF16)
    mstril_s = sb("mstril_s", [128, 128], BF16)
    mstriu_s = sb("mstriu_s", [128, 128], BF16)
    mtriu_s = sb("mtriu_s", [128, 128], BF16)
    gb_s = sb("gb_s", [1, 20], BF16)
    eps_s = sb("eps_s", [1, 20], F32)
    itbc_s = sb("itbc_s", [128, 80], BF16)
    eabc_s = sb("eabc_s", [128, 80], BF16)
    ebbc_s = sb("ebbc_s", [128, 80], BF16)
    nw_s = sb("nw_s", [128, 8], F32)

    warm_s = sb("warm_s", [1, 4], F32)
    def warmup(h):
        h.memzero(warm_s[:])
        h.activation(warm_s[:], warm_s[:], AF.Exp)
        h.activation(warm_s[:], warm_s[:], AF.Ln)
        h.activation(warm_s[:], warm_s[:], AF.Silu)
        h.activation(warm_s[:], warm_s[:], AF.Gelu)
        h.activation(warm_s[:], warm_s[:], AF.Sigmoid)
        h.activation(warm_s[:], warm_s[:], AF.Exp)
        return h.activation(warm_s[:], warm_s[:], AF.Copy)
    p.op("act", warmup, w=["warm"])

    for dst, src in [(ident_s, ident_d), (onesc_s, onesc_d), (onesr_s, onesr_d),
                     (mstril_s, mstril_d), (mstriu_s, mstriu_d), (mtriu_s, mtriu_d),
                     (gb_s, gb_d)]:
        p.op("sp", (lambda h, dst=dst, src=src: h.dma_start(out=dst[:], in_=src)),
             w=[dst.name])
    p.op("sp", (lambda h: h.dma_start(out=e4_s[:], in_=e4_d.rearrange("a b c -> a (b c)"))), w=["e4_s"])
    p.op("sp", (lambda h: h.dma_start(out=ones33_s[:], in_=ones33_d)), w=["ones33_s"])
    p.op("sp", (lambda h: h.dma_start(out=e20_s[:], in_=e20_d.rearrange("a b c -> a (b c)"))), w=["e20_s"])
    p.op("sp", (lambda h: h.dma_start(out=nw_s[:], in_=nw_d)), w=["nw_s"])

    ltmp_s = sb("ltmp_s", [1, 20], F32)
    p.op("sp", (lambda h: h.dma_start(out=ltmp_s[:], in_=ltmp_d)), w=["ltmp_s"])
    p.op("sp", (lambda h: h.dma_start(out=eps_s[:], in_=eps_d)), w=["eps_s"])

    r1_s = sb("r1_s", [1, 20], F32)
    r2_s = sb("r2_s", [1, 20], F32)
    r1b_s = sb("r1b_s", [1, 20], BF16)

    p.op("act", (lambda h: h.activation(r1_s[:], ltmp_s[:], AF.Exp, scale=-1.0)),
         r=["ltmp_s"], w=["r1_s"])
    p.op("dve", (lambda h: h.tensor_copy(r1b_s[:], r1_s[:])), r=["r1_s"], w=["r1b_s"], small=True)

    def bc80(h, rowap):
        for tt in range(4):
            i = h.matmul(ps[2][0:128, tt * 20:tt * 20 + 20], onesr_s[:], rowap,
                         start=True, stop=True)
        return i

    p.op("pe", (lambda h: bc80(h, r1b_s[:])), r=["r1b_s", "onesr_s"], w=["ps2"])
    p.op("act", (lambda h: h.activation(itbc_s[:], ps[2][0:128, 0:80], AF.Copy)),
         r=["ps2"], w=["itbc_s"])
    p.op("dve", (lambda h: h.tensor_scalar(r2_s[:], eps_s[:], 0.0, 0.2, ALU.max, ALU.min)),
         r=["eps_s"], w=["r2_s"], small=True)
    r2b_s = sb("r2b_s", [1, 20], BF16)
    p.op("dve", (lambda h: h.tensor_copy(r2b_s[:], r2_s[:])), r=["r2_s"], w=["r2b_s"], small=True)
    p.op("pe", (lambda h: bc80(h, r2b_s[:])), r=["r2b_s", "onesr_s"], w=["ps2"])
    p.op("act", (lambda h: h.activation(ebbc_s[:], ps[2][0:128, 0:80], AF.Copy)),
         r=["ps2"], w=["ebbc_s"])
    if debug:
        dump("r2", (lambda: r2_s[:]), [1, 20], F32, ["r2_s"])
        dump("r2b", (lambda: r2b_s[:]), [1, 20], BF16, ["r2b_s"])
        dump("eps_in", (lambda: eps_s[:]), [1, 20], F32, ["eps_s"])
    p.op("dve", (lambda h: h.tensor_scalar(eabc_s[:], ebbc_s[:], -5.0, 1.0, ALU.mult, ALU.add)),
         r=["ebbc_s"], w=["eabc_s"], small=True)

    ag = []

    def asb(name, shape, dt):
        g = nc.sbuf_tensor(name, list(shape), dt)
        t = g.__enter__()
        ag.append(g)
        return t

    wqr = asb("wqr", [128, 4, 8, 128], BF16)
    wkr = asb("wkr", [128, 4, 8, 128], BF16)
    wvr = asb("wvr", [128, 4, 8, 128], BF16)
    wb_s = asb("wb_s", [128, 8 * 4], BF16)
    gw_s = asb("gw_s", [128, 8 * 20], BF16)
    cq_s = asb("cq_s", [128, 8 * 4], F32)
    ck_s = asb("ck_s", [128, 8 * 4], F32)
    cv_s = asb("cv_s", [128, 8 * 4], F32)
    fcf_s = asb("fcf_s", [128, 4, 21 * 32], BF16)
    vrep_s = asb("vrep_s", [128, 2, 573], BF16)
    rsel_s = asb("rsel_s", [128, 4 * 32], BF16)
    w1r = asb("w1r", [128, 3, 8, 128], BF16)
    w2r = asb("w2r", [128, 3, 8, 128], BF16)
    t1r = asb("t1r", [128, 2, T], BF16)
    h0st = asb("h0st", [128, 2, T], BF16)
    s1_s = asb("s1_s", [128, 32], F32)
    s2_s = asb("s2_s", [128, 32], F32)

    for dst, src in [(wb_s, wb_d), (gw_s, gw_d)]:
        p.op("pool", (lambda h, dst=dst, src=src: h.dma_start(
            out=dst[:], in_=src.rearrange("p a b -> p (a b)") if src.ndim == 3
            else src.rearrange("p a b c -> p (a b c)"))), w=[dst.name])
    for dst, src in [(cq_s, cq_d), (ck_s, ck_d), (cv_s, cv_d)]:
        p.op("sp", (lambda h, dst=dst, src=src: h.dma_start(
            out=dst[:], in_=src.rearrange("p a b -> p (a b)"))), w=[dst.name])
    p.op("sp", (lambda h: h.dma_start(out=rsel_s[:], in_=rsel_d)), w=["rsel_s"])
    p.op("sp", (lambda h: h.dma_start(out=s1_s[:], in_=s1_d)), w=["s1_s"])
    p.op("sp", (lambda h: h.dma_start(out=s2_s[:], in_=s2_d)), w=["s2_s"])

    def w_at(ws, ki, co):
        return ws[:, (ki * 8 + co) * 128:(ki * 8 + co) * 128 + 128]

    x_bf = asb("x_bf", [128, 8, T], BF16)
    xn_s = asb("xn_s", [128, 2, 8, T], BF16)
    q_s = asb("q_s", [128, 8, T], BF16)
    k_s = asb("k_s", [128, 8, T], BF16)
    v_s = asb("v_s", [128, 8, 64 + T], BF16)
    pre_s = asb("pre_s", [128, 2, 3 + T], BF16)
    tails = asb("tails", [128, 3, 8, 3], BF16)
    cacc_s = asb("cacc_s", [128, 2, T], BF16)
    beta_s = asb("beta_s", [4, T], F32)
    beta_b = asb("beta_b", [4, T], BF16)
    m_row = asb("m_row", [1, T], F32)
    v_row = asb("v_row", [1, T], F32)
    mb_row = asb("mb_row", [1, T], BF16)
    rb_row = asb("rb_row", [1, T], BF16)
    sq_s = asb("sq_s", [128, 16, 128], BF16)
    lr_row = asb("lr_row", [1, T], F32)
    lr64_s = asb("lr64_s", [64, T], F32)
    lrb64_s = asb("lrb64_s", [64, T], BF16)
    rqbc_s = asb("rqbc_s", [128, T], BF16)
    rkbc_s = asb("rkbc_s", [128, T], BF16)
    bbc_s = asb("bbc_s", [128, T], BF16)
    ql_s = asb("ql_s", [128, 2, 8, 128], BF16)
    kl_s = asb("kl_s", [128, 8, 128], BF16)
    kb_s = asb("kb_s", [128, 8, 128], BF16)
    vb_s = asb("vb_s", [128, 8, 128], BF16)
    ktok_s = asb("ktok_s", [128, 2, 4, 768], BF16)
    Bp_s = asb("Bp_s", [128, 4, 128], BF16)
    Bpt_s = asb("Bpt_s", [128, 4, 128], BF16)
    Bp2_s = asb("Bp2_s", [128, 4, 128], BF16)
    Bpt2_s = asb("Bpt2_s", [128, 4, 128], BF16)
    Pt_s = asb("Pt_s", [128, 2, 4, 128], BF16)
    at_s = asb("at_s", [128, 2, 4, 128], BF16)
    npt_s = asb("npt_s", [128, 4, 128], BF16)
    nwT_s = asb("nwT_s", [128, 2, 512], BF16)
    uadj_s = asb("uadj_s", [128, 2, 512], BF16)
    otok_s = asb("otok_s", [128, 2, 512], BF16)
    S_s = asb("S_s", [128, 4, 512], F32)
    Sb_s = asb("Sb_s", [128, 4, 512], BF16)
    delta_s = asb("delta_s", [128, 8, T], BF16)
    omix_s = asb("omix_s", [128, 8, T], BF16)
    mixt_s = asb("mixt_s", [128, T], BF16)
    mixtp_s = asb("mixtp_s", [128, T], BF16)
    mixq_s = asb("mixq_s", [128, T], BF16)
    pbc5_s = asb("pbc5_s", [128, 5, T], BF16)
    probs_s = asb("probs_s", [128, 80], F32)
    probsb_s = asb("probsb_s", [128, 80], BF16)
    probsf_s = asb("probsf_s", [20, T], BF16)
    gsc_s = asb("gsc_s", [128, 80], F32)
    gmx_s = asb("gmx_s", [128, 16], F32)
    gsum_s = asb("gsum_s", [128, 16], F32)
    gexp_s = asb("gexp_s", [128, 80], BF16)

    p.op("dve", (lambda h: h.memset(v_s[:, :, 0:64], 0.0)),
         w=[f"v_s{c}" for c in range(NCK)])
    p.op("dve", (lambda h: h.memset(tails[:], 0.0)), w=["tails_all"])
    p.op("dve", (lambda h: h.memset(S_s[:], 0.0)),
         w=[f"S{hh}" for hh in range(H)])

    CQS = {0: cq_s, 1: ck_s, 2: cv_s}
    WS = {0: wqr, 1: wkr, 2: wvr}
    WD = {0: wq_d, 1: wk_d, 2: wv_d}
    OUTS = {0: q_s, 1: k_s, 2: v_s}

    BGQ = []
    bgs = {"i": 0, "banks": (5,), "j": 0}

    def bgbank():
        b = bgs["banks"][bgs["j"] % len(bgs["banks"])]
        bgs["j"] += 1
        return b

    def pump(reserve=0, n=1, warm=0):
        for _ in range(n):
            if len(BGQ) - bgs["i"] > reserve:
                BGQ[bgs["i"]]()
                bgs["i"] += 1
            else:
                break
        if False and warm and len(BGQ) - bgs["i"] <= 0:
            p.op("pe", (lambda h: h.matmul(ps[5][:, 0:128], ident_s[:], ident_s[:],
                                           start=True, stop=True)),
                 r=["ident_s"], w=["ps5"])

    def bg_drain(upto):
        while bgs["i"] < min(upto, len(BGQ)):
            BGQ[bgs["i"]]()
            bgs["i"] += 1

    def mk_c0(co):
        ws = co % 3
        def c():
            p.op("sp", (lambda h: h.dma_start(out=w1r[:, ws, :, :], in_=w1_d[co])),
                 w=[f"w1r{ws}"])
            p.op("pool", (lambda h: h.dma_start(out=w2r[:, ws, :, :], in_=w2_d[co])),
                 w=[f"w2r{ws}"])
        return c

    def mk_u1(t, co):
        ws, tb, pr = co % 3, co % 2, t % 2
        def c():
            b = bgbank()
            def mm(h):
                for ki in range(8):
                    i = h.matmul(ps[b][:], w1r[:, ws, ki, :], xn_s[:, pr, ki, :],
                                 start=(ki == 0), stop=(ki == 7))
                return i
            p.op("pe", mm, r=[f"w1r{ws}"] + [f"xn{pr}{ki}" for ki in range(8)], w=[PSB[b]])
            p.op("act", (lambda h: h.activation(t1r[:, tb, :], ps[b][:], AF.Silu,
                                                scale=s1_s[:, co:co + 1])),
                 r=[PSB[b], "s1_s"], w=[f"t1r{tb}"])
        return c

    def mk_u2(t, co):
        ws, tb, hs, pr = co % 3, co % 2, co % 2, t % 2
        def c():
            b = bgbank()
            def mm(h):
                for ki in range(8):
                    i = h.matmul(ps[b][:], w2r[:, ws, ki, :], xn_s[:, pr, ki, :],
                                 start=(ki == 0), stop=(ki == 7))
                return i
            p.op("pe", mm, r=[f"w2r{ws}"] + [f"xn{pr}{ki}" for ki in range(8)], w=[PSB[b]])
            p.op("dve", (lambda h: h.scalar_tensor_tensor(
                h0st[:, hs, :], ps[b][:], s2_s[:, co:co + 1], t1r[:, tb, :],
                ALU.mult, ALU.mult)), r=[PSB[b], "s2_s", f"t1r{tb}"], w=[f"h0st{hs}"])
            p.op("sp", (lambda h: h.dma_start(out=h0_h[:, co, t * T:(t + 1) * T],
                                              in_=h0st[:, hs, :])),
                 r=[f"h0st{hs}"], w=["h0_h", f"h0_h{t}"])
        return c

    def append_f1(t):
        if not P_F1:
            return
        BGQ.append(mk_c0(0))
        BGQ.append(mk_c0(1))
        for co in range(32):
            BGQ.append(mk_u1(t, co))
            BGQ.append(mk_u2(t, co))
            if co + 2 < 32:
                BGQ.append(mk_c0(co + 2))

    drain_target = 0
    for t in range(P_NT if P_ATTN else 0):
        t0 = t * T
        pr = t % 2
        xk = "x0"
        def xload(tt_):
            p.op("pool", (lambda h, tt_=tt_: h.dma_start(
                out=x_bf[:],
                in_=x_in.rearrange("(ck p) t -> p ck t", p=128)[:, :, tt_ * T:tt_ * T + T])),
                w=["x0"])
        if t == 0:
            xload(0)

        def ln_sum(h):
            for ck in range(NCK):
                i = h.matmul(ps[4][0:1, :], onesc_s[:], x_bf[:, ck, :],
                             start=(ck == 0), stop=(ck == NCK - 1))
            return i
        p.op("pe", ln_sum, r=[xk, "onesc_s"], w=["ps4"])
        for ck in range(NCK):
            sp_ = ck % 2
            p.op("dve", (lambda h, ck=ck, sp_=sp_: h.tensor_tensor(
                sq_s[:, 4 * sp_:4 * sp_ + 4, :].rearrange("p a b -> p (a b)"),
                x_bf[:, ck, :], x_bf[:, ck, :], ALU.mult)),
                r=[xk], w=[f"xsq{sp_}"])
            p.op("pe", (lambda h, ck=ck, sp_=sp_: h.matmul(
                ps[4][32:33, :], onesc_s[:],
                sq_s[:, 4 * sp_:4 * sp_ + 4, :].rearrange("p a b -> p (a b)"),
                start=(ck == 0), stop=(ck == NCK - 1), tile_position=(0, 32))),
                r=[f"xsq{sp_}", "onesc_s"], w=["ps4"])

        p.op("dve", (lambda h: h.tensor_scalar_mul(m_row[:], ps[4][0:1, :], 1.0 / D)),
             r=["ps4"], w=["m_row"], small=True)
        p.op("dve", (lambda h: h.tensor_scalar_mul(v_row[:], ps[4][32:33, :], 1.0 / D)),
             r=["ps4"], w=["v_row"], small=True)
        p.op("dve", (lambda h: h.tensor_tensor(lr_row[:], m_row[:], m_row[:], ALU.mult)),
             r=["m_row"], w=["lr_row"], small=True)
        p.op("dve", (lambda h: h.scalar_tensor_tensor(
            v_row[:], v_row[:], 1e-5, lr_row[:], ALU.add, ALU.subtract)),
            r=["lr_row"], w=["v_row"], small=True)
        p.op("act", (lambda h: h.activation(v_row[:], v_row[:], AF.Ln)), w=["v_row"])
        p.op("act", (lambda h: h.activation(v_row[:], v_row[:], AF.Exp, scale=-0.5)), w=["v_row"])
        p.op("dve", (lambda h: h.tensor_copy(mb_row[:], m_row[:])), r=["m_row"], w=["mb_row"], small=True)
        p.op("dve", (lambda h: h.tensor_copy(rb_row[:], v_row[:])), r=["v_row"], w=["rb_row"], small=True)
        p.op("pe", (lambda h: h.matmul(ps[2][:], onesr_s[:], mb_row[:], start=True, stop=True)),
             r=["mb_row", "onesr_s"], w=["ps2"])
        p.op("pe", (lambda h: h.matmul(ps[3][:], onesr_s[:], rb_row[:], start=True, stop=True)),
             r=["rb_row", "onesr_s"], w=["ps3"])
        for ck in range(NCK):
            p.op("dve", (lambda h, pr=pr, ck=ck: h.tensor_sub(
                xn_s[:, pr, ck, :], x_bf[:, ck, :], ps[2][:])), r=[xk, "ps2"], w=[f"xn{pr}{ck}"])
            p.op("dve", (lambda h, pr=pr, ck=ck: h.tensor_mul(
                xn_s[:, pr, ck, :], xn_s[:, pr, ck, :], ps[3][:])), r=["ps3"], w=[f"xn{pr}{ck}"])
        if t + 1 < (P_NT if P_ATTN else 0):
            xload(t + 1)
        drain_target = len(BGQ)
        append_f1(t)

        for tn in range(3 if P_PROJ else 0):
            ws = WS[tn]
            outb = OUTS[tn]
            cqs = CQS[tn]
            ooff = 64 if tn == 2 else 0

            def wload(tn_, co_):
                sl = co_ % 4
                q_ = "sp" if tn_ != 1 else "pool"
                p.op(q_, (lambda h, tn_=tn_, co_=co_, sl=sl: h.dma_start(
                    out=WS[tn_][:, sl, :, :], in_=WD[tn_][:, co_, :, :])),
                    w=[f"{WS[tn_].name}{sl}"])
            for co_ in range(4):
                wload(tn, co_)
            for co in range(NCK):
                bank = co % 4
                pk = PSB[bank]
                pp = (tn * NCK + co) % 2
                sl = co % 4

                def proj(h, ws=ws, sl=sl, bank=bank, pr=pr):
                    for ki in range(NCK):
                        i = h.matmul(ps[bank][:], ws[:, sl, ki, :], xn_s[:, pr, ki, :],
                                     start=(ki == 0), stop=(ki == NCK - 1))
                    return i
                p.op("pe", proj, r=[f"{ws.name}{sl}"] + [f"xn{pr}{ki}" for ki in range(NCK)],
                     w=[pk])
                if co + 4 < NCK:
                    wload(tn, co + 4)
                p.op("act", (lambda h, bank=bank, pp=pp: h.activation(
                    pre_s[:, pp, 3:3 + T], ps[bank][:], AF.Copy)), r=[pk], w=[f"pre{pp}"])
                p.op("dve", (lambda h, tn=tn, co=co, pp=pp: h.tensor_copy(
                    pre_s[:, pp, 0:3], tails[:, tn, co, :])), r=["tails_all"],
                    w=[f"pre{pp}"], small=True)
                p.op("dve", (lambda h, tn=tn, co=co, pp=pp: h.tensor_copy(
                    tails[:, tn, co, :], pre_s[:, pp, T:T + 3])), r=[f"pre{pp}"],
                    w=["tails_all"], small=True)
                p.op("dve", (lambda h, cqs=cqs, co=co, pp=pp: h.tensor_scalar_mul(
                    cacc_s[:, pp, :], pre_s[:, pp, 0:T], cqs[:, co * 4:co * 4 + 1])),
                    r=[f"pre{pp}", cqs.name], w=[f"cacc{pp}"])
                for j in range(1, 4):
                    p.op("dve", (lambda h, cqs=cqs, co=co, j=j, pp=pp: h.scalar_tensor_tensor(
                        cacc_s[:, pp, :], pre_s[:, pp, j:j + T], cqs[:, co * 4 + j:co * 4 + j + 1],
                        cacc_s[:, pp, :], ALU.mult, ALU.add)), r=[f"pre{pp}", cqs.name],
                        w=[f"cacc{pp}"])
                p.op("act", (lambda h, outb=outb, co=co, ooff=ooff, pp=pp: h.activation(
                    outb[:, co, ooff:ooff + T], cacc_s[:, pp, :], AF.Silu)),
                    r=[f"cacc{pp}"], w=[f"{outb.name}{co}"])
                pump(reserve=96)

        if P_PROJ:
            def beta_mm(h, pr=pr):
                for ki in range(NCK):
                    i = h.matmul(ps[4][0:4, :], wb_s[:, ki * 4:ki * 4 + 4], xn_s[:, pr, ki, :],
                                 start=(ki == 0), stop=(ki == NCK - 1))
                return i
            p.op("pe", beta_mm, r=["wb_s"] + [f"xn{pr}{ki}" for ki in range(NCK)], w=["ps4"])
            p.op("act", (lambda h: h.activation(beta_s[:], ps[4][0:4, :], AF.Sigmoid)),
                 r=["ps4"], w=["beta_s"])
            p.op("dve", (lambda h: h.tensor_copy(beta_b[:], beta_s[:])), r=["beta_s"], w=["beta_b"])
            pump(n=6, warm=1)

        def d1_chunk(c):
            cs0 = c * CC
            cp = c % 2
            for hh in range(H):
                for ci, ck in enumerate((2 * hh, 2 * hh + 1)):
                    for qi, src in enumerate((q_s, k_s)):
                        idx = hh * 4 + ci * 2 + qi
                        p.op("dve", (lambda h, src=src, ck=ck, cs0=cs0, idx=idx:
                                     h.tensor_tensor(
                            sq_s[:, idx, :], src[:, ck, cs0:cs0 + CC],
                            src[:, ck, cs0:cs0 + CC],
                            ALU.mult)), r=[f"{src.name}{ck}"], w=["xsq0"], small=True)

            def l2mm(h):
                for hh in range(H):
                    for qi in range(2):
                        for ci in range(2):
                            idx = hh * 4 + ci * 2 + qi
                            row = 0 if qi == 0 else 32
                            i = h.matmul(ps[4][row:row + 1, hh * 128:(hh + 1) * 128],
                                         onesc_s[:], sq_s[:, idx, :],
                                         start=(ci == 0), stop=(ci == 1),
                                         tile_position=(0, row))
                return i
            p.op("pe", l2mm, r=["xsq0", "onesc_s"], w=["ps4"])
            p.op("dve", (lambda h: h.tensor_scalar_add(lr64_s[:], ps[4][0:64, :], 1e-6)),
                 r=["ps4"], w=["lr64"], small=True)
            p.op("act", (lambda h: h.activation(lr64_s[:], lr64_s[:], AF.Ln)), w=["lr64"])
            p.op("act", (lambda h: h.activation(lr64_s[:], lr64_s[:], AF.Exp, scale=-0.5)),
                 w=["lr64"])
            p.op("dve", (lambda h: h.tensor_copy(lrb64_s[:], lr64_s[:])), r=["lr64"],
                 w=["lrb64"], small=True)
            pump(n=3)

            def bc_q(h):
                for hh in range(H):
                    i = h.matmul(ps[2][:, hh * 128:(hh + 1) * 128], onesr_s[:],
                                 lrb64_s[0:1, hh * 128:(hh + 1) * 128], start=True, stop=True)
                return i
            p.op("pe", bc_q, r=["lrb64", "onesr_s"], w=["ps2"])
            p.op("act", (lambda h: h.activation(rqbc_s[:], ps[2][:], AF.Copy)), r=["ps2"], w=["rqbc"])
            pump(warm=1)

            def bc_b(h, cs0=cs0):
                for hh in range(H):
                    i = h.matmul(ps[2][:, hh * 128:(hh + 1) * 128],
                                 e4_s[:, hh * 128:(hh + 1) * 128],
                                 beta_b[:, cs0:cs0 + CC], start=True, stop=True)
                return i
            p.op("pe", bc_b, r=["beta_b", "e4_s"], w=["ps2"])
            p.op("act", (lambda h: h.activation(bbc_s[:], ps[2][:], AF.Copy)), r=["ps2"], w=["bbc"])
            pump(warm=1)

            def bc_k(h):
                for hh in range(H):
                    i = h.matmul(ps[3][:, hh * 128:(hh + 1) * 128], ones33_s[32:33, :],
                                 lrb64_s[32:33, hh * 128:(hh + 1) * 128], start=True, stop=True)
                return i
            p.op("pe", bc_k, r=["lrb64", "ones33_s"], w=["ps3"])
            p.op("act", (lambda h: h.activation(rkbc_s[:], ps[3][:], AF.Copy)), r=["ps3"], w=["rkbc"])
            pump(warm=1)

            for hh in range(H):
                hc = slice(hh * 128, (hh + 1) * 128)
                for ck in (2 * hh, 2 * hh + 1):
                    p.op("dve", (lambda h, ck=ck, hc=hc, cs0=cs0, cp=cp: h.tensor_tensor(
                        ql_s[:, cp, ck, :], q_s[:, ck, cs0:cs0 + CC], rqbc_s[:, hc], ALU.mult)),
                        r=[f"q_s{ck}", "rqbc"], w=[f"ql{cp}{ck}"], small=True)
                    p.op("dve", (lambda h, ck=ck, hc=hc, cs0=cs0: h.tensor_tensor(
                        kl_s[:, ck, :], k_s[:, ck, cs0:cs0 + CC], rkbc_s[:, hc], ALU.mult)),
                        r=[f"k_s{ck}", "rkbc"], w=[f"kl{ck}"], small=True)
                    p.op("dve", (lambda h, ck=ck, hc=hc: h.tensor_tensor(
                        kb_s[:, ck, :], kl_s[:, ck, :], bbc_s[:, hc], ALU.mult)),
                        r=[f"kl{ck}", "bbc"], w=[f"kb{ck}"], small=True)
                    p.op("dve", (lambda h, ck=ck, hc=hc, cs0=cs0: h.tensor_tensor(
                        vb_s[:, ck, :], v_s[:, ck, 64 + cs0:64 + cs0 + CC], bbc_s[:, hc], ALU.mult)),
                        r=[f"v_s{ck}", "bbc"], w=[f"vb{ck}"], small=True)

            for hh in range(H):
                tp = hh % 2

                def trs(h, hh=hh, tp=tp):
                    for si, src in enumerate((kl_s, kb_s, vb_s)):
                        for ci, ck in enumerate((2 * hh, 2 * hh + 1)):
                            i = h.transpose(ps_tr[:, tp, si * 256 + ci * 128: si * 256 + ci * 128 + 128],
                                            src[:, ck, :], ident_s[:])
                    return i
                p.op("pe", trs, r=[f"kl{2*hh}", f"kl{2*hh+1}", f"kb{2*hh}", f"kb{2*hh+1}",
                                   f"vb{2*hh}", f"vb{2*hh+1}", "ident_s"], w=[PTR[tp]])
                p.op("dve", (lambda h, hh=hh, tp=tp, cp=cp: h.tensor_copy(
                    ktok_s[:, cp, hh, :], ps_tr[:, tp, 0:768])), r=[PTR[tp]], w=[f"ktok{cp}{hh}"])
                pump(warm=1)

            Bx_s, Bxt_s = (Bp_s, Bpt_s) if c % 2 == 0 else (Bp2_s, Bpt2_s)
            BK, BKT = ("Bp", "Bpt") if c % 2 == 0 else ("Bq", "Bqt")
            for hh in range(H):
                def bmm(h, hh=hh, cp=cp):
                    i = h.matmul(ps[0][:, 0:128], kb_s[:, 2 * hh, :], kl_s[:, 2 * hh, :],
                                 start=True, stop=False)
                    i = h.matmul(ps[0][:, 0:128], kb_s[:, 2 * hh + 1, :], kl_s[:, 2 * hh + 1, :],
                                 start=False, stop=True)
                    i = h.matmul(ps[0][:, 128:256], kl_s[:, 2 * hh, :], kb_s[:, 2 * hh, :],
                                 start=True, stop=False)
                    i = h.matmul(ps[0][:, 128:256], kl_s[:, 2 * hh + 1, :], kb_s[:, 2 * hh + 1, :],
                                 start=False, stop=True)
                    i = h.matmul(ps[0][:, 256:384], kl_s[:, 2 * hh, :], ql_s[:, cp, 2 * hh, :],
                                 start=True, stop=False)
                    i = h.matmul(ps[0][:, 256:384], kl_s[:, 2 * hh + 1, :], ql_s[:, cp, 2 * hh + 1, :],
                                 start=False, stop=True)
                    return i
                p.op("pe", bmm, r=[f"kb{2*hh}", f"kb{2*hh+1}", f"kl{2*hh}", f"kl{2*hh+1}",
                                   f"ql{cp}{2*hh}", f"ql{cp}{2*hh+1}"], w=["ps0"])
                p.op("dve", (lambda h, hh=hh, Bx_s=Bx_s: h.tensor_tensor(
                    Bx_s[:, hh, :], ps[0][:, 0:128], mstril_s[:], ALU.mult)),
                    r=["ps0", "mstril_s"], w=[f"{BK}{hh}"], small=True)
                p.op("dve", (lambda h, hh=hh, Bxt_s=Bxt_s: h.tensor_tensor(
                    Bxt_s[:, hh, :], ps[0][:, 128:256], mstriu_s[:], ALU.mult)),
                    r=["ps0", "mstriu_s"], w=[f"{BKT}{hh}"], small=True)
                p.op("dve", (lambda h, hh=hh, cp=cp: h.tensor_tensor(
                    at_s[:, cp, hh, :], ps[0][:, 256:384], mtriu_s[:], ALU.mult)),
                    r=["ps0", "mtriu_s"], w=[f"at{cp}{hh}"], small=True)
                p.op("dve", (lambda h, hh=hh, cp=cp, Bxt_s=Bxt_s: h.tensor_sub(
                    Pt_s[:, cp, hh, :], ident_s[:], Bxt_s[:, hh, :])),
                    r=[f"{BKT}{hh}", "ident_s"], w=[f"Pt{cp}{hh}"], small=True)
                pump(warm=1)

        def pairwaves(a, b):
            CFG = ((a, 4, 0, Bp_s, Bpt_s, "Bp", "Bpt"),
                   (b, 5, 1, Bp2_s, Bpt2_s, "Bq", "Bqt"))
            for it in range(1, 7):
                last = (it == 6)
                for c, bank, pslot, Bx, Bxt, BK, BKT in CFG:
                    BALL = [f"{BK}{hh}" for hh in range(H)]
                    BTALL = [f"{BKT}{hh}" for hh in range(H)]

                    def tsq(h, bank=bank, Bx=Bx, Bxt=Bxt):
                        for hh in range(H):
                            i = h.matmul(ps[bank][:, hh * 128:(hh + 1) * 128],
                                         Bxt[:, hh, :], Bx[:, hh, :], start=True, stop=True)
                        return i
                    p.op("pe", tsq, r=BALL + BTALL, w=[PSB[bank]])
                    p.op("act", (lambda h, bank=bank, Bx=Bx: h.activation(
                        Bx[:].rearrange("p a b -> p (a b)"), ps[bank][:], AF.Copy)),
                        r=[PSB[bank]], w=BALL)
                if not last:
                    for c, bank, pslot, Bx, Bxt, BK, BKT in CFG:
                        BALL = [f"{BK}{hh}" for hh in range(H)]
                        BTALL = [f"{BKT}{hh}" for hh in range(H)]

                        def ttr(h, pslot=pslot, Bx=Bx):
                            for hh in range(H):
                                i = h.transpose(ps_tr[:, pslot, hh * 128:(hh + 1) * 128],
                                                Bx[:, hh, :], ident_s[:])
                            return i
                        p.op("pe", ttr, r=BALL + ["ident_s"], w=[PTR[pslot]])
                        p.op("dve", (lambda h, pslot=pslot, Bxt=Bxt: h.tensor_copy(
                            Bxt[:].rearrange("p a b -> p (a b)"), ps_tr[:, pslot, 0:512])),
                            r=[PTR[pslot]], w=BTALL, small=True)
                for c, bank, pslot, Bx, Bxt, BK, BKT in CFG:
                    BALL = [f"{BK}{hh}" for hh in range(H)]
                    cp = c % 2
                    PTALL = [f"Pt{cp}{hh}" for hh in range(H)]

                    def tch(h, bank=bank, Bx=Bx, cp=cp):
                        for hh in range(H):
                            h.matmul(ps[bank][:, hh * 128:(hh + 1) * 128], Bx[:, hh, :],
                                     Pt_s[:, cp, hh, :], start=True, stop=False)
                            i = h.matmul(ps[bank][:, hh * 128:(hh + 1) * 128], ident_s[:],
                                         Pt_s[:, cp, hh, :], start=False, stop=True)
                        return i
                    p.op("pe", tch, r=BALL + PTALL + ["ident_s"], w=[PSB[bank]])
                    p.op("act", (lambda h, bank=bank, cp=cp: h.activation(
                        Pt_s[:, cp, :, :].rearrange("p a b -> p (a b)"), ps[bank][:], AF.Copy)),
                        r=[PSB[bank]], w=PTALL)

        def nwsl(hh, lo):
            return nwT_s[:, hh % 2, (hh // 2) * 256 + lo:(hh // 2) * 256 + lo + 128]

        def uasl(hh):
            return uadj_s[:, hh % 2, (hh // 2) * 256:(hh // 2) * 256 + 256]

        def post_waves(c):
            cs0 = c * CC
            cp = c % 2
            PTALL = [f"Pt{cp}{hh}" for hh in range(H)]
            waves = []

            def pw0():
                for hh in range(H):
                    p.op("dve", (lambda h, hh=hh, cp=cp: h.tensor_scalar_mul(
                        npt_s[:, hh, :], Pt_s[:, cp, hh, :], -1.0)), r=[f"Pt{cp}{hh}"],
                        w=[f"npt{hh}"], small=True)
                    p.op("dve", (lambda h, hh=hh: h.tensor_copy(Sb_s[:, hh, :], S_s[:, hh, :])),
                         r=[f"S{hh}"], w=[f"Sb{hh}"])
            waves.append(pw0)

            def pw1():
                def nwmm(h, cp=cp):
                    for hh in range(H):
                        nb = hh % 2
                        cb = (hh // 2) * 256
                        h.matmul(ps[nb][:, cb:cb + 128], ktok_s[:, cp, hh, 256:384],
                                 npt_s[:, hh, :], start=True, stop=True)
                        i = h.matmul(ps[nb][:, cb + 128:cb + 256], ktok_s[:, cp, hh, 384:512],
                                     npt_s[:, hh, :], start=True, stop=True)
                    return i
                p.op("pe", nwmm, r=[f"ktok{cp}{hh}" for hh in range(H)] +
                     [f"npt{hh}" for hh in range(H)], w=["ps0", "ps1"])
                p.op("act", (lambda h: h.activation(nwT_s[:, 0, :], ps[0][:], AF.Copy)),
                     r=["ps0"], w=["nwT0"])
                p.op("act", (lambda h: h.activation(nwT_s[:, 1, :], ps[1][:], AF.Copy)),
                     r=["ps1"], w=["nwT1"])
                pump(n=2, warm=1)
            waves.append(pw1)

            def pw2():
                def uadj(h, cp=cp):
                    for hh in range(H):
                        ub = 2 + hh % 2
                        cb = (hh // 2) * 256
                        h.matmul(ps[ub][:, cb:cb + 256], Pt_s[:, cp, hh, :],
                                 ktok_s[:, cp, hh, 512:768], start=True, stop=False)
                        h.matmul(ps[ub][:, cb:cb + 256], nwsl(hh, 0), Sb_s[:, hh, 0:256],
                                 start=False, stop=False)
                        i = h.matmul(ps[ub][:, cb:cb + 256], nwsl(hh, 128), Sb_s[:, hh, 256:512],
                                     start=False, stop=True)
                    return i
                p.op("pe", uadj, r=PTALL + [f"ktok{cp}{hh}" for hh in range(H)] + ["nwT0", "nwT1"] +
                     [f"Sb{hh}" for hh in range(H)], w=["ps2", "ps3"])
                p.op("act", (lambda h: h.activation(uadj_s[:, 0, :], ps[2][:], AF.Copy)),
                     r=["ps2"], w=["uadj0"])
                p.op("act", (lambda h: h.activation(uadj_s[:, 1, :], ps[3][:], AF.Copy)),
                     r=["ps3"], w=["uadj1"])
                pump(n=2, warm=1)
            waves.append(pw2)

            def pw3():
                def omm(h, cp=cp):
                    for hh in range(H):
                        ob = hh % 2
                        cb = (hh // 2) * 256
                        h.matmul(ps[ob][:, cb:cb + 256], ql_s[:, cp, 2 * hh, :], Sb_s[:, hh, 0:256],
                                 start=True, stop=False)
                        h.matmul(ps[ob][:, cb:cb + 256], ql_s[:, cp, 2 * hh + 1, :],
                                 Sb_s[:, hh, 256:512], start=False, stop=False)
                        i = h.matmul(ps[ob][:, cb:cb + 256], at_s[:, cp, hh, :], uasl(hh),
                                     start=False, stop=True)
                    return i
                p.op("pe", omm, r=[f"ql{cp}{ck}" for ck in range(NCK)] + ["uadj0", "uadj1"] +
                     [f"Sb{hh}" for hh in range(H)] + [f"at{cp}{hh}" for hh in range(H)],
                     w=["ps0", "ps1"])
                p.op("act", (lambda h: h.activation(otok_s[:, 0, :], ps[0][:], AF.Copy)),
                     r=["ps0"], w=["otok0"])
                p.op("act", (lambda h: h.activation(otok_s[:, 1, :], ps[1][:], AF.Copy)),
                     r=["ps1"], w=["otok1"])
                pump(n=2, warm=1)
            waves.append(pw3)

            def pw4():
                for wv in range(2):
                    def smm(h, wv=wv, cp=cp):
                        for hh in (2 * wv, 2 * wv + 1):
                            sb_ = 2 + hh % 2
                            h.matmul(ps[sb_][:, 0:256], ktok_s[:, cp, hh, 0:128], uasl(hh),
                                     start=True, stop=True)
                            i = h.matmul(ps[sb_][:, 256:512], ktok_s[:, cp, hh, 128:256], uasl(hh),
                                         start=True, stop=True)
                        return i
                    p.op("pe", smm, r=[f"ktok{cp}{2*wv}", f"ktok{cp}{2*wv+1}", "uadj0", "uadj1"],
                         w=["ps2", "ps3"])
                    for hh in (2 * wv, 2 * wv + 1):
                        p.op("dve", (lambda h, hh=hh: h.tensor_add(
                            S_s[:, hh, :], S_s[:, hh, :], ps[2 + hh % 2][:])),
                            r=[PSB[2 + hh % 2]], w=[f"S{hh}"])
                    pump(n=2, warm=1)
            waves.append(pw4)

            def pw5():
                for wv in range(2):
                    def otrw(h, wv=wv):
                        for hh in (2 * wv, 2 * wv + 1):
                            tp = hh % 2
                            cb = (hh // 2) * 256
                            h.transpose(ps_tr[:, tp, 768:896], otok_s[:, tp, cb:cb + 128],
                                        ident_s[:])
                            i = h.transpose(ps_tr[:, tp, 896:1024],
                                            otok_s[:, tp, cb + 128:cb + 256], ident_s[:])
                        return i
                    p.op("pe", otrw, r=["otok0", "otok1", "ident_s"], w=[PTR[0], PTR[1]])
                    for hh in (2 * wv, 2 * wv + 1):
                        tp = hh % 2
                        p.op("act", (lambda h, hh=hh, tp=tp, cs0=cs0: h.activation(
                            delta_s[:, 2 * hh, cs0:cs0 + CC], ps_tr[:, tp, 768:896], AF.Copy)),
                            r=[PTR[tp]], w=[f"delta{2*hh}"])
                        p.op("act", (lambda h, hh=hh, tp=tp, cs0=cs0: h.activation(
                            delta_s[:, 2 * hh + 1, cs0:cs0 + CC], ps_tr[:, tp, 896:1024], AF.Copy)),
                            r=[PTR[tp]], w=[f"delta{2*hh+1}"])
                    pump(n=2, warm=1)
            waves.append(pw5)
            return waves

        NCH_ = NCH if P_DELTA else 0
        if NCH_:
            for pa in range(0, NCH_, 2):
                if pa >= 2:
                    for wfn in post_waves(pa - 2):
                        wfn()
                    for wfn in post_waves(pa - 1):
                        wfn()
                d1_chunk(pa)
                d1_chunk(pa + 1)
                pairwaves(pa, pa + 1)
            for wfn in post_waves(NCH_ - 2):
                wfn()
            for wfn in post_waves(NCH_ - 1):
                wfn()

        if debug and t == 0:
            dump("q0", (lambda: q_s[:]), [128, 8, T], BF16, [f"q_s{c}" for c in range(8)])
            dump("k0", (lambda: k_s[:]), [128, 8, T], BF16, [f"k_s{c}" for c in range(8)])
            dump("v0", (lambda: v_s[:]), [128, 8, 64 + T], BF16, [f"v_s{c}" for c in range(8)])
            dump("beta0", (lambda: beta_s[:]), [4, T], F32, ["beta_s"])
            dump("delta0", (lambda: delta_s[:]), [128, 8, T], BF16,
                 [f"delta{c}" for c in range(8)])
            dump("xn0", (lambda: xn_s[:]), [128, 8, T], BF16, [f"xn{c}" for c in range(8)])

        if P_GATES:
            def gmm(h, pr=pr):
                for tt in range(4):
                    for ki in range(NCK):
                        i = h.matmul(ps[4][:, tt * 20:tt * 20 + 20],
                                     xn_s[:, pr, ki, tt * 128:(tt + 1) * 128],
                                     gw_s[:, ki * 20:(ki + 1) * 20],
                                     start=(ki == 0), stop=False)
                    i = h.matmul(ps[4][:, tt * 20:tt * 20 + 20],
                                 onesr_s[:], gb_s[:], start=False, stop=True)
                return i
            p.op("pe", gmm, r=[f"xn{pr}{ki}" for ki in range(NCK)] + ["gw_s", "gb_s", "onesr_s"], w=["ps4"])
            p.op("dve", (lambda h: h.tensor_tensor(
                gsc_s[:], ps[4][:, 0:80], itbc_s[:], ALU.mult)),
                r=["ps4", "itbc_s"], w=["gsc"], small=True)
            if debug and t == 0:
                dump("glog0", (lambda: gsc_s[:]), [128, 80], F32, ["gsc"])
                dump("itbc", (lambda: itbc_s[:]), [128, 80], BF16, ["itbc_s"])
                dump("eabc", (lambda: eabc_s[:]), [128, 80], BF16, ["eabc_s"])
                dump("ebbc", (lambda: ebbc_s[:]), [128, 80], BF16, ["ebbc_s"])
            p.op("dve", (lambda h: h.tensor_reduce(
                gmx_s[:], gsc_s[:].rearrange("p (a b) -> p a b", b=5), AX.X, ALU.max)),
                r=["gsc"], w=["gmx"], small=True)
            if debug and t == 0:
                dump("gmx0", (lambda: gmx_s[:]), [128, 16], F32, ["gmx"])
            for j in range(5):
                p.op("dve", (lambda h, j=j: h.tensor_copy(
                    gexp_s[:].rearrange("p (a b) -> p a b", b=5)[:, :, j], gmx_s[:])),
                    r=["gmx"], w=["gexp"], small=True)
            p.op("dve", (lambda h: h.tensor_sub(gsc_s[:], gsc_s[:], gexp_s[:])),
                 r=["gexp"], w=["gsc"], small=True)
            if debug and t == 0:
                dump("gsub0", (lambda: gsc_s[:]), [128, 80], F32, ["gsc"])
            pump()
            p.op("act", (lambda h: h.activation(gsc_s[:], gsc_s[:], AF.Exp)), w=["gsc"])
            if debug and t == 0:
                dump("gexpo0", (lambda: gsc_s[:]), [128, 80], F32, ["gsc"])
            p.op("dve", (lambda h: h.tensor_reduce(
                gsum_s[:], gsc_s[:].rearrange("p (a b) -> p a b", b=5), AX.X, ALU.add)),
                r=["gsc"], w=["gsum"], small=True)
            p.op("dve", (lambda h: h.reciprocal(gsum_s[:], gsum_s[:])), w=["gsum"], small=True)
            if debug and t == 0:
                dump("gsum0", (lambda: gsum_s[:]), [128, 16], F32, ["gsum"])
            for j in range(5):
                p.op("dve", (lambda h, j=j: h.tensor_copy(
                    gexp_s[:].rearrange("p (a b) -> p a b", b=5)[:, :, j], gsum_s[:])),
                    r=["gsum"], w=["gexp"], small=True)
            p.op("dve", (lambda h: h.tensor_mul(probs_s[:], gsc_s[:], gexp_s[:])),
                 r=["gsc", "gexp"], w=["probs"], small=True)
            if debug and t == 0:
                dump("gexp0", (lambda: gexp_s[:]), [128, 80], BF16, ["gexp"])
            p.op("dve", (lambda h: h.tensor_mul(probs_s[:], probs_s[:], eabc_s[:])),
                 r=["eabc_s"], w=["probs"], small=True)
            p.op("dve", (lambda h: h.tensor_add(probs_s[:], probs_s[:], ebbc_s[:])),
                 r=["ebbc_s"], w=["probs"], small=True)
            p.op("dve", (lambda h: h.tensor_copy(probsb_s[:], probs_s[:])), r=["probs"], w=["probsb"], small=True)

            def ptr_(h):
                for tt in range(4):
                    i = h.transpose(ps_tr[0:20, 0, tt * 128:tt * 128 + 128],
                                    probsb_s[:, tt * 20:tt * 20 + 20], ident_s[:])
                return i
            p.op("pe", ptr_, r=["probsb", "ident_s"], w=[PTR[0]])
            p.op("act", (lambda h: h.activation(probsf_s[:], ps_tr[0:20, 0, 0:512], AF.Copy)),
                 r=[PTR[0]], w=["probsf"])
            pump()

        FIRDEF = ((2, 0, 16), (3, 16, 4), (4, 20, 1))
        for ck in range(NCK if P_MIX else 0):
            hh = ck // 2
            if ck % 2 == 0:
                for j in range(5):
                    r_ = hh * 5 + j
                    p.op("pe", (lambda h, r_=r_: h.matmul(
                        ps[4][:], e20_s[:, r_ * 128:(r_ + 1) * 128], probsf_s[:],
                        start=True, stop=True)), r=["probsf", "e20_s"], w=["ps4"])
                    p.op("act", (lambda h, j=j: h.activation(
                        pbc5_s[:, j, :], ps[4][:], AF.Copy)), r=["ps4"], w=[f"pbc5_{j}"])
                    pump(warm=1)
            if P_FIR:
                def fload(rc_):
                    p.op("pool", (lambda h, rc_=rc_: h.dma_start(
                        out=fcf_s[:, rc_ % 4, :], in_=fcf_d[rc_])), w=[f"fcf{rc_ % 4}"])
                if ck == 0:
                    for rc_ in range(3):
                        fload(rc_)
                for sub in range(4):
                    rc = ck * 4 + sub
                    par = rc % 4
                    if rc + 3 < 32:
                        fload(rc + 3)

                    vpar = rc % 2
                    def repl(h, ck=ck, sub=sub):
                        for r in range(4):
                            h.matmul(ps[0][32 * r:32 * r + 32, :],
                                     rsel_s[:, sub * 32:sub * 32 + 32],
                                     v_s[:, ck, r:r + 512], start=True, stop=True,
                                     tile_position=(0, 32 * r))
                        for r in range(4):
                            i = h.matmul(ps[1][32 * r:32 * r + 32, 0:61],
                                         rsel_s[:, sub * 32:sub * 32 + 32],
                                         v_s[:, ck, r + 512:r + 573], start=True, stop=True,
                                         tile_position=(0, 32 * r))
                        return i
                    p.op("pe", repl, r=[f"v_s{ck}", "rsel_s"], w=["ps0", "ps1"])
                    p.op("act", (lambda h, vpar=vpar: h.activation(
                        vrep_s[:, vpar, 0:512], ps[0][:], AF.Copy)), r=["ps0"], w=[f"vrep{vpar}"])
                    p.op("act", (lambda h, vpar=vpar: h.activation(
                        vrep_s[:, vpar, 512:573], ps[1][:, 0:61], AF.Copy)), r=["ps1"], w=[f"vrep{vpar}"])
                    pump(warm=1)

                    def firmm(h, sub=sub, par=par, vpar=vpar):
                        for bank, goff, G in FIRDEF:
                            offbase = 64 - (4 * G - 1)
                            for g in range(G):
                                off = offbase + 4 * g
                                i = h.matmul(ps[bank][32 * sub:32 * sub + 32, :],
                                             fcf_s[:, par, (goff + g) * 32:(goff + g + 1) * 32],
                                             vrep_s[:, vpar, off:off + 512],
                                             start=(g == 0), stop=(g == G - 1),
                                             tile_position=(0, 32 * sub))
                        return i
                    p.op("pe", firmm, r=[f"fcf{par}", f"vrep{vpar}"],
                         w=[PSB[2], PSB[3], PSB[4]])
                    pump(warm=1)
            mdefs = [
                ("L", (lambda: ps[2][:]), PSB[2], 2),
                ("M", (lambda: ps[3][:]), PSB[3], 1),
                ("S", (lambda: ps[4][:]), PSB[4], 0),
                ("d", (lambda ck=ck: delta_s[:, ck, :]), f"delta{ck}", 3),
                ("v", (lambda ck=ck: v_s[:, ck, 64:64 + T]), f"v_s{ck}", 4),
            ]
            if P_FIR:
                p.op("pool", (lambda h, ck=ck: h.tensor_tensor(
                    mixtp_s[:], delta_s[:, ck, :], pbc5_s[:, 3, :], ALU.mult)),
                    r=[f"delta{ck}", "pbc5_3"], w=["mixtp"], dma=False)
                p.op("pool", (lambda h, ck=ck: h.tensor_tensor(
                    mixq_s[:], v_s[:, ck, 64:64 + T], pbc5_s[:, 4, :], ALU.mult)),
                    r=[f"v_s{ck}", "pbc5_4"], w=["mixq"], dma=False)
                p.op("pool", (lambda h: h.tensor_add(
                    mixtp_s[:], mixtp_s[:], mixq_s[:])), r=["mixq"], w=["mixtp"], dma=False)
                for bi, (nm, srcap, rk, j) in enumerate(mdefs[:3]):
                    if bi == 0:
                        p.op("dve", (lambda h, srcap=srcap, j=j, ck=ck: h.tensor_tensor(
                            omix_s[:, ck, :], srcap(), pbc5_s[:, j, :], ALU.mult)),
                            r=[rk, f"pbc5_{j}"], w=[f"omix{ck}"])
                    else:
                        p.op("dve", (lambda h, srcap=srcap, j=j: h.tensor_tensor(
                            mixt_s[:], srcap(), pbc5_s[:, j, :], ALU.mult)),
                            r=[rk, f"pbc5_{j}"], w=["mixt"])
                        p.op("dve", (lambda h, ck=ck: h.tensor_add(
                            omix_s[:, ck, :], omix_s[:, ck, :], mixt_s[:])),
                            r=["mixt"], w=[f"omix{ck}"])
            else:
                p.op("dve", (lambda h, ck=ck: h.tensor_tensor(
                    omix_s[:, ck, :], delta_s[:, ck, :], pbc5_s[:, 3, :], ALU.mult)),
                    r=[f"delta{ck}", "pbc5_3"], w=[f"omix{ck}"])
                p.op("pool", (lambda h, ck=ck: h.tensor_tensor(
                    mixtp_s[:], v_s[:, ck, 64:64 + T], pbc5_s[:, 4, :], ALU.mult)),
                    r=[f"v_s{ck}", "pbc5_4"], w=["mixtp"], dma=False)
            p.op("dve", (lambda h, ck=ck: h.tensor_add(
                omix_s[:, ck, :], omix_s[:, ck, :], mixtp_s[:])),
                r=["mixtp"], w=[f"omix{ck}"])
            pump(warm=1)

        for hh in range(H if P_MIX else 0):
            row = 32 * hh
            for ci, ck2 in enumerate((2 * hh, 2 * hh + 1)):
                p.op("dve", (lambda h, ck2=ck2, ci=ci: h.tensor_tensor(
                    sq_s[:, 4 * ci:4 * ci + 4, :].rearrange("p a b -> p (a b)"),
                    omix_s[:, ck2, :], omix_s[:, ck2, :], ALU.mult)),
                    r=[f"omix{ck2}"], w=[f"xsq{ci}"])
            def rmm(h, row=row):
                i = h.matmul(ps[4][row:row + 1, :], onesc_s[:],
                             sq_s[:, 0:4, :].rearrange("p a b -> p (a b)"),
                             start=True, stop=False, tile_position=(0, row))
                i = h.matmul(ps[4][row:row + 1, :], onesc_s[:],
                             sq_s[:, 4:8, :].rearrange("p a b -> p (a b)"),
                             start=False, stop=True, tile_position=(0, row))
                return i
            p.op("pe", rmm, r=["xsq0", "xsq1", "onesc_s"], w=["ps4"])
            p.op("dve", (lambda h, row=row: h.tensor_scalar(
                lr_row[:], ps[4][row:row + 1, :], 1.0 / DKV, 1e-5, ALU.mult, ALU.add)),
                r=["ps4"], w=["lr_row"], small=True)
            p.op("act", (lambda h: h.activation(lr_row[:], lr_row[:], AF.Ln)), w=["lr_row"])
            p.op("act", (lambda h: h.activation(
                lr_row[:], lr_row[:], AF.Exp, scale=-0.5)), w=["lr_row"])
            pump()
            p.op("dve", (lambda h: h.tensor_copy(
                lrb64_s[0:1, :], lr_row[:])), r=["lr_row"], w=["lrb64"], small=True)
            tp = hh % 2
            p.op("pe", (lambda h, tp=tp: h.matmul(
                ps[2 + tp][:], onesr_s[:], lrb64_s[0:1, :], start=True, stop=True)),
                r=["lrb64", "onesr_s"], w=[PSB[2 + tp]])
            for ck2 in (2 * hh, 2 * hh + 1):
                p.op("dve", (lambda h, ck2=ck2, tp=tp: h.tensor_mul(
                    omix_s[:, ck2, :], omix_s[:, ck2, :], ps[2 + tp][:])),
                    r=[PSB[2 + tp]], w=[f"omix{ck2}"])
                p.op("dve", (lambda h, ck2=ck2: h.tensor_scalar_mul(
                    delta_s[:, ck2, :], omix_s[:, ck2, :], nw_s[:, ck2:ck2 + 1])),
                    r=[f"omix{ck2}", "nw_s"], w=[f"delta{ck2}"])
                pump()
        if P_MIX:
            p.op("sp", (lambda h, t0=t0: h.dma_start(out=on_h[:, :, t0:t0 + T], in_=delta_s[:])),
                 r=[f"delta{ck2}" for ck2 in range(NCK)], w=["on_h"])

        if debug and t == 0:
            dump("probs0", (lambda: probs_s[:]), [128, 80], F32, ["probs"])
            dump("onorm0", (lambda: delta_s[:]), [128, 8, T], BF16,
                 [f"delta{c}" for c in range(8)])

        for ckk in range(NCK):
            p.op("dve", (lambda h, ckk=ckk: h.tensor_copy(
                v_s[:, ckk, 0:64], v_s[:, ckk, T:T + 64])), w=[f"v_s{ckk}"], small=True)
        bg_drain(drain_target)

    bgs["banks"] = (0, 1, 2, 5)
    bg_drain(len(BGQ))

    p.join()
    for g in reversed(ag):
        g.__exit__(None, None, None)

    fg = []

    def fsb(name, shape, dt):
        g = nc.sbuf_tensor(name, list(shape), dt)
        tt_ = g.__enter__()
        fg.append(g)
        return tt_

    def hidden_layer(src_h, dst_h, w_d_, s_d_, tag, qkeys):
        TQ = 1024
        hin = fsb(f"hin{tag}", [128, 2, 32, TQ], BF16)
        wb_ = fsb(f"whb{tag}", [128, 2, 32, 128], BF16)
        ss_ = fsb(f"ss{tag}", [128, 32], F32)
        ost = fsb(f"ost{tag}", [128, 2, TQ], BF16)
        p.op("sp", (lambda h: h.dma_start(out=ss_[:], in_=s_d_)), w=[f"ss{tag}"])

        def hload(t4):
            hp = t4 % 2
            for kq in range(4):
                p.op("sp", (lambda h, t4=t4, hp=hp, kq=kq: h.dma_start(
                    out=hin[:, hp, kq * 8:(kq + 1) * 8, :],
                    in_=src_h[:, kq * 8:(kq + 1) * 8, t4 * TQ:(t4 + 1) * TQ])),
                    r=qkeys(t4), w=[f"hin{tag}{hp}"])

        hload(0)
        for t4 in range(4):
            hp = t4 % 2
            if t4 + 1 < 4:
                hload(t4 + 1)
            for co in range(32):
                wp = co % 2
                p.op("pool", (lambda h, co=co, wp=wp: h.dma_start(
                    out=wb_[:, wp, :, :], in_=w_d_[co])), w=[f"whb{tag}{wp}"])
                for sub in range(2):
                    bb = 2 * (co % 2) + (sub % 2)
                    def f2mm(h, wp=wp, sub=sub, bb=bb, hp=hp):
                        for ki in range(32):
                            i = h.matmul(ps[bb][:], wb_[:, wp, ki, :],
                                         hin[:, hp, ki, sub * T:(sub + 1) * T],
                                         start=(ki == 0), stop=(ki == 31))
                        return i
                    p.op("pe", f2mm, r=[f"whb{tag}{wp}", f"hin{tag}{hp}"], w=[PSB[bb]])
                    p.op("act", (lambda h, co=co, wp=wp, sub=sub, bb=bb: h.activation(
                        ost[:, wp, sub * T:(sub + 1) * T], ps[bb][:], AF.Gelu,
                        scale=ss_[:, co:co + 1])), r=[PSB[bb], f"ss{tag}"], w=[f"ost{tag}{wp}"])
                p.op("sp", (lambda h, co=co, wp=wp, t4=t4: h.dma_start(
                    out=dst_h[:, co, t4 * TQ:(t4 + 1) * TQ], in_=ost[:, wp, :])),
                    r=[f"ost{tag}{wp}"], w=[f"{tag}dst", f"{tag}dst{t4}"])

    if P_F2:
        hidden_layer(h0_h, h1o_h, h1_d, sh1_d, "h1",
                     lambda t4: [f"h0_h{2 * t4}", f"h0_h{2 * t4 + 1}"])
    p.join()
    for g in reversed(fg):
        g.__exit__(None, None, None)
    fg = []
    if P_F3:
        hidden_layer(h1o_h, h2o_h, h2_d, sh2_d, "h2", lambda t4: [f"h1dst{t4}"])
    p.join()
    for g in reversed(fg):
        g.__exit__(None, None, None)
    fg = []

    h2in = fsb("h2in", [128, 2, 32, T], BF16)
    onin = fsb("onin", [128, 2, 8, T], BF16)
    xf32 = fsb("xf32", [128, 8, T], F32)
    w3s_ = fsb("w3s_", [128, 8, 32, 128], BF16)
    wos_ = fsb("wos_", [128, 8 * 8 * 128], BF16)
    s3_s = fsb("s3_s", [128, 8], F32)
    outst = fsb("outst", [128, 2, T], F32)

    p.op("sp", (lambda h: h.dma_start(out=s3_s[:], in_=s3_d)), w=["s3_s"])
    p.op("pool", (lambda h: h.dma_start(out=w3s_[:, :, :, :], in_=w3_d)), w=["w3s"])
    p.op("pool", (lambda h: h.dma_start(
        out=wos_[:], in_=wo_d.rearrange("p a b c -> p (a b c)"))), w=["wos"])

    def f4load(tt):
        fp = tt % 2
        p.op("sp", (lambda h, tt=tt, fp=fp: h.dma_start(
            out=h2in[:, fp, :, :], in_=h2o_h[:, :, tt * T:(tt + 1) * T])),
            r=[f"h2dst{tt // 2}"], w=[f"h2in{fp}"])
        p.op("pool", (lambda h, tt=tt, fp=fp: h.dma_start(
            out=onin[:, fp, :, :], in_=on_h[:, :, tt * T:(tt + 1) * T])),
            r=["on_h"], w=[f"onin{fp}"])

    def x4load(tt):
        p.op("pool", (lambda h, tt=tt: h.dma_start(
            out=xf32[:],
            in_=x_in.rearrange("(ck p) t -> p ck t", p=128)[:, :, tt * T:(tt + 1) * T])),
            w=["xf32"])

    if P_F4:
        f4load(0)
        x4load(0)
    for tt in range(NT if P_F4 else 0):
        fp = tt % 2
        if tt + 1 < NT:
            f4load(tt + 1)
        for co in range(8):
            bf_ = 2 * (co % 2)
            ba_ = bf_ + 1
            def f4mm(h, fp=fp, co=co, bf_=bf_, ba_=ba_):
                for ki in range(32):
                    i = h.matmul(ps[bf_][:], w3s_[:, co, ki, :], h2in[:, fp, ki, :],
                                 start=(ki == 0), stop=(ki == 31))
                for ki in range(8):
                    i = h.matmul(ps[ba_][:], w_at(wos_, ki, co), onin[:, fp, ki, :],
                                 start=(ki == 0), stop=(ki == 7))
                return i
            p.op("pe", f4mm, r=[f"h2in{fp}", f"onin{fp}", "w3s", "wos"], w=[PSB[bf_], PSB[ba_]])
            osl = co % 2
            p.op("dve", (lambda h, co=co, bf_=bf_, osl=osl: h.scalar_tensor_tensor(
                outst[:, osl, :], ps[bf_][:], s3_s[:, co:co + 1], xf32[:, co, :],
                ALU.mult, ALU.add)), r=[PSB[bf_], "s3_s", "xf32"], w=[f"outst{osl}"])
            p.op("dve", (lambda h, ba_=ba_, osl=osl: h.tensor_add(
                outst[:, osl, :], outst[:, osl, :], ps[ba_][:])),
                r=[PSB[ba_]], w=[f"outst{osl}"])
            p.op("sp", (lambda h, tt=tt, co=co, osl=osl: h.dma_start(
                out=out_d[:, co, tt * T:(tt + 1) * T], in_=outst[:, osl, :])),
                r=[f"outst{osl}"], w=["out_done"])
        if tt + 1 < NT:
            x4load(tt + 1)

    if debug:
        dump("on_h", (lambda: on_h), [128, 8, L], BF16, ["on_h"])
        dump("h0_h", (lambda: h0_h), [128, 32, L], BF16, ["h0_h"])
        dump("h1_h", (lambda: h1o_h), [128, 32, L], BF16, ["h1dst"])
        dump("h2_h", (lambda: h2o_h), [128, 32, L], BF16, ["h2dst"])
    p.waitop("sp", r=["out_done"], xw=([p.last_lab["sp"]] if debug else []))

    p.emit(nc)
    return nc, dbg_outs


def host_inputs(inputs):
    g = {}

    def blk8(w):
        wt = np.ascontiguousarray(w.T)
        return np.ascontiguousarray(wt.reshape(8, 128, 8, 128).transpose(1, 0, 2, 3))

    def blk8co(w):
        wt = np.ascontiguousarray(w.T)
        return np.ascontiguousarray(wt.reshape(8, 128, 8, 128).transpose(1, 2, 0, 3))

    g["wq"] = _bf(blk8co(inputs["L0_attn_Wq"]))
    g["wk"] = _bf(blk8co(inputs["L0_attn_Wk"]))
    g["wv"] = _bf(blk8co(inputs["L0_attn_Wv"]))
    g["wo"] = blk8(inputs["L0_attn_Wo"])
    g["wb"] = np.ascontiguousarray(
        inputs["L0_attn_Wb"].T.reshape(8, 128, 4).transpose(1, 0, 2))
    g["gw"] = np.ascontiguousarray(
        inputs["L0_attn_gate_w"].reshape(20, 1024).T.reshape(8, 128, 20).transpose(1, 0, 2))
    g["gb"] = _bf(inputs["L0_attn_gate_b"].reshape(1, 20))
    g["ltmp"] = np.ascontiguousarray(
        np.repeat(inputs["L0_attn_log_temp"].reshape(4), 5).reshape(1, 20))
    g["eps"] = np.ascontiguousarray(
        np.repeat(inputs["L0_attn_eps_param"].reshape(4), 5).reshape(1, 20))
    nw = inputs["L0_attn_norm_w"]
    g["nw"] = np.ascontiguousarray(
        np.stack([nw[(ck % 2) * 128:(ck % 2) * 128 + 128] for ck in range(8)], axis=1))
    for nm, key in (("cq", "L0_attn_conv_q"), ("ck", "L0_attn_conv_k"), ("cv", "L0_attn_conv_v")):
        g[nm] = np.ascontiguousarray(inputs[key].reshape(8, 128, 4).transpose(1, 0, 2))
    fcf = np.zeros((32, 128, 21, 32), np.float32)
    firs = [(0, inputs["L0_attn_fir_l"].reshape(1024, KL), KL, 16),
            (16, inputs["L0_attn_fir_m"].reshape(1024, KM), KM, 4),
            (20, inputs["L0_attn_fir_s"].reshape(1024, KS), KS, 1)]
    cidx = np.arange(32)
    for rc in range(32):
        ch0 = (rc // 4) * 128 + (rc % 4) * 32
        for goff, fl_, K, G in firs:
            pad = 4 * G - K
            for gi in range(G):
                for r in range(4):
                    jp = 4 * gi + r - pad
                    if jp >= 0:
                        fcf[rc, 32 * r + cidx, goff + gi, cidx] = fl_[ch0:ch0 + 32, jp]
    g["fcf"] = np.ascontiguousarray(fcf.reshape(32, 128, 21 * 32))
    rsel = np.zeros((128, 4, 32), np.float32)
    for b in range(4):
        rsel[32 * b + cidx, b, cidx] = 1.0
    g["rsel"] = _bf(rsel.reshape(128, 128))

    def blkffn(w, nki, nco):
        wt = np.ascontiguousarray(w.T)
        return np.ascontiguousarray(
            wt.reshape(nki, 128, nco, 128).transpose(2, 1, 0, 3))

    g["w1"] = _bf(blkffn(inputs["L0_ffn_w1_w"], 8, 32))
    g["w2"] = _bf(blkffn(inputs["L0_ffn_w2_w"], 8, 32))
    g["h1"] = _bf(blkffn(inputs["L0_ffn_h1_w"], 32, 32))
    g["h2"] = _bf(blkffn(inputs["L0_ffn_h2_w"], 32, 32))
    g["w3"] = _bf(np.ascontiguousarray(
        np.ascontiguousarray(inputs["L0_ffn_w3_w"].T).reshape(32, 128, 8, 128)
        .transpose(1, 2, 0, 3)))
    g["s1"] = np.ascontiguousarray(inputs["L0_ffn_w1_s"].reshape(32, 128).T)
    g["s2"] = np.ascontiguousarray(inputs["L0_ffn_w2_s"].reshape(32, 128).T)
    g["sh1"] = np.ascontiguousarray(inputs["L0_ffn_h1_s"].reshape(32, 128).T)
    g["sh2"] = np.ascontiguousarray(inputs["L0_ffn_h2_s"].reshape(32, 128).T)
    g["s3"] = np.ascontiguousarray(inputs["L0_ffn_w3_s"].reshape(8, 128).T)

    g["ident"] = _bf(np.eye(128, dtype=np.float32))
    g["onesc"] = _bf(np.ones((128, 1), np.float32))
    g["onesr"] = _bf(np.ones((1, 128), np.float32))
    g["ones33"] = _bf(np.ones((33, 128), np.float32))
    e4 = np.zeros((4, 4, 128), np.float32)
    for hh in range(4):
        e4[hh, hh, :] = 1.0
    g["e4"] = _bf(e4)
    e20 = np.zeros((20, 20, 128), np.float32)
    for r in range(20):
        e20[r, r, :] = 1.0
    g["e20"] = _bf(e20)
    one = np.ones((128, 128), np.float32)
    g["mstril"] = _bf(np.tril(one, -1))
    g["mstriu"] = _bf(np.triu(one, 1))
    g["mtriu"] = _bf(np.triu(one, 0))
    for k in g:
        if g[k].dtype == np.float64:
            g[k] = g[k].astype(np.float32)
    return g


_CACHE = {}


def kernel(**inputs):
    x = np.asarray(inputs["L0_attn_hidden_states"], np.float32)
    B = x.shape[0]
    shared = host_inputs(inputs)
    if "nc" not in _CACHE:
        _CACHE["nc"], _CACHE["dbg"] = build_program(debug=DBG)
    nc = _CACHE["nc"]
    in_maps = []
    for b in range(B):
        m = dict(shared)
        m["x"] = np.ascontiguousarray(x[b].T)
        in_maps.append(m)
    res = run_bass_kernel_spmd(nc, in_maps, core_ids=list(range(B)))
    outs = []
    for b in range(B):
        o = np.asarray(res.results[b]["out"])
        outs.append(o.transpose(2, 1, 0).reshape(L, D))
    if DBG:
        kernel.last_results = res
    return np.stack(outs, 0).astype(np.float32)


# version 21 (final)
# speedup vs baseline: 1.0499x; 1.0223x over previous
import numpy as np
import ml_dtypes
import concourse.bass as bass
import concourse.mybir as mybir
from concourse.bass_utils import run_bass_kernel_spmd

F32 = mybir.dt.float32
BF16 = mybir.dt.bfloat16
AF = mybir.ActivationFunctionType
ALU = mybir.AluOpType
AX = mybir.AxisListType

D = 1024
L = 4096
H = 4
DKV = 256
T = 512
NT = L // T
NCK = 8
CC = 128
NCH = T // CC
KS, KM, KL = 3, 15, 63
DH = 4096

DBG = False
P_NT = NT
P_ATTN = True
P_DELTA = True
P_FIR = True
P_MIX = True
P_PROJ = True
P_GATES = True
P_F1 = True
P_F2 = True
P_F3 = True
P_F4 = True

ENG2BLK = {"pe": "tensor", "act": "scalar", "dve": "vector", "pool": "gpsimd", "sp": "sync"}
DMA_ENG = {"sp", "pool"}


class _Res:
    __slots__ = ("w", "rs")

    def __init__(self):
        self.w = None
        self.rs = []


DMA_NSEM = 12


class Prog:
    def __init__(self):
        self.items = {e: [] for e in ENG2BLK}
        self.lab = []
        self.cnt = {e: 0 for e in ENG2BLK}
        for e in DMA_ENG:
            self.cnt[e + "_c"] = 0
            for i in range(DMA_NSEM):
                self.cnt[f"{e}{i}"] = 0
        self.dmasem_i = {e: 0 for e in DMA_ENG}
        self.sem_last = {}
        self.res = {}
        self.pending_join = None
        self.join_left = set()
        self.last_lab = {}

    def R(self, key):
        r = self.res.get(key)
        if r is None:
            r = self.res[key] = _Res()
        return r

    def op(self, eng, fn, r=(), w=(), xw=(), small=False, dma=None):
        is_dma = (eng in DMA_ENG) if dma is None else dma
        waits = []
        for k in r:
            rr = self.R(k)
            if rr.w is not None:
                waits.append(rr.w)
        for k in w:
            rr = self.R(k)
            waits.extend(rr.rs)
            if rr.w is not None:
                waits.append(rr.w)
        waits.extend(xw)
        if self.pending_join is not None and eng in self.join_left:
            waits.extend(self.pending_join)
            self.join_left.discard(eng)
            if not self.join_left:
                self.pending_join = None
        if is_dma:
            si = self.dmasem_i[eng]
            self.dmasem_i[eng] = (si + 1) % DMA_NSEM
            semkey = f"{eng}{si}"
            if semkey in self.sem_last:
                waits.append(self.sem_last[semkey])
            self.cnt[semkey] += 16
        else:
            semkey = eng if eng not in DMA_ENG else eng + "_c"
            self.cnt[semkey] += 1
        lid = len(self.lab)
        self.lab.append((semkey, self.cnt[semkey], is_dma or small or eng in DMA_ENG,
                         16 if is_dma else 1))
        self.sem_last[semkey] = lid
        self.items[eng].append((fn, tuple(sorted(set(waits))), lid))
        self.last_lab[eng] = lid
        for k in r:
            self.R(k).rs.append(lid)
        for k in w:
            rr = self.R(k)
            rr.w = lid
            rr.rs = []
        return lid

    def waitop(self, eng, r=(), xw=()):
        waits = list(xw)
        for k in r:
            rr = self.R(k)
            if rr.w is not None:
                waits.append(rr.w)
        self.items[eng].append((None, tuple(sorted(set(waits))), None))

    def join(self):
        labs = [self.last_lab[e] for e in ENG2BLK if e in self.last_lab]
        self.pending_join = labs
        self.join_left = set(ENG2BLK)

    def emit(self, nc):
        semkeys = list(self.cnt.keys())
        sems = {k: nc.semaphore("sem_" + k).__enter__() for k in semkeys}
        lab = self.lab
        for k in semkeys:
            nc.sync.sem_clear(sems[k])
        nc.all_engine_barrier()
        with nc.Block() as blk:
            for eng, blkname in ENG2BLK.items():
                items = self.items[eng]

                def body(h, eng=eng, items=items):
                    seen = {k: 0 for k in semkeys}
                    for fn, waits, lid in items:
                        tgt = {}
                        for wl in waits:
                            we, wc, nofilt, _inc = lab[wl]
                            if we == eng and not nofilt:
                                continue
                            if wc > tgt.get(we, 0):
                                tgt[we] = wc
                        for we in sorted(tgt):
                            wc = tgt[we]
                            if wc > seen[we]:
                                h.wait_ge(sems[we], wc)
                                seen[we] = wc
                        if fn is None:
                            continue
                        inst = fn(h)
                        inst.then_inc(sems[lab[lid][0]], lab[lid][3])

                getattr(blk, blkname)(body)


def _bf(x):
    return np.ascontiguousarray(x).astype(ml_dtypes.bfloat16)


def build_program(debug=False):
    nc = bass.Bass("TRN2", target_bir_lowering=False)
    p = Prog()

    def din(name, shape, dt=F32):
        return nc.dram_tensor(name, list(shape), dt, kind="ExternalInput").ap()

    x_in = din("x", [D, L])
    wq_d = din("wq", [128, 8, 8, 128], BF16)
    wk_d = din("wk", [128, 8, 8, 128], BF16)
    wv_d = din("wv", [128, 8, 8, 128], BF16)
    wo_d = din("wo", [128, 8, 8, 128])
    wb_d = din("wb", [128, 8, 4])
    gw_d = din("gw", [128, 8, 20])
    gb_d = din("gb", [1, 20], BF16)
    ltmp_d = din("ltmp", [1, 20])
    eps_d = din("eps", [1, 20])
    nw_d = din("nw", [128, 8])
    cq_d = din("cq", [128, 8, 4])
    ck_d = din("ck", [128, 8, 4])
    cv_d = din("cv", [128, 8, 4])
    fcf_d = din("fcf", [32, 128, 21 * 32])
    rsel_d = din("rsel", [128, 4 * 32], BF16)
    w1_d = din("w1", [32, 128, 8, 128], BF16)
    w2_d = din("w2", [32, 128, 8, 128], BF16)
    h1_d = din("h1", [32, 128, 32, 128], BF16)
    h2_d = din("h2", [32, 128, 32, 128], BF16)
    w3_d = din("w3", [128, 8, 32, 128], BF16)
    s1_d = din("s1", [128, 32])
    s2_d = din("s2", [128, 32])
    sh1_d = din("sh1", [128, 32])
    sh2_d = din("sh2", [128, 32])
    s3_d = din("s3", [128, 8])
    ident_d = din("ident", [128, 128], BF16)
    onesc_d = din("onesc", [128, 1], BF16)
    onesr_d = din("onesr", [1, 128], BF16)
    ones33_d = din("ones33", [33, 128], BF16)
    e4_d = din("e4", [4, 4, 128], BF16)
    e20_d = din("e20", [20, 20, 128], BF16)
    mstril_d = din("mstril", [128, 128], BF16)
    mstriu_d = din("mstriu", [128, 128], BF16)
    mtriu_d = din("mtriu", [128, 128], BF16)

    out_d = nc.dram_tensor("out", [128, 8, L], F32, kind="ExternalOutput").ap()

    on_h = nc.dram_tensor("on_h", [128, 8, L], BF16, kind="Internal").ap()
    h0_h = nc.dram_tensor("h0_h", [128, 32, L], BF16, kind="Internal").ap()
    h1o_h = nc.dram_tensor("h1o_h", [128, 32, L], BF16, kind="Internal").ap()
    h2o_h = nc.dram_tensor("h2o_h", [128, 32, L], BF16, kind="Internal").ap()

    dbg_outs = {}

    def dump(name, src_fn, shape, dt, rkeys):
        if not debug:
            return
        ap = nc.dram_tensor("dbg_" + name, list(shape), dt, kind="ExternalOutput").ap()
        dbg_outs[name] = ap
        p.op("sp", (lambda h, ap=ap, src_fn=src_fn: h.dma_start(out=ap, in_=src_fn())), r=rkeys)

    ps = [nc.alloc_psum_tensor(f"ps{i}", [128, 512], F32) for i in range(6)]
    ps_tr = nc.alloc_psum_tensor("ps_tr", [128, 2, 1024], BF16)
    PSB = [f"ps{i}" for i in range(6)]
    PTR = ["ptr0", "ptr1"]

    sb = nc.alloc_sbuf_tensor
    ident_s = sb("ident_s", [128, 128], BF16)
    onesc_s = sb("onesc_s", [128, 1], BF16)
    onesr_s = sb("onesr_s", [1, 128], BF16)
    ones33_s = sb("ones33_s", [33, 128], BF16)
    e4_s = sb("e4_s", [4, 4 * 128], BF16)
    e20_s = sb("e20_s", [20, 20 * 128], BF16)
    mstril_s = sb("mstril_s", [128, 128], BF16)
    mstriu_s = sb("mstriu_s", [128, 128], BF16)
    mtriu_s = sb("mtriu_s", [128, 128], BF16)
    gb_s = sb("gb_s", [1, 20], BF16)
    eps_s = sb("eps_s", [1, 20], F32)
    itbc_s = sb("itbc_s", [128, 80], BF16)
    eabc_s = sb("eabc_s", [128, 80], BF16)
    ebbc_s = sb("ebbc_s", [128, 80], BF16)
    nw_s = sb("nw_s", [128, 8], F32)

    warm_s = sb("warm_s", [1, 4], F32)
    p.op("act", (lambda h: h.memzero(warm_s[:])), w=["warm"], small=True)
    for fn_ in (AF.Exp, AF.Ln, AF.Silu, AF.Gelu, AF.Sigmoid, AF.Exp, AF.Copy):
        p.op("act", (lambda h, fn_=fn_: h.activation(warm_s[:], warm_s[:], fn_)),
             w=["warm"], small=True)

    for dst, src in [(ident_s, ident_d), (onesc_s, onesc_d), (onesr_s, onesr_d),
                     (mstril_s, mstril_d), (mstriu_s, mstriu_d), (mtriu_s, mtriu_d),
                     (gb_s, gb_d)]:
        p.op("sp", (lambda h, dst=dst, src=src: h.dma_start(out=dst[:], in_=src)),
             w=[dst.name])
    p.op("sp", (lambda h: h.dma_start(out=e4_s[:], in_=e4_d.rearrange("a b c -> a (b c)"))), w=["e4_s"])
    p.op("sp", (lambda h: h.dma_start(out=ones33_s[:], in_=ones33_d)), w=["ones33_s"])
    p.op("sp", (lambda h: h.dma_start(out=e20_s[:], in_=e20_d.rearrange("a b c -> a (b c)"))), w=["e20_s"])
    p.op("sp", (lambda h: h.dma_start(out=nw_s[:], in_=nw_d)), w=["nw_s"])

    ltmp_s = sb("ltmp_s", [1, 20], F32)
    p.op("sp", (lambda h: h.dma_start(out=ltmp_s[:], in_=ltmp_d)), w=["ltmp_s"])
    p.op("sp", (lambda h: h.dma_start(out=eps_s[:], in_=eps_d)), w=["eps_s"])

    r1_s = sb("r1_s", [1, 20], F32)
    r2_s = sb("r2_s", [1, 20], F32)
    r1b_s = sb("r1b_s", [1, 20], BF16)

    p.op("act", (lambda h: h.activation(r1_s[:], ltmp_s[:], AF.Exp, scale=-1.0)),
         r=["ltmp_s"], w=["r1_s"])
    p.op("dve", (lambda h: h.tensor_copy(r1b_s[:], r1_s[:])), r=["r1_s"], w=["r1b_s"], small=True)

    def bc80(h, rowap):
        for tt in range(4):
            i = h.matmul(ps[2][0:128, tt * 20:tt * 20 + 20], onesr_s[:], rowap,
                         start=True, stop=True)
        return i

    p.op("pe", (lambda h: bc80(h, r1b_s[:])), r=["r1b_s", "onesr_s"], w=["ps2"])
    p.op("act", (lambda h: h.activation(itbc_s[:], ps[2][0:128, 0:80], AF.Copy)),
         r=["ps2"], w=["itbc_s"])
    p.op("dve", (lambda h: h.tensor_scalar(r2_s[:], eps_s[:], 0.0, 0.2, ALU.max, ALU.min)),
         r=["eps_s"], w=["r2_s"], small=True)
    r2b_s = sb("r2b_s", [1, 20], BF16)
    p.op("dve", (lambda h: h.tensor_copy(r2b_s[:], r2_s[:])), r=["r2_s"], w=["r2b_s"], small=True)
    p.op("pe", (lambda h: bc80(h, r2b_s[:])), r=["r2b_s", "onesr_s"], w=["ps2"])
    p.op("act", (lambda h: h.activation(ebbc_s[:], ps[2][0:128, 0:80], AF.Copy)),
         r=["ps2"], w=["ebbc_s"])
    if debug:
        dump("r2", (lambda: r2_s[:]), [1, 20], F32, ["r2_s"])
        dump("r2b", (lambda: r2b_s[:]), [1, 20], BF16, ["r2b_s"])
        dump("eps_in", (lambda: eps_s[:]), [1, 20], F32, ["eps_s"])
    p.op("dve", (lambda h: h.tensor_scalar(eabc_s[:], ebbc_s[:], -5.0, 1.0, ALU.mult, ALU.add)),
         r=["ebbc_s"], w=["eabc_s"], small=True)

    ag = []

    def asb(name, shape, dt):
        g = nc.sbuf_tensor(name, list(shape), dt)
        t = g.__enter__()
        ag.append(g)
        return t

    wqr = asb("wqr", [128, 4, 8, 128], BF16)
    wkr = asb("wkr", [128, 4, 8, 128], BF16)
    wvr = asb("wvr", [128, 4, 8, 128], BF16)
    wb_s = asb("wb_s", [128, 8 * 4], BF16)
    gw_s = asb("gw_s", [128, 8 * 20], BF16)
    cq_s = asb("cq_s", [128, 8 * 4], F32)
    ck_s = asb("ck_s", [128, 8 * 4], F32)
    cv_s = asb("cv_s", [128, 8 * 4], F32)
    fcf_s = asb("fcf_s", [128, 4, 21 * 32], BF16)
    vrep_s = asb("vrep_s", [128, 2, 573], BF16)
    rsel_s = asb("rsel_s", [128, 4 * 32], BF16)
    w1r = asb("w1r", [128, 3, 8, 128], BF16)
    w2r = asb("w2r", [128, 3, 8, 128], BF16)
    t1r = asb("t1r", [128, 2, T], BF16)
    h0st = asb("h0st", [128, 2, T], BF16)
    s1_s = asb("s1_s", [128, 32], F32)
    s2_s = asb("s2_s", [128, 32], F32)

    for dst, src in [(wb_s, wb_d), (gw_s, gw_d)]:
        p.op("pool", (lambda h, dst=dst, src=src: h.dma_start(
            out=dst[:], in_=src.rearrange("p a b -> p (a b)") if src.ndim == 3
            else src.rearrange("p a b c -> p (a b c)"))), w=[dst.name])
    for dst, src in [(cq_s, cq_d), (ck_s, ck_d), (cv_s, cv_d)]:
        p.op("sp", (lambda h, dst=dst, src=src: h.dma_start(
            out=dst[:], in_=src.rearrange("p a b -> p (a b)"))), w=[dst.name])
    p.op("sp", (lambda h: h.dma_start(out=rsel_s[:], in_=rsel_d)), w=["rsel_s"])
    p.op("sp", (lambda h: h.dma_start(out=s1_s[:], in_=s1_d)), w=["s1_s"])
    p.op("sp", (lambda h: h.dma_start(out=s2_s[:], in_=s2_d)), w=["s2_s"])

    def w_at(ws, ki, co):
        return ws[:, (ki * 8 + co) * 128:(ki * 8 + co) * 128 + 128]

    x_bf = asb("x_bf", [128, 8, T], BF16)
    xn_s = asb("xn_s", [128, 2, 8, T], BF16)
    q_s = asb("q_s", [128, 8, T], BF16)
    k_s = asb("k_s", [128, 8, T], BF16)
    v_s = asb("v_s", [128, 8, 64 + T], BF16)
    pre_s = asb("pre_s", [128, 2, 3 + T], BF16)
    tails = asb("tails", [128, 3, 8, 3], BF16)
    cacc_s = asb("cacc_s", [128, 2, T], BF16)
    beta_s = asb("beta_s", [4, T], F32)
    beta_b = asb("beta_b", [4, T], BF16)
    m_row = asb("m_row", [1, T], F32)
    v_row = asb("v_row", [1, T], F32)
    mb_row = asb("mb_row", [1, T], BF16)
    rb_row = asb("rb_row", [1, T], BF16)
    sq_s = asb("sq_s", [128, 16, 128], BF16)
    lr_row = asb("lr_row", [1, T], F32)
    lr64_s = asb("lr64_s", [64, T], F32)
    lrb64_s = asb("lrb64_s", [64, T], BF16)
    rqbc_s = asb("rqbc_s", [128, T], BF16)
    rkbc_s = asb("rkbc_s", [128, T], BF16)
    bbc_s = asb("bbc_s", [128, T], BF16)
    ql_s = asb("ql_s", [128, 2, 8, 128], BF16)
    kl_s = asb("kl_s", [128, 8, 128], BF16)
    kb_s = asb("kb_s", [128, 8, 128], BF16)
    vb_s = asb("vb_s", [128, 8, 128], BF16)
    ktok_s = asb("ktok_s", [128, 2, 4, 768], BF16)
    Bp_s = asb("Bp_s", [128, 4, 128], BF16)
    Bpt_s = asb("Bpt_s", [128, 4, 128], BF16)
    Bp2_s = asb("Bp2_s", [128, 4, 128], BF16)
    Bpt2_s = asb("Bpt2_s", [128, 4, 128], BF16)
    Pt_s = asb("Pt_s", [128, 2, 4, 128], BF16)
    at_s = asb("at_s", [128, 2, 4, 128], BF16)
    npt_s = asb("npt_s", [128, 4, 128], BF16)
    nwT_s = asb("nwT_s", [128, 2, 512], BF16)
    uadj_s = asb("uadj_s", [128, 2, 512], BF16)
    otok_s = asb("otok_s", [128, 2, 512], BF16)
    S_s = asb("S_s", [128, 4, 512], F32)
    Sb_s = asb("Sb_s", [128, 4, 512], BF16)
    delta_s = asb("delta_s", [128, 8, T], BF16)
    omix_s = asb("omix_s", [128, 8, T], BF16)
    mixt_s = asb("mixt_s", [128, T], BF16)
    mixtp_s = asb("mixtp_s", [128, T], BF16)
    mixq_s = asb("mixq_s", [128, T], BF16)
    pbc5_s = asb("pbc5_s", [128, 5, T], BF16)
    probs_s = asb("probs_s", [128, 80], F32)
    probsb_s = asb("probsb_s", [128, 80], BF16)
    probsf_s = asb("probsf_s", [20, T], BF16)
    gsc_s = asb("gsc_s", [128, 80], F32)
    gmx_s = asb("gmx_s", [128, 16], F32)
    gsum_s = asb("gsum_s", [128, 16], F32)
    gexp_s = asb("gexp_s", [128, 80], BF16)

    p.op("dve", (lambda h: h.memset(v_s[:, :, 0:64], 0.0)),
         w=[f"v_s{c}" for c in range(NCK)])
    p.op("dve", (lambda h: h.memset(tails[:], 0.0)), w=["tails_all"])
    p.op("dve", (lambda h: h.memset(S_s[:], 0.0)),
         w=[f"S{hh}" for hh in range(H)])

    CQS = {0: cq_s, 1: ck_s, 2: cv_s}
    WS = {0: wqr, 1: wkr, 2: wvr}
    WD = {0: wq_d, 1: wk_d, 2: wv_d}
    OUTS = {0: q_s, 1: k_s, 2: v_s}

    BGQ = []
    bgs = {"i": 0, "banks": (5,), "j": 0}

    def bgbank():
        b = bgs["banks"][bgs["j"] % len(bgs["banks"])]
        bgs["j"] += 1
        return b

    def pump(reserve=0, n=1, warm=0):
        for _ in range(n):
            if len(BGQ) - bgs["i"] > reserve:
                BGQ[bgs["i"]]()
                bgs["i"] += 1
            else:
                break
        if False and warm and len(BGQ) - bgs["i"] <= 0:
            p.op("pe", (lambda h: h.matmul(ps[5][:, 0:128], ident_s[:], ident_s[:],
                                           start=True, stop=True)),
                 r=["ident_s"], w=["ps5"])

    def bg_drain(upto):
        while bgs["i"] < min(upto, len(BGQ)):
            BGQ[bgs["i"]]()
            bgs["i"] += 1

    def mk_c0(co):
        ws = co % 3
        def c():
            p.op("sp", (lambda h: h.dma_start(out=w1r[:, ws, :, :], in_=w1_d[co])),
                 w=[f"w1r{ws}"])
            p.op("pool", (lambda h: h.dma_start(out=w2r[:, ws, :, :], in_=w2_d[co])),
                 w=[f"w2r{ws}"])
        return c

    def mk_u1(t, co):
        ws, tb, pr = co % 3, co % 2, t % 2
        def c():
            b = bgbank()
            def mm(h):
                for ki in range(8):
                    i = h.matmul(ps[b][:], w1r[:, ws, ki, :], xn_s[:, pr, ki, :],
                                 start=(ki == 0), stop=(ki == 7))
                return i
            p.op("pe", mm, r=[f"w1r{ws}"] + [f"xn{pr}{ki}" for ki in range(8)], w=[PSB[b]])
            p.op("act", (lambda h: h.activation(t1r[:, tb, :], ps[b][:], AF.Silu,
                                                scale=s1_s[:, co:co + 1])),
                 r=[PSB[b], "s1_s"], w=[f"t1r{tb}"])
        return c

    def mk_u2(t, co):
        ws, tb, hs, pr = co % 3, co % 2, co % 2, t % 2
        def c():
            b = bgbank()
            def mm(h):
                for ki in range(8):
                    i = h.matmul(ps[b][:], w2r[:, ws, ki, :], xn_s[:, pr, ki, :],
                                 start=(ki == 0), stop=(ki == 7))
                return i
            p.op("pe", mm, r=[f"w2r{ws}"] + [f"xn{pr}{ki}" for ki in range(8)], w=[PSB[b]])
            p.op("dve", (lambda h: h.scalar_tensor_tensor(
                h0st[:, hs, :], ps[b][:], s2_s[:, co:co + 1], t1r[:, tb, :],
                ALU.mult, ALU.mult)), r=[PSB[b], "s2_s", f"t1r{tb}"], w=[f"h0st{hs}"])
            p.op("sp", (lambda h: h.dma_start(out=h0_h[:, co, t * T:(t + 1) * T],
                                              in_=h0st[:, hs, :])),
                 r=[f"h0st{hs}"], w=["h0_h", f"h0_h{t}"])
        return c

    def append_f1(t):
        if not P_F1:
            return
        BGQ.append(mk_c0(0))
        BGQ.append(mk_c0(1))
        for co in range(32):
            BGQ.append(mk_u1(t, co))
            BGQ.append(mk_u2(t, co))
            if co + 2 < 32:
                BGQ.append(mk_c0(co + 2))

    drain_target = 0
    for t in range(P_NT if P_ATTN else 0):
        t0 = t * T
        pr = t % 2
        xk = "x0"
        def xload(tt_):
            p.op("pool", (lambda h, tt_=tt_: h.dma_start(
                out=x_bf[:],
                in_=x_in.rearrange("(ck p) t -> p ck t", p=128)[:, :, tt_ * T:tt_ * T + T])),
                w=["x0"])
        if t == 0:
            xload(0)

        def ln_sum(h):
            for ck in range(NCK):
                i = h.matmul(ps[4][0:1, :], onesc_s[:], x_bf[:, ck, :],
                             start=(ck == 0), stop=(ck == NCK - 1))
            return i
        p.op("pe", ln_sum, r=[xk, "onesc_s"], w=["ps4"])
        for ck in range(NCK):
            sp_ = ck % 2
            p.op("dve", (lambda h, ck=ck, sp_=sp_: h.tensor_tensor(
                sq_s[:, 4 * sp_:4 * sp_ + 4, :].rearrange("p a b -> p (a b)"),
                x_bf[:, ck, :], x_bf[:, ck, :], ALU.mult)),
                r=[xk], w=[f"xsq{sp_}"])
            p.op("pe", (lambda h, ck=ck, sp_=sp_: h.matmul(
                ps[4][32:33, :], onesc_s[:],
                sq_s[:, 4 * sp_:4 * sp_ + 4, :].rearrange("p a b -> p (a b)"),
                start=(ck == 0), stop=(ck == NCK - 1), tile_position=(0, 32))),
                r=[f"xsq{sp_}", "onesc_s"], w=["ps4"])

        p.op("dve", (lambda h: h.tensor_scalar_mul(m_row[:], ps[4][0:1, :], 1.0 / D)),
             r=["ps4"], w=["m_row"], small=True)
        p.op("dve", (lambda h: h.tensor_scalar_mul(v_row[:], ps[4][32:33, :], 1.0 / D)),
             r=["ps4"], w=["v_row"], small=True)
        p.op("dve", (lambda h: h.tensor_tensor(lr_row[:], m_row[:], m_row[:], ALU.mult)),
             r=["m_row"], w=["lr_row"], small=True)
        p.op("dve", (lambda h: h.scalar_tensor_tensor(
            v_row[:], v_row[:], 1e-5, lr_row[:], ALU.add, ALU.subtract)),
            r=["lr_row"], w=["v_row"], small=True)
        p.op("act", (lambda h: h.activation(v_row[:], v_row[:], AF.Ln)), w=["v_row"])
        p.op("act", (lambda h: h.activation(v_row[:], v_row[:], AF.Exp, scale=-0.5)), w=["v_row"])
        p.op("dve", (lambda h: h.tensor_copy(mb_row[:], m_row[:])), r=["m_row"], w=["mb_row"], small=True)
        p.op("dve", (lambda h: h.tensor_copy(rb_row[:], v_row[:])), r=["v_row"], w=["rb_row"], small=True)
        p.op("pe", (lambda h: h.matmul(ps[2][:], onesr_s[:], mb_row[:], start=True, stop=True)),
             r=["mb_row", "onesr_s"], w=["ps2"])
        p.op("pe", (lambda h: h.matmul(ps[3][:], onesr_s[:], rb_row[:], start=True, stop=True)),
             r=["rb_row", "onesr_s"], w=["ps3"])
        for ck in range(NCK):
            p.op("dve", (lambda h, pr=pr, ck=ck: h.tensor_sub(
                xn_s[:, pr, ck, :], x_bf[:, ck, :], ps[2][:])), r=[xk, "ps2"], w=[f"xn{pr}{ck}"])
            p.op("dve", (lambda h, pr=pr, ck=ck: h.tensor_mul(
                xn_s[:, pr, ck, :], xn_s[:, pr, ck, :], ps[3][:])), r=["ps3"], w=[f"xn{pr}{ck}"])
        if t + 1 < (P_NT if P_ATTN else 0):
            xload(t + 1)
        drain_target = len(BGQ)
        append_f1(t)

        for tn in range(3 if P_PROJ else 0):
            ws = WS[tn]
            outb = OUTS[tn]
            cqs = CQS[tn]
            ooff = 64 if tn == 2 else 0

            def wload(tn_, co_):
                sl = co_ % 4
                q_ = "sp" if tn_ != 1 else "pool"
                p.op(q_, (lambda h, tn_=tn_, co_=co_, sl=sl: h.dma_start(
                    out=WS[tn_][:, sl, :, :], in_=WD[tn_][:, co_, :, :])),
                    w=[f"{WS[tn_].name}{sl}"])
            for co_ in range(4):
                wload(tn, co_)
            for co in range(NCK):
                bank = co % 4
                pk = PSB[bank]
                pp = (tn * NCK + co) % 2
                sl = co % 4

                def proj(h, ws=ws, sl=sl, bank=bank, pr=pr):
                    for ki in range(NCK):
                        i = h.matmul(ps[bank][:], ws[:, sl, ki, :], xn_s[:, pr, ki, :],
                                     start=(ki == 0), stop=(ki == NCK - 1))
                    return i
                p.op("pe", proj, r=[f"{ws.name}{sl}"] + [f"xn{pr}{ki}" for ki in range(NCK)],
                     w=[pk])
                if co + 4 < NCK:
                    wload(tn, co + 4)
                p.op("act", (lambda h, bank=bank, pp=pp: h.activation(
                    pre_s[:, pp, 3:3 + T], ps[bank][:], AF.Copy)), r=[pk], w=[f"pre{pp}"])
                p.op("dve", (lambda h, tn=tn, co=co, pp=pp: h.tensor_copy(
                    pre_s[:, pp, 0:3], tails[:, tn, co, :])), r=["tails_all"],
                    w=[f"pre{pp}"], small=True)
                p.op("dve", (lambda h, tn=tn, co=co, pp=pp: h.tensor_copy(
                    tails[:, tn, co, :], pre_s[:, pp, T:T + 3])), r=[f"pre{pp}"],
                    w=["tails_all"], small=True)
                p.op("dve", (lambda h, cqs=cqs, co=co, pp=pp: h.tensor_scalar_mul(
                    cacc_s[:, pp, :], pre_s[:, pp, 0:T], cqs[:, co * 4:co * 4 + 1])),
                    r=[f"pre{pp}", cqs.name], w=[f"cacc{pp}"])
                for j in range(1, 4):
                    p.op("dve", (lambda h, cqs=cqs, co=co, j=j, pp=pp: h.scalar_tensor_tensor(
                        cacc_s[:, pp, :], pre_s[:, pp, j:j + T], cqs[:, co * 4 + j:co * 4 + j + 1],
                        cacc_s[:, pp, :], ALU.mult, ALU.add)), r=[f"pre{pp}", cqs.name],
                        w=[f"cacc{pp}"])
                p.op("act", (lambda h, outb=outb, co=co, ooff=ooff, pp=pp: h.activation(
                    outb[:, co, ooff:ooff + T], cacc_s[:, pp, :], AF.Silu)),
                    r=[f"cacc{pp}"], w=[f"{outb.name}{co}"])
                pump(reserve=96)

        if P_PROJ:
            def beta_mm(h, pr=pr):
                for ki in range(NCK):
                    i = h.matmul(ps[4][0:4, :], wb_s[:, ki * 4:ki * 4 + 4], xn_s[:, pr, ki, :],
                                 start=(ki == 0), stop=(ki == NCK - 1))
                return i
            p.op("pe", beta_mm, r=["wb_s"] + [f"xn{pr}{ki}" for ki in range(NCK)], w=["ps4"])
            p.op("act", (lambda h: h.activation(beta_s[:], ps[4][0:4, :], AF.Sigmoid)),
                 r=["ps4"], w=["beta_s"])
            p.op("dve", (lambda h: h.tensor_copy(beta_b[:], beta_s[:])), r=["beta_s"], w=["beta_b"])
            pump(n=6, warm=1)

        def d1_chunk(c):
            cs0 = c * CC
            cp = c % 2
            for hh in range(H):
                for ci, ck in enumerate((2 * hh, 2 * hh + 1)):
                    for qi, src in enumerate((q_s, k_s)):
                        idx = hh * 4 + ci * 2 + qi
                        p.op("dve", (lambda h, src=src, ck=ck, cs0=cs0, idx=idx:
                                     h.tensor_tensor(
                            sq_s[:, idx, :], src[:, ck, cs0:cs0 + CC],
                            src[:, ck, cs0:cs0 + CC],
                            ALU.mult)), r=[f"{src.name}{ck}"], w=["xsq0"], small=True)

            def l2mm(h):
                for hh in range(H):
                    for qi in range(2):
                        for ci in range(2):
                            idx = hh * 4 + ci * 2 + qi
                            row = 0 if qi == 0 else 32
                            i = h.matmul(ps[4][row:row + 1, hh * 128:(hh + 1) * 128],
                                         onesc_s[:], sq_s[:, idx, :],
                                         start=(ci == 0), stop=(ci == 1),
                                         tile_position=(0, row))
                return i
            p.op("pe", l2mm, r=["xsq0", "onesc_s"], w=["ps4"])
            p.op("dve", (lambda h: h.tensor_scalar_add(lr64_s[:], ps[4][0:64, :], 1e-6)),
                 r=["ps4"], w=["lr64"], small=True)
            p.op("act", (lambda h: h.activation(lr64_s[:], lr64_s[:], AF.Ln)), w=["lr64"])
            p.op("act", (lambda h: h.activation(lr64_s[:], lr64_s[:], AF.Exp, scale=-0.5)),
                 w=["lr64"])
            p.op("dve", (lambda h: h.tensor_copy(lrb64_s[:], lr64_s[:])), r=["lr64"],
                 w=["lrb64"], small=True)

            def bc_q(h):
                for hh in range(H):
                    i = h.matmul(ps[2][:, hh * 128:(hh + 1) * 128], onesr_s[:],
                                 lrb64_s[0:1, hh * 128:(hh + 1) * 128], start=True, stop=True)
                return i
            p.op("pe", bc_q, r=["lrb64", "onesr_s"], w=["ps2"])
            p.op("act", (lambda h: h.activation(rqbc_s[:], ps[2][:], AF.Copy)), r=["ps2"], w=["rqbc"])
            pump(warm=1)

            def bc_b(h, cs0=cs0):
                for hh in range(H):
                    i = h.matmul(ps[2][:, hh * 128:(hh + 1) * 128],
                                 e4_s[:, hh * 128:(hh + 1) * 128],
                                 beta_b[:, cs0:cs0 + CC], start=True, stop=True)
                return i
            p.op("pe", bc_b, r=["beta_b", "e4_s"], w=["ps2"])
            p.op("act", (lambda h: h.activation(bbc_s[:], ps[2][:], AF.Copy)), r=["ps2"], w=["bbc"])
            pump(warm=1)

            def bc_k(h):
                for hh in range(H):
                    i = h.matmul(ps[3][:, hh * 128:(hh + 1) * 128], ones33_s[32:33, :],
                                 lrb64_s[32:33, hh * 128:(hh + 1) * 128], start=True, stop=True)
                return i
            p.op("pe", bc_k, r=["lrb64", "ones33_s"], w=["ps3"])
            p.op("act", (lambda h: h.activation(rkbc_s[:], ps[3][:], AF.Copy)), r=["ps3"], w=["rkbc"])
            pump(warm=1)

            for hh in range(H):
                hc = slice(hh * 128, (hh + 1) * 128)
                for ck in (2 * hh, 2 * hh + 1):
                    p.op("dve", (lambda h, ck=ck, hc=hc, cs0=cs0, cp=cp: h.tensor_tensor(
                        ql_s[:, cp, ck, :], q_s[:, ck, cs0:cs0 + CC], rqbc_s[:, hc], ALU.mult)),
                        r=[f"q_s{ck}", "rqbc"], w=[f"ql{cp}{ck}"], small=True)
                    p.op("dve", (lambda h, ck=ck, hc=hc, cs0=cs0: h.tensor_tensor(
                        kl_s[:, ck, :], k_s[:, ck, cs0:cs0 + CC], rkbc_s[:, hc], ALU.mult)),
                        r=[f"k_s{ck}", "rkbc"], w=[f"kl{ck}"], small=True)
                    p.op("dve", (lambda h, ck=ck, hc=hc: h.tensor_tensor(
                        kb_s[:, ck, :], kl_s[:, ck, :], bbc_s[:, hc], ALU.mult)),
                        r=[f"kl{ck}", "bbc"], w=[f"kb{ck}"], small=True)
                    p.op("dve", (lambda h, ck=ck, hc=hc, cs0=cs0: h.tensor_tensor(
                        vb_s[:, ck, :], v_s[:, ck, 64 + cs0:64 + cs0 + CC], bbc_s[:, hc], ALU.mult)),
                        r=[f"v_s{ck}", "bbc"], w=[f"vb{ck}"], small=True)

            for hh in range(H):
                tp = hh % 2

                def trs(h, hh=hh, tp=tp):
                    for si, src in enumerate((kl_s, kb_s, vb_s)):
                        for ci, ck in enumerate((2 * hh, 2 * hh + 1)):
                            i = h.transpose(ps_tr[:, tp, si * 256 + ci * 128: si * 256 + ci * 128 + 128],
                                            src[:, ck, :], ident_s[:])
                    return i
                p.op("pe", trs, r=[f"kl{2*hh}", f"kl{2*hh+1}", f"kb{2*hh}", f"kb{2*hh+1}",
                                   f"vb{2*hh}", f"vb{2*hh+1}", "ident_s"], w=[PTR[tp]])
                p.op("dve", (lambda h, hh=hh, tp=tp, cp=cp: h.tensor_copy(
                    ktok_s[:, cp, hh, :], ps_tr[:, tp, 0:768])), r=[PTR[tp]], w=[f"ktok{cp}{hh}"])
                pump(warm=1)

            Bx_s, Bxt_s = (Bp_s, Bpt_s) if c % 2 == 0 else (Bp2_s, Bpt2_s)
            BK, BKT = ("Bp", "Bpt") if c % 2 == 0 else ("Bq", "Bqt")
            for hh in range(H):
                def bmm(h, hh=hh, cp=cp):
                    i = h.matmul(ps[0][:, 0:128], kb_s[:, 2 * hh, :], kl_s[:, 2 * hh, :],
                                 start=True, stop=False)
                    i = h.matmul(ps[0][:, 0:128], kb_s[:, 2 * hh + 1, :], kl_s[:, 2 * hh + 1, :],
                                 start=False, stop=True)
                    i = h.matmul(ps[0][:, 128:256], kl_s[:, 2 * hh, :], kb_s[:, 2 * hh, :],
                                 start=True, stop=False)
                    i = h.matmul(ps[0][:, 128:256], kl_s[:, 2 * hh + 1, :], kb_s[:, 2 * hh + 1, :],
                                 start=False, stop=True)
                    i = h.matmul(ps[0][:, 256:384], kl_s[:, 2 * hh, :], ql_s[:, cp, 2 * hh, :],
                                 start=True, stop=False)
                    i = h.matmul(ps[0][:, 256:384], kl_s[:, 2 * hh + 1, :], ql_s[:, cp, 2 * hh + 1, :],
                                 start=False, stop=True)
                    return i
                p.op("pe", bmm, r=[f"kb{2*hh}", f"kb{2*hh+1}", f"kl{2*hh}", f"kl{2*hh+1}",
                                   f"ql{cp}{2*hh}", f"ql{cp}{2*hh+1}"], w=["ps0"])
                p.op("dve", (lambda h, hh=hh, Bx_s=Bx_s: h.tensor_tensor(
                    Bx_s[:, hh, :], ps[0][:, 0:128], mstril_s[:], ALU.mult)),
                    r=["ps0", "mstril_s"], w=[f"{BK}{hh}"], small=True)
                p.op("dve", (lambda h, hh=hh, Bxt_s=Bxt_s: h.tensor_tensor(
                    Bxt_s[:, hh, :], ps[0][:, 128:256], mstriu_s[:], ALU.mult)),
                    r=["ps0", "mstriu_s"], w=[f"{BKT}{hh}"], small=True)
                p.op("dve", (lambda h, hh=hh, cp=cp: h.tensor_tensor(
                    at_s[:, cp, hh, :], ps[0][:, 256:384], mtriu_s[:], ALU.mult)),
                    r=["ps0", "mtriu_s"], w=[f"at{cp}{hh}"], small=True)
                p.op("dve", (lambda h, hh=hh, cp=cp, Bxt_s=Bxt_s: h.tensor_sub(
                    Pt_s[:, cp, hh, :], ident_s[:], Bxt_s[:, hh, :])),
                    r=[f"{BKT}{hh}", "ident_s"], w=[f"Pt{cp}{hh}"], small=True)
                pump(warm=1)

        def pairwaves(a, b):
            CFG = ((a, 4, 0, Bp_s, Bpt_s, "Bp", "Bpt"),
                   (b, 5, 1, Bp2_s, Bpt2_s, "Bq", "Bqt"))
            for it in range(1, 7):
                last = (it == 6)
                for c, bank, pslot, Bx, Bxt, BK, BKT in CFG:
                    BALL = [f"{BK}{hh}" for hh in range(H)]
                    BTALL = [f"{BKT}{hh}" for hh in range(H)]

                    def tsq(h, bank=bank, Bx=Bx, Bxt=Bxt):
                        for hh in range(H):
                            i = h.matmul(ps[bank][:, hh * 128:(hh + 1) * 128],
                                         Bxt[:, hh, :], Bx[:, hh, :], start=True, stop=True)
                        return i
                    p.op("pe", tsq, r=BALL + BTALL, w=[PSB[bank]])
                    p.op("act", (lambda h, bank=bank, Bx=Bx: h.activation(
                        Bx[:].rearrange("p a b -> p (a b)"), ps[bank][:], AF.Copy)),
                        r=[PSB[bank]], w=BALL)
                if not last:
                    for c, bank, pslot, Bx, Bxt, BK, BKT in CFG:
                        BALL = [f"{BK}{hh}" for hh in range(H)]
                        BTALL = [f"{BKT}{hh}" for hh in range(H)]

                        def ttr(h, pslot=pslot, Bx=Bx):
                            for hh in range(H):
                                i = h.transpose(ps_tr[:, pslot, hh * 128:(hh + 1) * 128],
                                                Bx[:, hh, :], ident_s[:])
                            return i
                        p.op("pe", ttr, r=BALL + ["ident_s"], w=[PTR[pslot]])
                        p.op("dve", (lambda h, pslot=pslot, Bxt=Bxt: h.tensor_copy(
                            Bxt[:].rearrange("p a b -> p (a b)"), ps_tr[:, pslot, 0:512])),
                            r=[PTR[pslot]], w=BTALL, small=True)
                for c, bank, pslot, Bx, Bxt, BK, BKT in CFG:
                    BALL = [f"{BK}{hh}" for hh in range(H)]
                    cp = c % 2
                    PTALL = [f"Pt{cp}{hh}" for hh in range(H)]

                    def tch(h, bank=bank, Bx=Bx, cp=cp):
                        for hh in range(H):
                            h.matmul(ps[bank][:, hh * 128:(hh + 1) * 128], Bx[:, hh, :],
                                     Pt_s[:, cp, hh, :], start=True, stop=False)
                            i = h.matmul(ps[bank][:, hh * 128:(hh + 1) * 128], ident_s[:],
                                         Pt_s[:, cp, hh, :], start=False, stop=True)
                        return i
                    p.op("pe", tch, r=BALL + PTALL + ["ident_s"], w=[PSB[bank]])
                    p.op("act", (lambda h, bank=bank, cp=cp: h.activation(
                        Pt_s[:, cp, :, :].rearrange("p a b -> p (a b)"), ps[bank][:], AF.Copy)),
                        r=[PSB[bank]], w=PTALL)

        def nwsl(hh, lo):
            return nwT_s[:, hh % 2, (hh // 2) * 256 + lo:(hh // 2) * 256 + lo + 128]

        def uasl(hh):
            return uadj_s[:, hh % 2, (hh // 2) * 256:(hh // 2) * 256 + 256]

        def post_waves(c):
            cs0 = c * CC
            cp = c % 2
            PTALL = [f"Pt{cp}{hh}" for hh in range(H)]
            waves = []

            def pw0():
                for hh in range(H):
                    p.op("dve", (lambda h, hh=hh, cp=cp: h.tensor_scalar_mul(
                        npt_s[:, hh, :], Pt_s[:, cp, hh, :], -1.0)), r=[f"Pt{cp}{hh}"],
                        w=[f"npt{hh}"], small=True)
                    p.op("dve", (lambda h, hh=hh: h.tensor_copy(Sb_s[:, hh, :], S_s[:, hh, :])),
                         r=[f"S{hh}"], w=[f"Sb{hh}"])
            waves.append(pw0)

            def pw1():
                def nwmm(h, cp=cp):
                    for hh in range(H):
                        nb = hh % 2
                        cb = (hh // 2) * 256
                        h.matmul(ps[nb][:, cb:cb + 128], ktok_s[:, cp, hh, 256:384],
                                 npt_s[:, hh, :], start=True, stop=True)
                        i = h.matmul(ps[nb][:, cb + 128:cb + 256], ktok_s[:, cp, hh, 384:512],
                                     npt_s[:, hh, :], start=True, stop=True)
                    return i
                p.op("pe", nwmm, r=[f"ktok{cp}{hh}" for hh in range(H)] +
                     [f"npt{hh}" for hh in range(H)], w=["ps0", "ps1"])
                p.op("act", (lambda h: h.activation(nwT_s[:, 0, :], ps[0][:], AF.Copy)),
                     r=["ps0"], w=["nwT0"])
                p.op("act", (lambda h: h.activation(nwT_s[:, 1, :], ps[1][:], AF.Copy)),
                     r=["ps1"], w=["nwT1"])
                pump(warm=1)
            waves.append(pw1)

            def pw2():
                def uadj(h, cp=cp):
                    for hh in range(H):
                        ub = 2 + hh % 2
                        cb = (hh // 2) * 256
                        h.matmul(ps[ub][:, cb:cb + 256], Pt_s[:, cp, hh, :],
                                 ktok_s[:, cp, hh, 512:768], start=True, stop=False)
                        h.matmul(ps[ub][:, cb:cb + 256], nwsl(hh, 0), Sb_s[:, hh, 0:256],
                                 start=False, stop=False)
                        i = h.matmul(ps[ub][:, cb:cb + 256], nwsl(hh, 128), Sb_s[:, hh, 256:512],
                                     start=False, stop=True)
                    return i
                p.op("pe", uadj, r=PTALL + [f"ktok{cp}{hh}" for hh in range(H)] + ["nwT0", "nwT1"] +
                     [f"Sb{hh}" for hh in range(H)], w=["ps2", "ps3"])
                p.op("act", (lambda h: h.activation(uadj_s[:, 0, :], ps[2][:], AF.Copy)),
                     r=["ps2"], w=["uadj0"])
                p.op("act", (lambda h: h.activation(uadj_s[:, 1, :], ps[3][:], AF.Copy)),
                     r=["ps3"], w=["uadj1"])
                pump(warm=1)
            waves.append(pw2)

            def pw3():
                def omm(h, cp=cp):
                    for hh in range(H):
                        ob = hh % 2
                        cb = (hh // 2) * 256
                        h.matmul(ps[ob][:, cb:cb + 256], ql_s[:, cp, 2 * hh, :], Sb_s[:, hh, 0:256],
                                 start=True, stop=False)
                        h.matmul(ps[ob][:, cb:cb + 256], ql_s[:, cp, 2 * hh + 1, :],
                                 Sb_s[:, hh, 256:512], start=False, stop=False)
                        i = h.matmul(ps[ob][:, cb:cb + 256], at_s[:, cp, hh, :], uasl(hh),
                                     start=False, stop=True)
                    return i
                p.op("pe", omm, r=[f"ql{cp}{ck}" for ck in range(NCK)] + ["uadj0", "uadj1"] +
                     [f"Sb{hh}" for hh in range(H)] + [f"at{cp}{hh}" for hh in range(H)],
                     w=["ps0", "ps1"])
                p.op("act", (lambda h: h.activation(otok_s[:, 0, :], ps[0][:], AF.Copy)),
                     r=["ps0"], w=["otok0"])
                p.op("act", (lambda h: h.activation(otok_s[:, 1, :], ps[1][:], AF.Copy)),
                     r=["ps1"], w=["otok1"])
                pump(warm=1)
            waves.append(pw3)

            def pw4():
                for wv in range(2):
                    def smm(h, wv=wv, cp=cp):
                        for hh in (2 * wv, 2 * wv + 1):
                            sb_ = 2 + hh % 2
                            h.matmul(ps[sb_][:, 0:256], ktok_s[:, cp, hh, 0:128], uasl(hh),
                                     start=True, stop=True)
                            i = h.matmul(ps[sb_][:, 256:512], ktok_s[:, cp, hh, 128:256], uasl(hh),
                                         start=True, stop=True)
                        return i
                    p.op("pe", smm, r=[f"ktok{cp}{2*wv}", f"ktok{cp}{2*wv+1}", "uadj0", "uadj1"],
                         w=["ps2", "ps3"])
                    for hh in (2 * wv, 2 * wv + 1):
                        p.op("dve", (lambda h, hh=hh: h.tensor_add(
                            S_s[:, hh, :], S_s[:, hh, :], ps[2 + hh % 2][:])),
                            r=[PSB[2 + hh % 2]], w=[f"S{hh}"])
                    pump(warm=1)
            waves.append(pw4)

            def pw5():
                for wv in range(2):
                    def otrw(h, wv=wv):
                        for hh in (2 * wv, 2 * wv + 1):
                            tp = hh % 2
                            cb = (hh // 2) * 256
                            h.transpose(ps_tr[:, tp, 768:896], otok_s[:, tp, cb:cb + 128],
                                        ident_s[:])
                            i = h.transpose(ps_tr[:, tp, 896:1024],
                                            otok_s[:, tp, cb + 128:cb + 256], ident_s[:])
                        return i
                    p.op("pe", otrw, r=["otok0", "otok1", "ident_s"], w=[PTR[0], PTR[1]])
                    for hh in (2 * wv, 2 * wv + 1):
                        tp = hh % 2
                        p.op("act", (lambda h, hh=hh, tp=tp, cs0=cs0: h.activation(
                            delta_s[:, 2 * hh, cs0:cs0 + CC], ps_tr[:, tp, 768:896], AF.Copy)),
                            r=[PTR[tp]], w=[f"delta{2*hh}"])
                        p.op("act", (lambda h, hh=hh, tp=tp, cs0=cs0: h.activation(
                            delta_s[:, 2 * hh + 1, cs0:cs0 + CC], ps_tr[:, tp, 896:1024], AF.Copy)),
                            r=[PTR[tp]], w=[f"delta{2*hh+1}"])
                    pump(warm=1)
            waves.append(pw5)
            return waves

        NCH_ = NCH if P_DELTA else 0
        if NCH_:
            for pa in range(0, NCH_, 2):
                if pa >= 2:
                    for wfn in post_waves(pa - 2):
                        wfn()
                    for wfn in post_waves(pa - 1):
                        wfn()
                d1_chunk(pa)
                d1_chunk(pa + 1)
                pairwaves(pa, pa + 1)
            for wfn in post_waves(NCH_ - 2):
                wfn()
            for wfn in post_waves(NCH_ - 1):
                wfn()

        if debug and t == 0:
            dump("q0", (lambda: q_s[:]), [128, 8, T], BF16, [f"q_s{c}" for c in range(8)])
            dump("k0", (lambda: k_s[:]), [128, 8, T], BF16, [f"k_s{c}" for c in range(8)])
            dump("v0", (lambda: v_s[:]), [128, 8, 64 + T], BF16, [f"v_s{c}" for c in range(8)])
            dump("beta0", (lambda: beta_s[:]), [4, T], F32, ["beta_s"])
            dump("delta0", (lambda: delta_s[:]), [128, 8, T], BF16,
                 [f"delta{c}" for c in range(8)])
            dump("xn0", (lambda: xn_s[:]), [128, 8, T], BF16, [f"xn{c}" for c in range(8)])

        if P_GATES:
            def gmm(h, pr=pr):
                for tt in range(4):
                    for ki in range(NCK):
                        i = h.matmul(ps[4][:, tt * 20:tt * 20 + 20],
                                     xn_s[:, pr, ki, tt * 128:(tt + 1) * 128],
                                     gw_s[:, ki * 20:(ki + 1) * 20],
                                     start=(ki == 0), stop=False)
                    i = h.matmul(ps[4][:, tt * 20:tt * 20 + 20],
                                 onesr_s[:], gb_s[:], start=False, stop=True)
                return i
            p.op("pe", gmm, r=[f"xn{pr}{ki}" for ki in range(NCK)] + ["gw_s", "gb_s", "onesr_s"], w=["ps4"])
            p.op("dve", (lambda h: h.tensor_tensor(
                gsc_s[:], ps[4][:, 0:80], itbc_s[:], ALU.mult)),
                r=["ps4", "itbc_s"], w=["gsc"], small=True)
            if debug and t == 0:
                dump("glog0", (lambda: gsc_s[:]), [128, 80], F32, ["gsc"])
                dump("itbc", (lambda: itbc_s[:]), [128, 80], BF16, ["itbc_s"])
                dump("eabc", (lambda: eabc_s[:]), [128, 80], BF16, ["eabc_s"])
                dump("ebbc", (lambda: ebbc_s[:]), [128, 80], BF16, ["ebbc_s"])
            p.op("dve", (lambda h: h.tensor_reduce(
                gmx_s[:], gsc_s[:].rearrange("p (a b) -> p a b", b=5), AX.X, ALU.max)),
                r=["gsc"], w=["gmx"], small=True)
            if debug and t == 0:
                dump("gmx0", (lambda: gmx_s[:]), [128, 16], F32, ["gmx"])
            for j in range(5):
                p.op("dve", (lambda h, j=j: h.tensor_copy(
                    gexp_s[:].rearrange("p (a b) -> p a b", b=5)[:, :, j], gmx_s[:])),
                    r=["gmx"], w=["gexp"], small=True)
            p.op("dve", (lambda h: h.tensor_sub(gsc_s[:], gsc_s[:], gexp_s[:])),
                 r=["gexp"], w=["gsc"], small=True)
            if debug and t == 0:
                dump("gsub0", (lambda: gsc_s[:]), [128, 80], F32, ["gsc"])
            pump()
            p.op("act", (lambda h: h.activation(gsc_s[:], gsc_s[:], AF.Exp)), w=["gsc"])
            if debug and t == 0:
                dump("gexpo0", (lambda: gsc_s[:]), [128, 80], F32, ["gsc"])
            p.op("dve", (lambda h: h.tensor_reduce(
                gsum_s[:], gsc_s[:].rearrange("p (a b) -> p a b", b=5), AX.X, ALU.add)),
                r=["gsc"], w=["gsum"], small=True)
            p.op("dve", (lambda h: h.reciprocal(gsum_s[:], gsum_s[:])), w=["gsum"], small=True)
            if debug and t == 0:
                dump("gsum0", (lambda: gsum_s[:]), [128, 16], F32, ["gsum"])
            for j in range(5):
                p.op("dve", (lambda h, j=j: h.tensor_copy(
                    gexp_s[:].rearrange("p (a b) -> p a b", b=5)[:, :, j], gsum_s[:])),
                    r=["gsum"], w=["gexp"], small=True)
            p.op("dve", (lambda h: h.tensor_mul(probs_s[:], gsc_s[:], gexp_s[:])),
                 r=["gsc", "gexp"], w=["probs"], small=True)
            if debug and t == 0:
                dump("gexp0", (lambda: gexp_s[:]), [128, 80], BF16, ["gexp"])
            p.op("dve", (lambda h: h.tensor_mul(probs_s[:], probs_s[:], eabc_s[:])),
                 r=["eabc_s"], w=["probs"], small=True)
            p.op("dve", (lambda h: h.tensor_add(probs_s[:], probs_s[:], ebbc_s[:])),
                 r=["ebbc_s"], w=["probs"], small=True)
            p.op("dve", (lambda h: h.tensor_copy(probsb_s[:], probs_s[:])), r=["probs"], w=["probsb"], small=True)

            def ptr_(h):
                for tt in range(4):
                    i = h.transpose(ps_tr[0:20, 0, tt * 128:tt * 128 + 128],
                                    probsb_s[:, tt * 20:tt * 20 + 20], ident_s[:])
                return i
            p.op("pe", ptr_, r=["probsb", "ident_s"], w=[PTR[0]])
            p.op("act", (lambda h: h.activation(probsf_s[:], ps_tr[0:20, 0, 0:512], AF.Copy)),
                 r=[PTR[0]], w=["probsf"])
            pump()

        FIRDEF = ((2, 0, 16), (3, 16, 4), (4, 20, 1))
        for ck in range(NCK if P_MIX else 0):
            hh = ck // 2
            if ck % 2 == 0:
                for j in range(5):
                    r_ = hh * 5 + j
                    p.op("pe", (lambda h, r_=r_: h.matmul(
                        ps[4][:], e20_s[:, r_ * 128:(r_ + 1) * 128], probsf_s[:],
                        start=True, stop=True)), r=["probsf", "e20_s"], w=["ps4"])
                    p.op("act", (lambda h, j=j: h.activation(
                        pbc5_s[:, j, :], ps[4][:], AF.Copy)), r=["ps4"], w=[f"pbc5_{j}"])
                    pump(warm=1)
            if P_FIR:
                def fload(rc_):
                    p.op("pool", (lambda h, rc_=rc_: h.dma_start(
                        out=fcf_s[:, rc_ % 4, :], in_=fcf_d[rc_])), w=[f"fcf{rc_ % 4}"])
                if ck == 0:
                    for rc_ in range(3):
                        fload(rc_)
                for sub in range(4):
                    rc = ck * 4 + sub
                    par = rc % 4
                    if rc + 3 < 32:
                        fload(rc + 3)

                    vpar = rc % 2
                    def repl(h, ck=ck, sub=sub):
                        for r in range(4):
                            h.matmul(ps[0][32 * r:32 * r + 32, :],
                                     rsel_s[:, sub * 32:sub * 32 + 32],
                                     v_s[:, ck, r:r + 512], start=True, stop=True,
                                     tile_position=(0, 32 * r))
                        for r in range(4):
                            i = h.matmul(ps[1][32 * r:32 * r + 32, 0:61],
                                         rsel_s[:, sub * 32:sub * 32 + 32],
                                         v_s[:, ck, r + 512:r + 573], start=True, stop=True,
                                         tile_position=(0, 32 * r))
                        return i
                    p.op("pe", repl, r=[f"v_s{ck}", "rsel_s"], w=["ps0", "ps1"])
                    p.op("act", (lambda h, vpar=vpar: h.activation(
                        vrep_s[:, vpar, 0:512], ps[0][:], AF.Copy)), r=["ps0"], w=[f"vrep{vpar}"])
                    p.op("act", (lambda h, vpar=vpar: h.activation(
                        vrep_s[:, vpar, 512:573], ps[1][:, 0:61], AF.Copy)), r=["ps1"], w=[f"vrep{vpar}"])
                    pump(warm=1)

                    def firmm(h, sub=sub, par=par, vpar=vpar):
                        for bank, goff, G in FIRDEF:
                            offbase = 64 - (4 * G - 1)
                            for g in range(G):
                                off = offbase + 4 * g
                                i = h.matmul(ps[bank][32 * sub:32 * sub + 32, :],
                                             fcf_s[:, par, (goff + g) * 32:(goff + g + 1) * 32],
                                             vrep_s[:, vpar, off:off + 512],
                                             start=(g == 0), stop=(g == G - 1),
                                             tile_position=(0, 32 * sub))
                        return i
                    p.op("pe", firmm, r=[f"fcf{par}", f"vrep{vpar}"],
                         w=[PSB[2], PSB[3], PSB[4]])
                    pump(warm=1)
            mdefs = [
                ("L", (lambda: ps[2][:]), PSB[2], 2),
                ("M", (lambda: ps[3][:]), PSB[3], 1),
                ("S", (lambda: ps[4][:]), PSB[4], 0),
                ("d", (lambda ck=ck: delta_s[:, ck, :]), f"delta{ck}", 3),
                ("v", (lambda ck=ck: v_s[:, ck, 64:64 + T]), f"v_s{ck}", 4),
            ]
            if P_FIR:
                p.op("pool", (lambda h, ck=ck: h.tensor_tensor(
                    mixtp_s[:], delta_s[:, ck, :], pbc5_s[:, 3, :], ALU.mult)),
                    r=[f"delta{ck}", "pbc5_3"], w=["mixtp"], dma=False)
                p.op("pool", (lambda h, ck=ck: h.tensor_tensor(
                    mixq_s[:], v_s[:, ck, 64:64 + T], pbc5_s[:, 4, :], ALU.mult)),
                    r=[f"v_s{ck}", "pbc5_4"], w=["mixq"], dma=False)
                p.op("pool", (lambda h: h.tensor_add(
                    mixtp_s[:], mixtp_s[:], mixq_s[:])), r=["mixq"], w=["mixtp"], dma=False)
                for bi, (nm, srcap, rk, j) in enumerate(mdefs[:3]):
                    if bi == 0:
                        p.op("dve", (lambda h, srcap=srcap, j=j, ck=ck: h.tensor_tensor(
                            omix_s[:, ck, :], srcap(), pbc5_s[:, j, :], ALU.mult)),
                            r=[rk, f"pbc5_{j}"], w=[f"omix{ck}"])
                    else:
                        p.op("dve", (lambda h, srcap=srcap, j=j: h.tensor_tensor(
                            mixt_s[:], srcap(), pbc5_s[:, j, :], ALU.mult)),
                            r=[rk, f"pbc5_{j}"], w=["mixt"])
                        p.op("dve", (lambda h, ck=ck: h.tensor_add(
                            omix_s[:, ck, :], omix_s[:, ck, :], mixt_s[:])),
                            r=["mixt"], w=[f"omix{ck}"])
            else:
                p.op("dve", (lambda h, ck=ck: h.tensor_tensor(
                    omix_s[:, ck, :], delta_s[:, ck, :], pbc5_s[:, 3, :], ALU.mult)),
                    r=[f"delta{ck}", "pbc5_3"], w=[f"omix{ck}"])
                p.op("pool", (lambda h, ck=ck: h.tensor_tensor(
                    mixtp_s[:], v_s[:, ck, 64:64 + T], pbc5_s[:, 4, :], ALU.mult)),
                    r=[f"v_s{ck}", "pbc5_4"], w=["mixtp"], dma=False)
            p.op("dve", (lambda h, ck=ck: h.tensor_add(
                omix_s[:, ck, :], omix_s[:, ck, :], mixtp_s[:])),
                r=["mixtp"], w=[f"omix{ck}"])
            pump(warm=1)

        for hh in range(H if P_MIX else 0):
            row = 32 * hh
            for ci, ck2 in enumerate((2 * hh, 2 * hh + 1)):
                p.op("dve", (lambda h, ck2=ck2, ci=ci: h.tensor_tensor(
                    sq_s[:, 4 * ci:4 * ci + 4, :].rearrange("p a b -> p (a b)"),
                    omix_s[:, ck2, :], omix_s[:, ck2, :], ALU.mult)),
                    r=[f"omix{ck2}"], w=[f"xsq{ci}"])
            def rmm(h, row=row):
                i = h.matmul(ps[4][row:row + 1, :], onesc_s[:],
                             sq_s[:, 0:4, :].rearrange("p a b -> p (a b)"),
                             start=True, stop=False, tile_position=(0, row))
                i = h.matmul(ps[4][row:row + 1, :], onesc_s[:],
                             sq_s[:, 4:8, :].rearrange("p a b -> p (a b)"),
                             start=False, stop=True, tile_position=(0, row))
                return i
            p.op("pe", rmm, r=["xsq0", "xsq1", "onesc_s"], w=["ps4"])
            p.op("dve", (lambda h, row=row: h.tensor_scalar(
                lr_row[:], ps[4][row:row + 1, :], 1.0 / DKV, 1e-5, ALU.mult, ALU.add)),
                r=["ps4"], w=["lr_row"], small=True)
            p.op("act", (lambda h: h.activation(lr_row[:], lr_row[:], AF.Ln)), w=["lr_row"])
            p.op("act", (lambda h: h.activation(
                lr_row[:], lr_row[:], AF.Exp, scale=-0.5)), w=["lr_row"])
            pump()
            p.op("dve", (lambda h: h.tensor_copy(
                lrb64_s[0:1, :], lr_row[:])), r=["lr_row"], w=["lrb64"], small=True)
            tp = hh % 2
            p.op("pe", (lambda h, tp=tp: h.matmul(
                ps[2 + tp][:], onesr_s[:], lrb64_s[0:1, :], start=True, stop=True)),
                r=["lrb64", "onesr_s"], w=[PSB[2 + tp]])
            for ck2 in (2 * hh, 2 * hh + 1):
                p.op("dve", (lambda h, ck2=ck2, tp=tp: h.tensor_mul(
                    omix_s[:, ck2, :], omix_s[:, ck2, :], ps[2 + tp][:])),
                    r=[PSB[2 + tp]], w=[f"omix{ck2}"])
                p.op("dve", (lambda h, ck2=ck2: h.tensor_scalar_mul(
                    delta_s[:, ck2, :], omix_s[:, ck2, :], nw_s[:, ck2:ck2 + 1])),
                    r=[f"omix{ck2}", "nw_s"], w=[f"delta{ck2}"])
                pump()
        if P_MIX:
            p.op("sp", (lambda h, t0=t0: h.dma_start(out=on_h[:, :, t0:t0 + T], in_=delta_s[:])),
                 r=[f"delta{ck2}" for ck2 in range(NCK)], w=["on_h"])

        if debug and t == 0:
            dump("probs0", (lambda: probs_s[:]), [128, 80], F32, ["probs"])
            dump("onorm0", (lambda: delta_s[:]), [128, 8, T], BF16,
                 [f"delta{c}" for c in range(8)])

        for ckk in range(NCK):
            p.op("dve", (lambda h, ckk=ckk: h.tensor_copy(
                v_s[:, ckk, 0:64], v_s[:, ckk, T:T + 64])), w=[f"v_s{ckk}"], small=True)
        bg_drain(drain_target)

    bgs["banks"] = (0, 1, 2, 5)
    bg_drain(len(BGQ))

    p.join()
    for g in reversed(ag):
        g.__exit__(None, None, None)

    fg = []

    def fsb(name, shape, dt):
        g = nc.sbuf_tensor(name, list(shape), dt)
        tt_ = g.__enter__()
        fg.append(g)
        return tt_

    def hidden_layer(src_h, dst_h, w_d_, s_d_, tag, qkeys):
        TQ = 1024
        hin = fsb(f"hin{tag}", [128, 2, 32, TQ], BF16)
        wb_ = fsb(f"whb{tag}", [128, 2, 32, 128], BF16)
        ss_ = fsb(f"ss{tag}", [128, 32], F32)
        ost = fsb(f"ost{tag}", [128, 2, TQ], BF16)
        p.op("sp", (lambda h: h.dma_start(out=ss_[:], in_=s_d_)), w=[f"ss{tag}"])

        def hload(t4):
            hp = t4 % 2
            for kq in range(4):
                p.op("sp", (lambda h, t4=t4, hp=hp, kq=kq: h.dma_start(
                    out=hin[:, hp, kq * 8:(kq + 1) * 8, :],
                    in_=src_h[:, kq * 8:(kq + 1) * 8, t4 * TQ:(t4 + 1) * TQ])),
                    r=qkeys(t4), w=[f"hin{tag}{hp}"])

        hload(0)
        for t4 in range(4):
            hp = t4 % 2
            if t4 + 1 < 4:
                hload(t4 + 1)
            for co in range(32):
                wp = co % 2
                p.op("pool", (lambda h, co=co, wp=wp: h.dma_start(
                    out=wb_[:, wp, :, :], in_=w_d_[co])), w=[f"whb{tag}{wp}"])
                for sub in range(2):
                    bb = 2 * (co % 2) + (sub % 2)
                    def f2mm(h, wp=wp, sub=sub, bb=bb, hp=hp):
                        for ki in range(32):
                            i = h.matmul(ps[bb][:], wb_[:, wp, ki, :],
                                         hin[:, hp, ki, sub * T:(sub + 1) * T],
                                         start=(ki == 0), stop=(ki == 31))
                        return i
                    p.op("pe", f2mm, r=[f"whb{tag}{wp}", f"hin{tag}{hp}"], w=[PSB[bb]])
                    p.op("act", (lambda h, co=co, wp=wp, sub=sub, bb=bb: h.activation(
                        ost[:, wp, sub * T:(sub + 1) * T], ps[bb][:], AF.Gelu,
                        scale=ss_[:, co:co + 1])), r=[PSB[bb], f"ss{tag}"], w=[f"ost{tag}{wp}"])
                p.op("sp", (lambda h, co=co, wp=wp, t4=t4: h.dma_start(
                    out=dst_h[:, co, t4 * TQ:(t4 + 1) * TQ], in_=ost[:, wp, :])),
                    r=[f"ost{tag}{wp}"], w=[f"{tag}dst", f"{tag}dst{t4}"])

    if P_F2:
        hidden_layer(h0_h, h1o_h, h1_d, sh1_d, "h1",
                     lambda t4: [f"h0_h{2 * t4}", f"h0_h{2 * t4 + 1}"])
    p.join()
    for g in reversed(fg):
        g.__exit__(None, None, None)
    fg = []
    if P_F3:
        hidden_layer(h1o_h, h2o_h, h2_d, sh2_d, "h2", lambda t4: [f"h1dst{t4}"])
    p.join()
    for g in reversed(fg):
        g.__exit__(None, None, None)
    fg = []

    h2in = fsb("h2in", [128, 2, 32, T], BF16)
    onin = fsb("onin", [128, 2, 8, T], BF16)
    xf32 = fsb("xf32", [128, 8, T], F32)
    w3s_ = fsb("w3s_", [128, 8, 32, 128], BF16)
    wos_ = fsb("wos_", [128, 8 * 8 * 128], BF16)
    s3_s = fsb("s3_s", [128, 8], F32)
    outst = fsb("outst", [128, 2, T], F32)

    p.op("sp", (lambda h: h.dma_start(out=s3_s[:], in_=s3_d)), w=["s3_s"])
    p.op("pool", (lambda h: h.dma_start(out=w3s_[:, :, :, :], in_=w3_d)), w=["w3s"])
    p.op("pool", (lambda h: h.dma_start(
        out=wos_[:], in_=wo_d.rearrange("p a b c -> p (a b c)"))), w=["wos"])

    def f4load(tt):
        fp = tt % 2
        p.op("sp", (lambda h, tt=tt, fp=fp: h.dma_start(
            out=h2in[:, fp, :, :], in_=h2o_h[:, :, tt * T:(tt + 1) * T])),
            r=[f"h2dst{tt // 2}"], w=[f"h2in{fp}"])
        p.op("pool", (lambda h, tt=tt, fp=fp: h.dma_start(
            out=onin[:, fp, :, :], in_=on_h[:, :, tt * T:(tt + 1) * T])),
            r=["on_h"], w=[f"onin{fp}"])

    def x4load(tt):
        p.op("pool", (lambda h, tt=tt: h.dma_start(
            out=xf32[:],
            in_=x_in.rearrange("(ck p) t -> p ck t", p=128)[:, :, tt * T:(tt + 1) * T])),
            w=["xf32"])

    if P_F4:
        f4load(0)
        x4load(0)
    for tt in range(NT if P_F4 else 0):
        fp = tt % 2
        if tt + 1 < NT:
            f4load(tt + 1)
        for co in range(8):
            bf_ = 2 * (co % 2)
            ba_ = bf_ + 1
            def f4mm(h, fp=fp, co=co, bf_=bf_, ba_=ba_):
                for ki in range(32):
                    i = h.matmul(ps[bf_][:], w3s_[:, co, ki, :], h2in[:, fp, ki, :],
                                 start=(ki == 0), stop=(ki == 31))
                for ki in range(8):
                    i = h.matmul(ps[ba_][:], w_at(wos_, ki, co), onin[:, fp, ki, :],
                                 start=(ki == 0), stop=(ki == 7))
                return i
            p.op("pe", f4mm, r=[f"h2in{fp}", f"onin{fp}", "w3s", "wos"], w=[PSB[bf_], PSB[ba_]])
            osl = co % 2
            p.op("dve", (lambda h, co=co, bf_=bf_, osl=osl: h.scalar_tensor_tensor(
                outst[:, osl, :], ps[bf_][:], s3_s[:, co:co + 1], xf32[:, co, :],
                ALU.mult, ALU.add)), r=[PSB[bf_], "s3_s", "xf32"], w=[f"outst{osl}"])
            p.op("dve", (lambda h, ba_=ba_, osl=osl: h.tensor_add(
                outst[:, osl, :], outst[:, osl, :], ps[ba_][:])),
                r=[PSB[ba_]], w=[f"outst{osl}"])
            p.op("sp", (lambda h, tt=tt, co=co, osl=osl: h.dma_start(
                out=out_d[:, co, tt * T:(tt + 1) * T], in_=outst[:, osl, :])),
                r=[f"outst{osl}"], w=["out_done"])
        if tt + 1 < NT:
            x4load(tt + 1)

    if debug:
        dump("on_h", (lambda: on_h), [128, 8, L], BF16, ["on_h"])
        dump("h0_h", (lambda: h0_h), [128, 32, L], BF16, ["h0_h"])
        dump("h1_h", (lambda: h1o_h), [128, 32, L], BF16, ["h1dst"])
        dump("h2_h", (lambda: h2o_h), [128, 32, L], BF16, ["h2dst"])
    p.waitop("sp", r=["out_done"], xw=([p.last_lab["sp"]] if debug else []))

    p.emit(nc)
    return nc, dbg_outs


def host_inputs(inputs):
    g = {}

    def blk8(w):
        wt = np.ascontiguousarray(w.T)
        return np.ascontiguousarray(wt.reshape(8, 128, 8, 128).transpose(1, 0, 2, 3))

    def blk8co(w):
        wt = np.ascontiguousarray(w.T)
        return np.ascontiguousarray(wt.reshape(8, 128, 8, 128).transpose(1, 2, 0, 3))

    g["wq"] = _bf(blk8co(inputs["L0_attn_Wq"]))
    g["wk"] = _bf(blk8co(inputs["L0_attn_Wk"]))
    g["wv"] = _bf(blk8co(inputs["L0_attn_Wv"]))
    g["wo"] = blk8(inputs["L0_attn_Wo"])
    g["wb"] = np.ascontiguousarray(
        inputs["L0_attn_Wb"].T.reshape(8, 128, 4).transpose(1, 0, 2))
    g["gw"] = np.ascontiguousarray(
        inputs["L0_attn_gate_w"].reshape(20, 1024).T.reshape(8, 128, 20).transpose(1, 0, 2))
    g["gb"] = _bf(inputs["L0_attn_gate_b"].reshape(1, 20))
    g["ltmp"] = np.ascontiguousarray(
        np.repeat(inputs["L0_attn_log_temp"].reshape(4), 5).reshape(1, 20))
    g["eps"] = np.ascontiguousarray(
        np.repeat(inputs["L0_attn_eps_param"].reshape(4), 5).reshape(1, 20))
    nw = inputs["L0_attn_norm_w"]
    g["nw"] = np.ascontiguousarray(
        np.stack([nw[(ck % 2) * 128:(ck % 2) * 128 + 128] for ck in range(8)], axis=1))
    for nm, key in (("cq", "L0_attn_conv_q"), ("ck", "L0_attn_conv_k"), ("cv", "L0_attn_conv_v")):
        g[nm] = np.ascontiguousarray(inputs[key].reshape(8, 128, 4).transpose(1, 0, 2))
    fcf = np.zeros((32, 128, 21, 32), np.float32)
    firs = [(0, inputs["L0_attn_fir_l"].reshape(1024, KL), KL, 16),
            (16, inputs["L0_attn_fir_m"].reshape(1024, KM), KM, 4),
            (20, inputs["L0_attn_fir_s"].reshape(1024, KS), KS, 1)]
    cidx = np.arange(32)
    for rc in range(32):
        ch0 = (rc // 4) * 128 + (rc % 4) * 32
        for goff, fl_, K, G in firs:
            pad = 4 * G - K
            for gi in range(G):
                for r in range(4):
                    jp = 4 * gi + r - pad
                    if jp >= 0:
                        fcf[rc, 32 * r + cidx, goff + gi, cidx] = fl_[ch0:ch0 + 32, jp]
    g["fcf"] = np.ascontiguousarray(fcf.reshape(32, 128, 21 * 32))
    rsel = np.zeros((128, 4, 32), np.float32)
    for b in range(4):
        rsel[32 * b + cidx, b, cidx] = 1.0
    g["rsel"] = _bf(rsel.reshape(128, 128))

    def blkffn(w, nki, nco):
        wt = np.ascontiguousarray(w.T)
        return np.ascontiguousarray(
            wt.reshape(nki, 128, nco, 128).transpose(2, 1, 0, 3))

    g["w1"] = _bf(blkffn(inputs["L0_ffn_w1_w"], 8, 32))
    g["w2"] = _bf(blkffn(inputs["L0_ffn_w2_w"], 8, 32))
    g["h1"] = _bf(blkffn(inputs["L0_ffn_h1_w"], 32, 32))
    g["h2"] = _bf(blkffn(inputs["L0_ffn_h2_w"], 32, 32))
    g["w3"] = _bf(np.ascontiguousarray(
        np.ascontiguousarray(inputs["L0_ffn_w3_w"].T).reshape(32, 128, 8, 128)
        .transpose(1, 2, 0, 3)))
    g["s1"] = np.ascontiguousarray(inputs["L0_ffn_w1_s"].reshape(32, 128).T)
    g["s2"] = np.ascontiguousarray(inputs["L0_ffn_w2_s"].reshape(32, 128).T)
    g["sh1"] = np.ascontiguousarray(inputs["L0_ffn_h1_s"].reshape(32, 128).T)
    g["sh2"] = np.ascontiguousarray(inputs["L0_ffn_h2_s"].reshape(32, 128).T)
    g["s3"] = np.ascontiguousarray(inputs["L0_ffn_w3_s"].reshape(8, 128).T)

    g["ident"] = _bf(np.eye(128, dtype=np.float32))
    g["onesc"] = _bf(np.ones((128, 1), np.float32))
    g["onesr"] = _bf(np.ones((1, 128), np.float32))
    g["ones33"] = _bf(np.ones((33, 128), np.float32))
    e4 = np.zeros((4, 4, 128), np.float32)
    for hh in range(4):
        e4[hh, hh, :] = 1.0
    g["e4"] = _bf(e4)
    e20 = np.zeros((20, 20, 128), np.float32)
    for r in range(20):
        e20[r, r, :] = 1.0
    g["e20"] = _bf(e20)
    one = np.ones((128, 128), np.float32)
    g["mstril"] = _bf(np.tril(one, -1))
    g["mstriu"] = _bf(np.triu(one, 1))
    g["mtriu"] = _bf(np.triu(one, 0))
    for k in g:
        if g[k].dtype == np.float64:
            g[k] = g[k].astype(np.float32)
    return g


_CACHE = {}


def kernel(**inputs):
    x = np.asarray(inputs["L0_attn_hidden_states"], np.float32)
    B = x.shape[0]
    shared = host_inputs(inputs)
    if "nc" not in _CACHE:
        _CACHE["nc"], _CACHE["dbg"] = build_program(debug=DBG)
    nc = _CACHE["nc"]
    in_maps = []
    for b in range(B):
        m = dict(shared)
        m["x"] = np.ascontiguousarray(x[b].T)
        in_maps.append(m)
    res = run_bass_kernel_spmd(nc, in_maps, core_ids=list(range(B)))
    outs = []
    for b in range(B):
        o = np.asarray(res.results[b]["out"])
        outs.append(o.transpose(2, 1, 0).reshape(L, D))
    if DBG:
        kernel.last_results = res
    return np.stack(outs, 0).astype(np.float32)
